# version 11
# baseline (speedup 1.0000x reference)
import numpy as np
from contextlib import ExitStack

import concourse.bass as bass
import concourse.mybir as mybir

F32 = mybir.dt.float32
BF16 = mybir.dt.bfloat16
ALU = mybir.AluOpType
AF = mybir.ActivationFunctionType
AX = mybir.AxisListType

D = 1024
DC = D // 128
HID = 4096
HC = HID // 128
RK = 16
EPS = 1e-5
NCORES = 8
GROUPS = [[0, 1, 2, 3], [4, 5, 6, 7]]


class Ctx:
    def __init__(self, nc, es):
        self.nc = nc
        self.es = es
        self.sems = {}
        self.cnt = {}

    def sem(self, name):
        if name not in self.sems:
            self.sems[name] = self.es.enter_context(self.nc.semaphore(name))
            self.cnt[name] = 0
        return self.sems[name]

    def inc(self, instr, name, k=1):
        instr.then_inc(self.sem(name), k)
        self.cnt[name] += k
        return self.cnt[name]

    def n(self, name):
        self.sem(name)
        return self.cnt[name]

    def wait(self, eng, name, value=None):
        v = self.cnt[name] if value is None else value
        if v > 0:
            eng.wait_ge(self.sem(name), v)


def build_nc(T):
    NT = T // 128
    GT = min(256, T)
    NG = T // GT
    L_full = 4 * T
    ATT_SCALE = float(np.float32(1.0 / np.sqrt(L_full)))
    NB = max(1, T // 512)
    WB = min(512, T)
    TB = max(1, T // 1024)
    VB = min(1024, T)

    nc = bass.Bass("TRN2", target_bir_lowering=False)

    xT = nc.dram_tensor("xT", [D, T], F32, kind="ExternalInput").ap()
    wqT = nc.dram_tensor("wqT", [D, D], F32, kind="ExternalInput").ap()
    wkT = nc.dram_tensor("wkT", [D, D], F32, kind="ExternalInput").ap()
    wvT = nc.dram_tensor("wvT", [D, D], F32, kind="ExternalInput").ap()
    wprojT = nc.dram_tensor("wprojT", [DC, D, 128], F32, kind="ExternalInput").ap()
    wfc = nc.dram_tensor("wfc", [HID, D], F32, kind="ExternalInput").ap()
    wpj = nc.dram_tensor("wpj", [D, HID], F32, kind="ExternalInput").ap()
    vlrT = nc.dram_tensor("vlrT", [D, RK], F32, kind="ExternalInput").ap()
    fbT = nc.dram_tensor("fbT", [RK, RK], F32, kind="ExternalInput").ap()
    uT = nc.dram_tensor("uT", [RK, HID], F32, kind="ExternalInput").ap()
    bq_row = nc.dram_tensor("bq_row", [1, D], F32, kind="ExternalInput").ap()
    bk_row = nc.dram_tensor("bk_row", [1, D], F32, kind="ExternalInput").ap()
    bv_col = nc.dram_tensor("bv_col", [128, DC], F32, kind="ExternalInput").ap()
    pb_col = nc.dram_tensor("pb_col", [128, DC], F32, kind="ExternalInput").ap()
    gam_col = nc.dram_tensor("gam_col", [128, DC], F32, kind="ExternalInput").ap()
    bet_col = nc.dram_tensor("bet_col", [128, DC], F32, kind="ExternalInput").ap()
    gate_in = nc.dram_tensor("gate_in", [1, 1], F32, kind="ExternalInput").ap()
    out_dram = nc.dram_tensor("outT", [NG, DC, 128, GT], F32, kind="ExternalOutput").ap()

    wspl_dram = {w: (nc.dram_tensor(f"w{w}_h", [D, D], BF16).ap(),
                     nc.dram_tensor(f"w{w}_l", [D, D], BF16).ap())
                 for w in ("v", "k", "q")}
    ternfc_dram = nc.dram_tensor("ternfc_d", [128, HC, DC, 128], BF16).ap()
    ternpj_dram = nc.dram_tensor("ternpj_d", [128, DC, HC, 128], BF16).ap()
    k_sph = nc.dram_tensor("k_sph", [NT, 128, D], BF16).ap()
    k_spl = nc.dram_tensor("k_spl", [NT, 128, D], BF16).ap()
    q_sph = nc.dram_tensor("q_sph", [NT, 128, D], BF16).ap()
    q_spl = nc.dram_tensor("q_spl", [NT, 128, D], BF16).ap()
    v_sph = nc.dram_tensor("v_sph", [DC, 128, T], BF16).ap()
    v_spl = nc.dram_tensor("v_spl", [DC, 128, T], BF16).ap()
    S_part = nc.dram_tensor("S_part", [DC, 128, D], F32).ap()
    S_full = nc.dram_tensor("S_full", [DC, 128, D], F32).ap()
    wT_dramh = nc.dram_tensor("wT_dramh", [DC, DC, 128, 128], BF16).ap()
    wT_draml = nc.dram_tensor("wT_draml", [DC, DC, 128, 128], BF16).ap()

    es = ExitStack()
    with es:
        ctx = Ctx(nc, es)
        sem = ctx.sem
        sbt = lambda name, shape, dt=F32: es.enter_context(nc.sbuf_tensor(name, shape, dt))
        gam_sb = sbt("gam_sb", [128, DC]); bet_sb = sbt("bet_sb", [128, DC])
        bv_sb = sbt("bv_sb", [128, DC]); pb_sb = sbt("pb_sb", [128, DC])
        vlr_sb = sbt("vlr_sb", [128, DC, RK])
        vlrh = sbt("vlrh", [128, DC, RK], BF16)
        vlrl = sbt("vlrl", [128, DC, RK], BF16)
        fb_sb = sbt("fb_sb", [RK, RK])
        gate_sb = sbt("gate_sb", [1, 1]); gate16 = sbt("gate16", [RK, 1])
        onec = sbt("onec", [128, 1]); oner = sbt("oner", [1, 128])
        eps_c = sbt("eps_c", [128, 1])
        identF = sbt("identF", [128, 128]); identB = sbt("identB", [128, 128], BF16)
        sfc_col = sbt("sfc_col", [128, HC]); spj_col = sbt("spj_col", [128, DC])
        r_row = sbt("r_row", [1, T]); mur_row = sbt("mur_row", [1, T])

        blk = es.enter_context(nc.Block())

        def VEC(fn):
            def w(v):
                v.drain()
                fn(v)
            blk.vector(w)

        def SCA(fn):
            def w(s):
                s.drain()
                fn(s)
            blk.scalar(w)

        @blk.sync
        def _(sy):
            for dst, src in [(gam_sb, gam_col), (bet_sb, bet_col),
                             (bv_sb, bv_col), (pb_sb, pb_col),
                             (fb_sb, fbT), (gate_sb, gate_in)]:
                ctx.inc(sy.dma_start(out=dst[:], in_=src), "sml", 16)
            ctx.inc(sy.dma_start(out=vlr_sb[:],
                                 in_=vlrT.rearrange("(dc p) r -> p dc r", p=128)),
                    "sml", 16)

        @blk.gpsimd
        def _(g):
            g.iota(identF[:], pattern=[[-1, 128]], base=0, channel_multiplier=1,
                   allow_small_or_imprecise_dtypes=True)
            ctx.inc(g.engine_nop(), "gp")

        @VEC
        def _(v):
            ctx.wait(v, "gp")
            v.memset(onec[:], 1.0)
            v.memset(oner[:], 1.0)
            v.memset(eps_c[:], EPS)
            v.tensor_scalar(out=identF[:], in0=identF[:], scalar1=0.0, scalar2=None,
                            op0=ALU.is_equal)
            v.drain()
            v.tensor_copy(out=identB[:], in_=identF[:])
            ctx.inc(v.drain(), "vv")
        tok_consts = ctx.n("vv")

        es0 = ExitStack()
        with es0:
            ps_g = es0.enter_context(nc.psum_tensor("ps_g", [128, 1], F32))

            @blk.tensor
            def _(te):
                ctx.wait(te, "vv", tok_consts)
                ctx.wait(te, "sml")
                ctx.inc(te.matmul(ps_g[:], oner[:], gate_sb[:], start=True, stop=True),
                        "pe")
            tok_g = ctx.n("pe")

            @VEC
            def _(v):
                ctx.wait(v, "pe", tok_g)
                v.tensor_copy(out=gate16[:], in_=ps_g[0:RK, :])
                v.drain()
                v.tensor_scalar(out=vlr_sb[:], in0=vlr_sb[:],
                                scalar1=ps_g[:, 0:1], scalar2=None, op0=ALU.mult)
                v.drain()
                v.tensor_copy(out=vlrh[:].rearrange("p a b -> p (a b)"),
                              in_=vlr_sb[:].rearrange("p a b -> p (a b)"))
                v.drain()
                v.tensor_tensor(out=vlrl[:].rearrange("p a b -> p (a b)"),
                                in0=vlr_sb[:].rearrange("p a b -> p (a b)"),
                                in1=vlrh[:].rearrange("p a b -> p (a b)"),
                                op=ALU.subtract)
                ctx.inc(v.drain(), "vv")
        tok_consts = ctx.n("vv")

        def make_prep_emitters(src_dram, n_chunks, width, s_col, tern_dram, tag,
                               bufs, first_waits=()):
            nw = width // 128
            wbuf, absum, scr, tern, hp, stage, ps_tr = bufs
            NSL = tern.shape[1]
            TRN = f"trn_{tag}"
            TEV = f"tev_{tag}"
            TRD = f"trd_{tag}"
            STW = f"stw_{tag}"

            def emit_front(ic):
                sl = ic % 2
                tsl = ic % NSL

                @blk.sync
                def _(sy, ic=ic, sl=sl):
                    if ic == 0:
                        for nm, val in first_waits:
                            ctx.wait(sy, nm, val)
                    if ic >= 2:
                        ctx.wait(sy, TRD, ic - 1)
                    d = sy.dma_start(out=wbuf[:, sl],
                                     in_=src_dram[ic * 128:(ic + 1) * 128, :])
                    ctx.inc(d, f"ldw{sl}_{tag}", 16)

                @SCA
                def _(s, ic=ic, sl=sl):
                    s.wait_ge(sem(f"ldw{sl}_{tag}"), 16 * (ic // 2 + 1))
                    if ic == 0:
                        for nm, val in first_waits:
                            ctx.wait(s, nm, val)
                    if ic >= 1:
                        ctx.wait(s, TRD, ic)
                    i = s.activation(out=scr[:], in_=wbuf[:, sl], func=AF.Abs,
                                     accum_out=absum[:, sl:sl + 1])
                    ctx.inc(i, "sc")
                tok_abs = ctx.n("sc")

                @VEC
                def _(v, ic=ic, sl=sl, tsl=tsl, tok_abs=tok_abs):
                    ctx.wait(v, "sc", tok_abs)
                    if ic >= NSL:
                        ctx.wait(v, TRN, (ic - NSL + 1) * nw)
                    v.tensor_scalar(out=s_col[:, ic:ic + 1], in0=absum[:, sl:sl + 1],
                                    scalar1=1.0 / width, scalar2=1e-5,
                                    op0=ALU.mult, op1=ALU.max)
                    v.drain()
                    v.tensor_scalar(out=hp[:, sl:sl + 1], in0=s_col[:, ic:ic + 1],
                                    scalar1=0.5, scalar2=None, op0=ALU.mult)
                    v.drain()
                    v.tensor_scalar(out=scr[:], in0=wbuf[:, sl],
                                    scalar1=hp[:, sl:sl + 1], scalar2=None,
                                    op0=ALU.is_gt)
                    v.tensor_scalar(out=tern[:, tsl], in0=wbuf[:, sl], scalar1=-1.0,
                                    scalar2=hp[:, sl:sl + 1], op0=ALU.mult,
                                    op1=ALU.is_gt)
                    v.drain()
                    i = v.tensor_tensor(out=tern[:, tsl], in0=scr[:], in1=tern[:, tsl],
                                        op=ALU.subtract)
                    ctx.inc(i, TRD)

            def emit_back(ic):
                tsl = ic % NSL
                for j in range(nw):
                    idx = ic * nw + j

                    @blk.tensor
                    def _(te, ic=ic, tsl=tsl, j=j, idx=idx):
                        if j == 0:
                            ctx.wait(te, TRD, ic + 1)
                            if ic == 0:
                                for nm, val in first_waits:
                                    ctx.wait(te, nm, val)
                        if idx >= 2:
                            ctx.wait(te, TEV, idx - 1)
                        i = te.transpose(ps_tr[idx % 2],
                                         tern[:, tsl, j * 128:(j + 1) * 128], identB[:])
                        ctx.inc(i, TRN)

                    @VEC
                    def _(v, ic=ic, j=j, idx=idx):
                        ctx.wait(v, TRN, idx + 1)
                        if ic >= 1 and j == 0:
                            v.wait_ge(sem(STW), 16 * ic)
                        i = v.tensor_copy(out=stage[:, j], in_=ps_tr[idx % 2])
                        ctx.inc(i, TEV)

                @blk.gpsimd
                def _(g, ic=ic):
                    ctx.wait(g, TEV, (ic + 1) * nw)
                    d = g.dma_start(out=tern_dram[:, ic], in_=stage[:])
                    ctx.inc(d, STW, 16)

            fronts = [lambda ic=ic: emit_front(ic) for ic in range(n_chunks)]
            backs = [lambda ic=ic: emit_back(ic) for ic in range(n_chunks)]
            return fronts, backs

        esW = ExitStack()
        with esW:
            wqb = esW.enter_context(nc.sbuf_tensor("wqb", [128, 2, D], F32))
            wqbh = esW.enter_context(nc.sbuf_tensor("wqbh", [128, 2, D], BF16))
            wqbl = esW.enter_context(nc.sbuf_tensor("wqbl", [128, 2, D], BF16))
            srcs = [("v", wvT), ("k", wkT), ("q", wqT)]
            gw_tok = {}
            for wi, (wn, wsrc) in enumerate(srcs):
                for dc in range(DC):
                    ci = wi * DC + dc
                    sl = ci % 2

                    @blk.sync
                    def _(sy, wn=wn, wsrc=wsrc, dc=dc, ci=ci, sl=sl):
                        if ci >= 2:
                            ctx.wait(sy, "gpw", gw_tok[ci - 2])
                        d = sy.dma_start(out=wqb[:, sl],
                                         in_=wsrc[dc * 128:(dc + 1) * 128, :])
                        ctx.inc(d, f"ldsp{sl}", 16)
                    ld_tok = ctx.n(f"ldsp{sl}")

                    @blk.gpsimd
                    def _(g, wn=wn, dc=dc, ci=ci, sl=sl, ld_tok=ld_tok):
                        g.wait_ge(sem(f"ldsp{sl}"), ld_tok)
                        if ci >= 2:
                            g.wait_ge(sem(f"spw{sl}"), 32 * (ci // 2))
                        g.tensor_copy(out=wqbh[:, sl], in_=wqb[:, sl])
                        g.drain()
                        i = g.tensor_tensor(out=wqbl[:, sl], in0=wqb[:, sl],
                                            in1=wqbh[:, sl], op=ALU.subtract)
                        gw_tok[ci] = ctx.inc(i, "gpw")

                    @SCA
                    def _(s, wn=wn, dc=dc, ci=ci, sl=sl):
                        ctx.wait(s, "gpw", gw_tok[ci])
                        dh, dl = wspl_dram[wn]
                        ctx.inc(s.dma_start(out=dh[dc * 128:(dc + 1) * 128, :],
                                            in_=wqbh[:, sl]), f"spw{sl}", 16)
                        ctx.inc(s.dma_start(out=dl[dc * 128:(dc + 1) * 128, :],
                                            in_=wqbl[:, sl]), f"spw{sl}", 16)
        n_spw0 = ctx.n("spw0")
        n_spw1 = ctx.n("spw1")


        es3 = ExitStack()
        with es3:
            hh_sb = es3.enter_context(nc.sbuf_tensor("hh_sb", [128, DC, T], BF16))
            hl_sb = es3.enter_context(nc.sbuf_tensor("hl_sb", [128, DC, T], BF16))

            es4 = ExitStack()
            with es4:
                xbuf = es4.enter_context(nc.sbuf_tensor("xbuf", [128, 2, T], F32))
                scrA = es4.enter_context(nc.sbuf_tensor("scrA", [128, T], F32))
                scrB = es4.enter_context(nc.sbuf_tensor("scrB", [128, T], F32))
                rowA = es4.enter_context(nc.sbuf_tensor("rowA", [1, T], F32))
                rowB = es4.enter_context(nc.sbuf_tensor("rowB", [1, T], F32))
                rowC = es4.enter_context(nc.sbuf_tensor("rowC", [1, T], F32))
                bc1 = es4.enter_context(nc.sbuf_tensor("bc1", [128, T], F32))
                bc2 = es4.enter_context(nc.sbuf_tensor("bc2", [128, T], F32))

                esPS = ExitStack()
                esPS.__enter__()
                ps_sx = esPS.enter_context(nc.psum_tensor("ps_sx", [1, T], F32))
                ps_sx2 = esPS.enter_context(nc.psum_tensor("ps_sx2", [1, T], F32))

                pe_sq_tok = {}
                for dc in range(DC):
                    sl = dc % 2

                    @blk.sync
                    def _(sy, dc=dc, sl=sl):
                        if dc == 0:
                            sy.wait_ge(sem("spw0"), n_spw0)
                            sy.wait_ge(sem("spw1"), n_spw1)
                        if dc >= 2:
                            ctx.wait(sy, "pe", pe_sq_tok[dc - 2])
                        d = sy.dma_start(out=xbuf[:, sl],
                                         in_=xT[dc * 128:(dc + 1) * 128, :])
                        ctx.inc(d, f"ldx{sl}", 16)
                    ldx_tok = ctx.n(f"ldx{sl}")

                    @SCA
                    def _(s, dc=dc, sl=sl, ldx_tok=ldx_tok):
                        s.wait_ge(sem(f"ldx{sl}"), ldx_tok)
                        if dc == 0:
                            s.wait_ge(sem("spw0"), n_spw0)
                            s.wait_ge(sem("spw1"), n_spw1)
                        else:
                            ctx.wait(s, "pe", pe_sq_tok[dc - 1])
                        i = s.activation(out=scrA[:], in_=xbuf[:, sl], func=AF.Square)
                        ctx.inc(i, "sc")
                    tok_sq = ctx.n("sc")

                    @blk.tensor
                    def _(te, dc=dc, sl=sl, tok_sq=tok_sq):
                        if dc == 0:
                            ctx.wait(te, "vv", tok_consts)
                        ctx.wait(te, "sc", tok_sq)
                        for b in range(NB):
                            te.matmul(ps_sx[0:1, b * 512:b * 512 + WB], onec[:],
                                      xbuf[:, sl, b * 512:b * 512 + WB],
                                      start=(dc == 0), stop=(dc == DC - 1))
                            i = te.matmul(ps_sx2[0:1, b * 512:b * 512 + WB], onec[:],
                                          scrA[:, b * 512:b * 512 + WB],
                                          start=(dc == 0), stop=(dc == DC - 1))
                        pe_sq_tok[dc] = ctx.inc(i, "pe")
                stats_pe = ctx.n("pe")

                @VEC
                def _(v):
                    ctx.wait(v, "pe", stats_pe)
                    v.tensor_scalar(out=rowA[:], in0=ps_sx[:], scalar1=1.0 / D,
                                    scalar2=None, op0=ALU.mult)
                    v.tensor_scalar(out=rowB[:], in0=ps_sx2[:], scalar1=1.0 / D,
                                    scalar2=None, op0=ALU.mult)
                    v.drain()
                    v.tensor_tensor(out=rowC[:], in0=rowA[:], in1=rowA[:], op=ALU.mult)
                    v.drain()
                    v.tensor_tensor(out=rowB[:], in0=rowB[:], in1=rowC[:],
                                    op=ALU.subtract)
                    ctx.inc(v.drain(), "vv")
                tok_var = ctx.n("vv")

                @SCA
                def _(s):
                    ctx.wait(s, "vv", tok_var)
                    ctx.inc(s.activation(out=rowC[:], in_=rowB[:], func=AF.Sqrt,
                                         bias=eps_c[0:1, 0:1]), "sc")
                tok_sd = ctx.n("sc")

                @VEC
                def _(v):
                    ctx.wait(v, "sc", tok_sd)
                    v.reciprocal(out=r_row[:], in_=rowC[:])
                    v.drain()
                    v.tensor_tensor(out=mur_row[:], in0=rowA[:], in1=r_row[:], op=ALU.mult)
                    v.tensor_tensor(out=rowC[:], in0=rowB[:], in1=r_row[:], op=ALU.mult)
                    v.drain()
                    v.tensor_tensor(out=rowC[:], in0=rowC[:], in1=r_row[:], op=ALU.mult)
                    ctx.inc(v.drain(), "vv")
                tok_v2 = ctx.n("vv")

                @SCA
                def _(s):
                    ctx.wait(s, "vv", tok_v2)
                    ctx.inc(s.activation(out=rowB[:], in_=rowC[:], func=AF.Sqrt,
                                         bias=eps_c[0:1, 0:1]), "sc")
                tok_sd2 = ctx.n("sc")

                @VEC
                def _(v):
                    ctx.wait(v, "sc", tok_sd2)
                    v.reciprocal(out=rowC[:], in_=rowB[:])
                    v.drain()
                    v.tensor_tensor(out=rowA[:], in0=r_row[:], in1=rowC[:], op=ALU.mult)
                    v.drain()
                    v.tensor_tensor(out=rowB[:], in0=mur_row[:], in1=rowC[:], op=ALU.mult)
                    ctx.inc(v.drain(), "vv")
                tok_rows = ctx.n("vv")

                esPS.__exit__(None, None, None)
                esBC = ExitStack()
                esBC.__enter__()
                ps_bc = esBC.enter_context(nc.psum_tensor("ps_bc", [128, 512], F32))
                bc_toks = []
                for bi, (dst, srow) in enumerate([(bc1, rowA), (bc2, rowB)]):
                    for b in range(NB):
                        w = min(512, T)

                        @blk.tensor
                        def _(te, bi=bi, b=b, w=w, dst=dst, srow=srow):
                            ctx.wait(te, "vv", tok_rows)
                            if bc_toks:
                                ctx.wait(te, "vv", bc_toks[-1])
                            ctx.inc(te.matmul(ps_bc[:, 0:w], oner[:],
                                              srow[:, b * 512:b * 512 + w],
                                              start=True, stop=True), "pe")
                        bc_pe = ctx.n("pe")

                        @VEC
                        def _(v, bi=bi, b=b, w=w, dst=dst, bc_pe=bc_pe):
                            ctx.wait(v, "pe", bc_pe)
                            i = v.tensor_copy(out=dst[:, b * 512:b * 512 + w],
                                              in_=ps_bc[:, 0:w])
                            bc_toks.append(ctx.inc(i, "vv"))
                esBC.__exit__(None, None, None)
                tok_bc = ctx.n("vv")

                h_tok = {}
                hb_tok = {}
                sc_h_tok = {}
                for dc in range(DC):
                    sl = dc % 2

                    @blk.sync
                    def _(sy, dc=dc, sl=sl):
                        if dc < 2:
                            ctx.wait(sy, "pe", stats_pe)
                        else:
                            ctx.wait(sy, "vv", h_tok[dc - 2])
                        d = sy.dma_start(out=xbuf[:, sl],
                                         in_=xT[dc * 128:(dc + 1) * 128, :])
                        ctx.inc(d, f"ldx{sl}", 16)
                    ldh_tok = ctx.n(f"ldx{sl}")

                    @VEC
                    def _(v, dc=dc, sl=sl, ldh_tok=ldh_tok):
                        v.wait_ge(sem(f"ldx{sl}"), ldh_tok)
                        if dc == 0:
                            ctx.wait(v, "vv", tok_bc)
                        if dc >= 1:
                            ctx.wait(v, "sc", sc_h_tok[dc - 1])
                        v.tensor_tensor(out=scrA[:], in0=xbuf[:, sl], in1=bc1[:],
                                        op=ALU.mult)
                        v.drain()
                        i = v.tensor_tensor(out=scrA[:], in0=scrA[:], in1=bc2[:],
                                            op=ALU.subtract)
                        hb_tok[dc] = ctx.inc(i, "vv")

                    @SCA
                    def _(s, dc=dc):
                        ctx.wait(s, "vv", hb_tok[dc])
                        if dc >= 1:
                            ctx.wait(s, "vv", h_tok[dc - 1])
                        s.activation(out=scrB[:], in_=scrA[:], func=AF.Identity,
                                     scale=gam_sb[:, dc:dc + 1],
                                     bias=bet_sb[:, dc:dc + 1])
                        s.drain()
                        i = s.copy(out=hh_sb[:, dc], in_=scrB[:])
                        sc_h_tok[dc] = ctx.inc(i, "sc")

                    @VEC
                    def _(v, dc=dc):
                        ctx.wait(v, "sc", sc_h_tok[dc])
                        i = v.tensor_tensor(out=hl_sb[:, dc], in0=scrB[:],
                                            in1=hh_sb[:, dc], op=ALU.subtract)
                        h_tok[dc] = ctx.inc(i, "vv")
                tok_h = ctx.n("vv")

            es5 = ExitStack()
            with es5:
                wph = es5.enter_context(nc.sbuf_tensor("wph", [128, 2, DC, D], BF16))
                wpl = es5.enter_context(nc.sbuf_tensor("wpl", [128, 2, DC, D], BF16))
                bq_sb = es5.enter_context(nc.sbuf_tensor("bq_sb", [1, D], F32))
                bk_sb = es5.enter_context(nc.sbuf_tensor("bk_sb", [1, D], F32))
                stage_vh = es5.enter_context(nc.sbuf_tensor("stage_vh", [128, 2, VB], BF16))
                stage_vl = es5.enter_context(nc.sbuf_tensor("stage_vl", [128, 2, VB], BF16))
                stage_kqh = es5.enter_context(nc.sbuf_tensor("stage_kqh", [128, 2, D], BF16))
                stage_kql = es5.enter_context(nc.sbuf_tensor("stage_kql", [128, 2, D], BF16))
                ps_kq = [es5.enter_context(nc.psum_tensor(f"ps_kq{i}", [128, D], F32))
                         for i in range(2)]

                esPF = ExitStack()
                pf = lambda name, shape, dt: esPF.enter_context(nc.sbuf_tensor(name, shape, dt))
                fc_bufs = (pf("wbuf_fc", [128, 2, D], F32),
                           pf("absum_fc", [128, 2], F32),
                           pf("scr_fc", [128, D], BF16),
                           pf("tern_fc", [128, 4, D], BF16),
                           pf("hp_fc", [128, 2], F32),
                           pf("stage_fc", [128, DC, 128], BF16),
                           [esPF.enter_context(nc.psum_tensor(f"pstr{i}_fc", [128, 128], BF16))[:]
                            for i in range(2)])
                fc_fronts, fc_backs = make_prep_emitters(
                    wfc, HC, D, sfc_col, ternfc_dram, "fc", fc_bufs,
                    first_waits=[("vv", tok_h)])
                fc_sched = []
                for i2 in range(HC + 2):
                    if i2 < HC:
                        fc_sched.append(fc_fronts[i2])
                    if i2 >= 2:
                        fc_sched.append(fc_backs[i2 - 2])

                def pump_fc(n=1):
                    for _ in range(n):
                        if fc_sched:
                            fc_sched.pop(0)()

                def load_wpair(tag, slot, wait_pe=None):
                    dh, dl = wspl_dram[tag]

                    @blk.sync
                    def _(sy):
                        if wait_pe is not None:
                            ctx.wait(sy, "pe", wait_pe)
                        ctx.inc(sy.dma_start(
                            out=wph[:, slot].rearrange("p a b -> p (a b)"),
                            in_=dh.rearrange("(a p) b -> p a b", p=128)),
                            f"ldw{tag}", 16)
                        ctx.inc(sy.dma_start(
                            out=wpl[:, slot].rearrange("p a b -> p (a b)"),
                            in_=dl.rearrange("(a p) b -> p a b", p=128)),
                            f"ldw{tag}", 16)
                    return ctx.n(f"ldw{tag}")

                @blk.sync
                def _(sy):
                    ctx.wait(sy, "vv", tok_h)
                    ctx.inc(sy.dma_start(out=bq_sb[:], in_=bq_row), "ldwv", 16)
                    ctx.inc(sy.dma_start(out=bk_sb[:], in_=bk_row), "ldwv", 16)
                n_ldv = load_wpair("v", 0)
                n_ldk_w = load_wpair("k", 1)
                vev_tok = {}
                vmm_tok = {}
                for vf in range(DC):
                    for tb in range(TB):
                        vi = vf * TB + tb
                        sl = vi % 2

                        pump_fc(2)

                        @blk.tensor
                        def _(te, vf=vf, tb=tb, vi=vi, sl=sl):
                            if vi == 0:
                                te.wait_ge(sem("ldwv"), n_ldv)
                                ctx.wait(te, "vv", tok_h)
                            if vi >= 2:
                                ctx.wait(te, "vv", vev_tok[vi - 2])
                            for b in range(VB // 512 if VB >= 512 else 1):
                                w = min(512, VB)
                                t0 = tb * VB + b * 512
                                for wop, hop in ((wph, hh_sb), (wph, hl_sb), (wpl, hh_sb)):
                                    for dc in range(DC):
                                        i = te.matmul(
                                            ps_kq[sl][:, b * 512:b * 512 + w],
                                            wop[:, 0, dc, vf * 128:(vf + 1) * 128],
                                            hop[:, dc, t0:t0 + w],
                                            start=(wop is wph and hop is hh_sb and dc == 0),
                                            stop=(wop is wpl and dc == DC - 1))
                            vmm_tok[vi] = ctx.inc(i, "pe")

                        @VEC
                        def _(v, vf=vf, tb=tb, vi=vi, sl=sl):
                            ctx.wait(v, "pe", vmm_tok[vi])
                            if vi >= 2:
                                v.wait_ge(sem(f"spv{vi % 2}"), 32 * (vi // 2))
                            v.tensor_scalar(out=stage_vh[:, vi % 2],
                                            in0=ps_kq[sl][:, 0:VB],
                                            scalar1=bv_sb[:, vf:vf + 1], scalar2=None,
                                            op0=ALU.add)
                            v.drain()
                            i = v.scalar_tensor_tensor(out=stage_vl[:, vi % 2],
                                                       in0=ps_kq[sl][:, 0:VB],
                                                       scalar=bv_sb[:, vf:vf + 1],
                                                       in1=stage_vh[:, vi % 2],
                                                       op0=ALU.add, op1=ALU.subtract)
                            vev_tok[vi] = ctx.inc(i, "vv")

                        @SCA
                        def _(s, vf=vf, tb=tb, vi=vi):
                            ctx.wait(s, "vv", vev_tok[vi])
                            d = s.dma_start(out=v_sph[vf, :, tb * VB:(tb + 1) * VB],
                                            in_=stage_vh[:, vi % 2])
                            ctx.inc(d, f"spv{vi % 2}", 16)
                            d = s.dma_start(out=v_spl[vf, :, tb * VB:(tb + 1) * VB],
                                            in_=stage_vl[:, vi % 2])
                            ctx.inc(d, f"spv{vi % 2}", 16)

                pe_after_v = ctx.n("pe")
                def kq_phase(b_sb_row, scale, sp_h, sp_l, tag, wslot, ld_n,
                             prev_spill=()):
                    spname = f"sp{tag}"
                    ev_tok = {}
                    last_vv = ctx.n("vv")
                    for tt in range(NT):
                        sl = tt % 2

                        pump_fc(2)

                        @blk.tensor
                        def _(te, tt=tt, sl=sl, last_vv=last_vv):
                            if tt == 0:
                                te.wait_ge(sem(f"ldw{tag}"), ld_n)
                                ctx.wait(te, "vv", last_vv)
                            if tt >= 2:
                                ctx.wait(te, "vv", ev_tok[tt - 2])
                            for b in range(2):
                                for hop, wop in ((hh_sb, wph), (hh_sb, wpl), (hl_sb, wph)):
                                    for dc in range(DC):
                                        te.matmul(ps_kq[sl][:, b * 512:(b + 1) * 512],
                                                  hop[:, dc, tt * 128:(tt + 1) * 128],
                                                  wop[:, wslot, dc, b * 512:(b + 1) * 512],
                                                  start=(hop is hh_sb and wop is wph and dc == 0),
                                                  stop=False)
                                i = te.matmul(ps_kq[sl][:, b * 512:(b + 1) * 512],
                                              oner[:], b_sb_row[:, b * 512:(b + 1) * 512],
                                              start=False, stop=True)
                            ctx.inc(i, "pe")
                        mm_tok = ctx.n("pe")

                        @VEC
                        def _(v, tt=tt, sl=sl, mm_tok=mm_tok):
                            ctx.wait(v, "pe", mm_tok)
                            if tt < 2:
                                for nm, val in prev_spill:
                                    v.wait_ge(sem(nm), val)
                            else:
                                v.wait_ge(sem(f"{spname}{tt % 2}"), 32 * (tt // 2))
                            v.tensor_scalar(out=stage_kqh[:, tt % 2], in0=ps_kq[sl][:],
                                            scalar1=scale, scalar2=None, op0=ALU.mult)
                            v.drain()
                            i = v.scalar_tensor_tensor(out=stage_kql[:, tt % 2],
                                                       in0=ps_kq[sl][:], scalar=scale,
                                                       in1=stage_kqh[:, tt % 2],
                                                       op0=ALU.mult, op1=ALU.subtract)
                            ev_tok[tt] = ctx.inc(i, "vv")

                        @SCA
                        def _(s, tt=tt):
                            ctx.wait(s, "vv", ev_tok[tt])
                            ctx.inc(s.dma_start(out=sp_h[tt], in_=stage_kqh[:, tt % 2]),
                                    f"{spname}{tt % 2}", 16)
                            ctx.inc(s.dma_start(out=sp_l[tt], in_=stage_kql[:, tt % 2]),
                                    f"{spname}{tt % 2}", 16)

                kq_phase(bk_sb, ATT_SCALE, k_sph, k_spl, "k", 1, n_ldk_w)
                n_ldq_w = load_wpair("q", 0, wait_pe=pe_after_v)
                kq_phase(bq_sb, 1.0, q_sph, q_spl, "q", 0, n_ldq_w,
                         prev_spill=[("spk0", ctx.n("spk0")), ("spk1", ctx.n("spk1"))])
                pump_fc(len(fc_sched))
                esPF.close()
            n_spk0 = ctx.n("spk0"); n_spk1 = ctx.n("spk1")
            n_spq0 = ctx.n("spq0"); n_spq1 = ctx.n("spq1")
            n_spv0 = ctx.n("spv0"); n_spv1 = ctx.n("spv1")
            bar_kq_pe = ctx.n("pe")
            bar_kq_vv = ctx.n("vv")

            es6 = ExitStack()
            with es6:
                kbufh = es6.enter_context(nc.sbuf_tensor("kbufh", [128, 2, D], BF16))
                kbufl = es6.enter_context(nc.sbuf_tensor("kbufl", [128, 2, D], BF16))
                qbufh = es6.enter_context(nc.sbuf_tensor("qbufh", [128, 2, 512], BF16))
                qbufl = es6.enter_context(nc.sbuf_tensor("qbufl", [128, 2, 512], BF16))
                stageS = es6.enter_context(nc.sbuf_tensor("stageS", [128, 2, D], F32))
                ps_S = [es6.enter_context(nc.psum_tensor(f"ps_S{i}", [128, D], F32))
                        for i in range(4)]
                QH = 512
                smm_tok = {}
                sev_tok = {}
                for half in range(2):
                    for tt in range(NT):
                        sl = tt % 2
                        idx = half * NT + tt

                        @blk.sync
                        def _(sy, half=half, tt=tt, sl=sl, idx=idx):
                            if idx == 0:
                                for nm, val in [("spk0", n_spk0), ("spk1", n_spk1),
                                                ("spq0", n_spq0), ("spq1", n_spq1)]:
                                    sy.wait_ge(sem(nm), val)
                                sy.wait_ge(sem("stw_fc"), 16 * HC)
                                ctx.wait(sy, "vv", bar_kq_vv)
                            if idx >= 2:
                                ctx.wait(sy, "pe", smm_tok[idx - 2])
                            if tt == 0 and half == 1:
                                for qc in range(4):
                                    ctx.wait(sy, "vv", sev_tok[(0, qc)])
                            ctx.inc(sy.dma_start(out=kbufh[:, sl], in_=k_sph[tt]),
                                    f"ldk{sl}", 16)
                            ctx.inc(sy.dma_start(out=kbufl[:, sl], in_=k_spl[tt]),
                                    f"ldk{sl}", 16)
                            d = sy.dma_start(out=qbufh[:, sl],
                                             in_=q_sph[tt, :, half * QH:(half + 1) * QH])
                            ctx.inc(d, f"ldk{sl}", 16)
                            d = sy.dma_start(out=qbufl[:, sl],
                                             in_=q_spl[tt, :, half * QH:(half + 1) * QH])
                            ctx.inc(d, f"ldk{sl}", 16)
                        ldk_tok = ctx.n(f"ldk{sl}")

                        @blk.tensor
                        def _(te, half=half, tt=tt, sl=sl, idx=idx, ldk_tok=ldk_tok):
                            te.wait_ge(sem(f"ldk{sl}"), ldk_tok)
                            if idx == 0:
                                ctx.wait(te, "pe", bar_kq_pe)
                                ctx.wait(te, "vv", bar_kq_vv)
                            for qc in range(4):
                                for b in range(2):
                                    for qop, kop in ((qbufh, kbufh), (qbufh, kbufl),
                                                     (qbufl, kbufh)):
                                        i = te.matmul(
                                            ps_S[qc][:, b * 512:(b + 1) * 512],
                                            qop[:, sl, qc * 128:(qc + 1) * 128],
                                            kop[:, sl, b * 512:(b + 1) * 512],
                                            start=(tt == 0 and qop is qbufh and kop is kbufh),
                                            stop=(tt == NT - 1 and qop is qbufl))
                            smm_tok[idx] = ctx.inc(i, "pe")

                    for qc in range(4):
                        @VEC
                        def _(v, half=half, qc=qc):
                            ctx.wait(v, "pe", smm_tok[half * NT + NT - 1])
                            di = 4 * half + qc
                            if di >= 2:
                                v.wait_ge(sem(f"spS{di % 2}"), 16 * (di // 2))
                            i = v.tensor_copy(out=stageS[:, qc % 2], in_=ps_S[qc][:])
                            sev_tok[(half, qc)] = ctx.inc(i, "vv")

                        @SCA
                        def _(s, half=half, qc=qc):
                            di = 4 * half + qc
                            ctx.wait(s, "vv", sev_tok[(half, qc)])
                            d = s.dma_start(out=S_part[half * 4 + qc],
                                            in_=stageS[:, qc % 2])
                            ctx.inc(d, f"spS{di % 2}", 16)
                n_spS_all = [("spS0", ctx.n("spS0")), ("spS1", ctx.n("spS1"))]
        bar_s1_pe = ctx.n("pe")
        bar_s1_vv = ctx.n("vv")
        bar_s1_sc = ctx.n("sc")
        n_ldwv = ctx.n("ldwv"); n_ldwk = ctx.n("ldwk"); n_ldwq = ctx.n("ldwq")
        n_ldk0 = ctx.n("ldk0"); n_ldk1 = ctx.n("ldk1")
        n_ldx0 = ctx.n("ldx0"); n_ldx1 = ctx.n("ldx1")
        n_spS0 = ctx.n("spS0"); n_spS1 = ctx.n("spS1")

        @blk.gpsimd
        def _(g):
            for nm, val in n_spS_all:
                g.wait_ge(sem(nm), val)
            i = g.collective_compute("AllReduce", ALU.add, replica_groups=GROUPS,
                                     ins=[S_part.opt()], outs=[S_full.opt()])
            ctx.inc(i, "cc")

        es7 = ExitStack()
        with es7:
            sb7 = lambda name, shape, dt=F32: es7.enter_context(nc.sbuf_tensor(name, shape, dt))
            NTF = 3
            NTP = 2
            tfc_buf = sb7("tfc_buf", [128, NTF, DC, 128], BF16)
            tpj_buf = sb7("tpj_buf", [128, NTP, HC, 128], BF16)
            wpj_buf = sb7("wpj_buf", [128, 2, DC, 128])
            u_sb = sb7("u_sb", [RK, HID])
            out_acc2 = sb7("out_acc2", [128, 2, DC, GT])
            xnh2 = sb7("xnh2", [128, 2, DC, GT], BF16)
            xnl2 = sb7("xnl2", [128, 2, DC, GT], BF16)
            x_g1 = sb7("x_g1", [128, DC, GT])
            r2h = sb7("r2h", [128, HC, GT], BF16)
            r2l = sb7("r2l", [128, HC, GT], BF16)
            arena = sb7("arena", [128, max(DC * GT, 3 * D)])
            a_g = arena[:, 0:DC * GT].rearrange("p (dc t) -> p dc t", t=GT)
            smS = arena[:, 0:2 * D].rearrange("p (s d) -> p s d", d=D)
            scrE = arena[:, 2 * D:3 * D]
            assert DC * GT >= 2 * D
            v_gh = sb7("v_gh", [128, DC, GT], BF16)
            v_gl = sb7("v_gl", [128, DC, GT], BF16)
            stageWh = sb7("stageWh", [128, DC, 128], BF16)
            stageWl = sb7("stageWl", [128, DC, 128], BF16)
            wTbufh = sb7("wTbufh", [128, 2, DC * 128], BF16)
            wTbufl = sb7("wTbufl", [128, 2, DC * 128], BF16)
            scr1 = sb7("scr1", [128, GT])
            scrR = sb7("scrR", [128, GT])
            scrR2 = sb7("scrR2", [128, GT])
            br_g2 = sb7("br_g2", [128, 2, GT])
            bmur_g2 = sb7("bmur_g2", [128, 2, GT])
            y1_sb = sb7("y1_sb", [RK, GT])
            y2_sb2 = sb7("y2_sb2", [RK, 2, GT])
            nmax = sb7("nmax", [128, 1])
            ssum = sb7("ssum", [128, 1])
            rsum = sb7("rsum", [128, 1])

            ps_fc = [es7.enter_context(nc.psum_tensor(f"ps_fc{i}", [128, 512], F32))
                     for i in range(3)]
            ps_sh = es7.enter_context(nc.psum_tensor("ps_sh", [128, 512], F32))
            ps_tr2 = [es7.enter_context(nc.psum_tensor(f"ps_tr2{i}", [128, 128], F32))
                      for i in range(2)]
            ps_av = [es7.enter_context(nc.psum_tensor(f"ps_av{i}", [128, 512], F32))
                     for i in range(2)]

            pj_wbuf = sb7("pj_wbuf", [128, 2, 1024])
            pj_absq = sb7("pj_absq", [128, 4])
            pj_scr = sb7("pj_scr", [128, 1024], BF16)
            pj_tern = sb7("pj_tern", [128, 3, 1024])
            pj_hp = sb7("pj_hp", [128, 1])
            pj_stage = sb7("pj_stage", [128, 1, 8, 128], BF16)

            def wait_bar_s1(eng):
                ctx.wait(eng, "pe", bar_s1_pe)
                ctx.wait(eng, "vv", bar_s1_vv)
                ctx.wait(eng, "sc", bar_s1_sc)
                for nm, val in [("ldk0", n_ldk0), ("ldk1", n_ldk1),
                                ("spS0", n_spS0), ("spS1", n_spS1),
                                ("ldx0", n_ldx0), ("ldx1", n_ldx1),
                                ("spv0", n_spv0), ("spv1", n_spv1),
                                ("spk0", n_spk0), ("spk1", n_spk1),
                                ("spq0", n_spq0), ("spq1", n_spq1),
                                ("ldwv", n_ldwv), ("ldwk", n_ldwk),
                                ("ldwq", n_ldwq)]:
                    eng.wait_ge(sem(nm), val)

            @blk.sync
            def _(sy):
                wait_bar_s1(sy)
                ctx.inc(sy.dma_start(out=u_sb[:], in_=uT), "lds2", 16)
            n_lds2 = ctx.n("lds2")

            esU = ExitStack()
            with esU:
                rs_row = esU.enter_context(nc.sbuf_tensor("rs_row", [1, HID // 4], F32))
                rs_tr = esU.enter_context(nc.sbuf_tensor("rs_tr", [HC, 128], F32))
                rs_col = esU.enter_context(nc.sbuf_tensor("rs_col", [128, HC], F32))

                @VEC
                def _(v):
                    wait_bar_s1(v)
                    v.reciprocal(out=rs_col[:], in_=sfc_col[:])
                    ctx.inc(v.drain(), "vv")
                tok_rsc = ctx.n("vv")

                @blk.tensor
                def _(te):
                    ctx.wait(te, "vv", tok_rsc)
                    ctx.wait(te, "pe", bar_s1_pe)
                    ctx.inc(te.transpose(ps_tr2[0][0:HC, :], rs_col[:], identF[:]), "pe")
                tok_rst = ctx.n("pe")

                @VEC
                def _(v):
                    ctx.wait(v, "pe", tok_rst)
                    ctx.inc(v.tensor_copy(out=rs_tr[:], in_=ps_tr2[0][0:HC, :]), "vv")
                tok_rstr = ctx.n("vv")

                ps_rs16 = ps_av[0][0:RK, :]
                rsb_toks = []
                for hf in range(4):
                    @SCA
                    def _(s, hf=hf):
                        ctx.wait(s, "vv", tok_rstr)
                        if hf >= 1:
                            ctx.wait(s, "pe")
                        d = s.dma_start(
                            out=rs_row[:].rearrange("o (c p) -> o c p", p=128),
                            in_=rs_tr[hf * (HC // 4):(hf + 1) * (HC // 4), :])
                        ctx.inc(d, "ldrs", 16)
                    ldrs_tok = ctx.n("ldrs")

                    for b in range(HID // 2048):
                        @blk.tensor
                        def _(te, hf=hf, b=b, ldrs_tok=ldrs_tok):
                            te.wait_ge(sem("ldrs"), ldrs_tok)
                            if rsb_toks:
                                ctx.wait(te, "vv", rsb_toks[-1])
                            ctx.inc(te.matmul(ps_rs16, oner[0:1, 0:RK],
                                              rs_row[:, b * 512:(b + 1) * 512],
                                              start=True, stop=True), "pe")
                        rsb_pe = ctx.n("pe")

                        @VEC
                        def _(v, hf=hf, b=b, rsb_pe=rsb_pe):
                            ctx.wait(v, "pe", rsb_pe)
                            if hf == 0 and b == 0:
                                v.wait_ge(sem("lds2"), n_lds2)
                            off = hf * (HID // 4) + b * 512
                            i = v.tensor_tensor(out=u_sb[:, off:off + 512],
                                                in0=u_sb[:, off:off + 512],
                                                in1=ps_rs16, op=ALU.mult)
                            rsb_toks.append(ctx.inc(i, "vv"))
            tok_uprime = ctx.n("vv")

            state = {}
            pj_state = {"absn": 0, "cmp": {}, "trn": 0, "tev": 0}

            def emit_pj_absq(fc8, qt):
                sl = (fc8 * 4 + qt) % 2

                @SCA
                def _(sy, fc8=fc8, qt=qt, sl=sl):
                    if fc8 == 0 and qt == 0:
                        wait_bar_s1(sy)
                    prev = fc8 * 4 + qt - 2
                    if prev >= 0:
                        ctx.wait(sy, "vv", pj_state[("abs", prev)])
                    ctx.wait(sy, "vv")
                    d = sy.dma_start(out=pj_wbuf[:, sl],
                                     in_=wpj[fc8 * 128:(fc8 + 1) * 128,
                                             qt * 1024:(qt + 1) * 1024])
                    ctx.inc(d, f"pjA{sl}", 16)
                ld_tok = ctx.n(f"pjA{sl}")

                @VEC
                def _(v, fc8=fc8, qt=qt, sl=sl, ld_tok=ld_tok):
                    v.wait_ge(sem(f"pjA{sl}"), ld_tok)
                    if fc8 == 0 and qt == 0:
                        wait_bar_s1(v)
                    i = v.tensor_reduce(out=pj_absq[:, qt:qt + 1], in_=pj_wbuf[:, sl],
                                        axis=AX.X, op=ALU.add, apply_absolute_value=True)
                    pj_state[("abs", fc8 * 4 + qt)] = ctx.inc(i, "vv")

            def emit_pj_s(fc8):
                tok = pj_state[("abs", fc8 * 4 + 3)]

                @VEC
                def _(v, fc8=fc8, tok=tok):
                    ctx.wait(v, "vv", tok)
                    v.tensor_tensor(out=pj_absq[:, 0:1], in0=pj_absq[:, 0:1],
                                    in1=pj_absq[:, 1:2], op=ALU.add)
                    v.tensor_tensor(out=pj_absq[:, 2:3], in0=pj_absq[:, 2:3],
                                    in1=pj_absq[:, 3:4], op=ALU.add)
                    v.drain()
                    v.tensor_tensor(out=pj_absq[:, 0:1], in0=pj_absq[:, 0:1],
                                    in1=pj_absq[:, 2:3], op=ALU.add)
                    v.drain()
                    v.tensor_scalar(out=spj_col[:, fc8:fc8 + 1], in0=pj_absq[:, 0:1],
                                    scalar1=1.0 / HID, scalar2=1e-5,
                                    op0=ALU.mult, op1=ALU.max)
                    v.drain()
                    i = v.tensor_scalar(out=pj_hp[:], in0=spj_col[:, fc8:fc8 + 1],
                                        scalar1=0.5, scalar2=None, op0=ALU.mult)
                    pj_state[("s", fc8)] = ctx.inc(i, "vv")

            def emit_pj_bfront(fc8, qt):
                u = fc8 * 4 + qt
                sl = u % 2
                tsl = u % 3

                @SCA
                def _(sy, fc8=fc8, qt=qt, sl=sl, u=u):
                    if u >= 2:
                        ctx.wait(sy, "vv", pj_state[("cmp", u - 2)])
                    ctx.wait(sy, "vv", pj_state[("abs", fc8 * 4 + 3)])
                    d = sy.dma_start(out=pj_wbuf[:, sl],
                                     in_=wpj[fc8 * 128:(fc8 + 1) * 128,
                                             qt * 1024:(qt + 1) * 1024])
                    ctx.inc(d, f"pjB{sl}", 16)
                ld_tok = ctx.n(f"pjB{sl}")

                @VEC
                def _(v, fc8=fc8, qt=qt, sl=sl, tsl=tsl, u=u, ld_tok=ld_tok):
                    v.wait_ge(sem(f"pjB{sl}"), ld_tok)
                    ctx.wait(v, "vv", pj_state[("s", fc8)])
                    if u >= 3:
                        ctx.wait(v, "pe", pj_state[("trn", (u - 3) * 8 + 7)])
                    v.tensor_scalar(out=pj_scr[:], in0=pj_wbuf[:, sl],
                                    scalar1=pj_hp[:, 0:1], scalar2=None, op0=ALU.is_gt)
                    v.tensor_scalar(out=pj_tern[:, tsl], in0=pj_wbuf[:, sl], scalar1=-1.0,
                                    scalar2=pj_hp[:, 0:1], op0=ALU.mult, op1=ALU.is_gt)
                    v.drain()
                    i = v.tensor_tensor(out=pj_tern[:, tsl], in0=pj_scr[:],
                                        in1=pj_tern[:, tsl], op=ALU.subtract)
                    pj_state[("cmp", u)] = ctx.inc(i, "vv")

            def emit_pj_bback(fc8, qt):
                u = fc8 * 4 + qt
                tsl = u % 3
                for j in range(8):
                    idx = u * 8 + j

                    @blk.tensor
                    def _(te, tsl=tsl, u=u, j=j, idx=idx):
                        if j == 0:
                            ctx.wait(te, "vv", pj_state[("cmp", u)])
                            if u == 0:
                                ctx.wait(te, "vv", tok_uprime)
                        if idx >= 2:
                            ctx.wait(te, "sc", pj_state[("tev", idx - 2)])
                        i = te.transpose(ps_tr2[idx % 2][:],
                                         pj_tern[:, tsl, j * 128:(j + 1) * 128], identF[:])
                        pj_state[("trn", idx)] = ctx.inc(i, "pe")

                    @SCA
                    def _(s, tsl=tsl, u=u, j=j, idx=idx):
                        ctx.wait(s, "pe", pj_state[("trn", idx)])
                        if u >= 1 and j == 0:
                            s.wait_ge(sem("stw_pj"), 16 * u)
                        i = s.copy(out=pj_stage[:, 0, j], in_=ps_tr2[idx % 2][:])
                        pj_state[("tev", idx)] = ctx.inc(i, "sc")

                @blk.gpsimd
                def _(g, fc8=fc8, qt=qt, u=u):
                    ctx.wait(g, "sc", pj_state[("tev", u * 8 + 7)])
                    d = g.dma_start(out=ternpj_dram[:, fc8, qt * 8:(qt + 1) * 8, :],
                                    in_=pj_stage[:, 0])
                    ctx.inc(d, "stw_pj", 16)

            pj_units = []
            pj_backlog = []
            for fc8 in range(DC):
                pj_units.append(lambda fc8=fc8: [emit_pj_absq(fc8, q) for q in range(4)])
                pj_units.append(lambda fc8=fc8: emit_pj_s(fc8))
                for qt in range(4):
                    pj_units.append(lambda fc8=fc8, qt=qt: emit_pj_bfront(fc8, qt))
                    pj_units.append(lambda fc8=fc8, qt=qt: emit_pj_bback(fc8, qt))
            sched = []
            pending_back = []
            for unit_fc8 in range(len(pj_units)):
                pass
            raw = pj_units
            pj_units = []
            lagq = []
            for i2, it in enumerate(raw):
                pj_units.append(it)

            def pump_pj(n=1):
                for _ in range(n):
                    if not pj_units:
                        break
                    pj_units.pop(0)()

            def prologue(g_idx):
                p = g_idx % 2
                gsl = slice(g_idx * GT, (g_idx + 1) * GT)
                x_g = x_g1
                xnh = xnh2[:, p]; xnl = xnl2[:, p]
                out_acc = out_acc2[:, p]
                br_g = br_g2[:, p]; bmur_g = bmur_g2[:, p]
                y2_sb = y2_sb2[:, p]

                @blk.sync
                def _(sy):
                    if g_idx == 0:
                        wait_bar_s1(sy)
                    else:
                        ctx.wait(sy, "vv", state[("xn", g_idx - 1)])
                    for dc in range(DC):
                        ctx.inc(sy.dma_start(out=x_g[:, dc],
                                             in_=xT[dc * 128:(dc + 1) * 128, gsl]),
                                "ldxg", 16)
                ld_tok = ctx.n("ldxg")

                bcg_toks = []
                for dst, srow in [(br_g, r_row), (bmur_g, bmur_src := mur_row)]:
                    @blk.tensor
                    def _(te, dst=dst, srow=srow):
                        if g_idx == 0 and not bcg_toks:
                            ctx.wait(te, "vv", tok_uprime)
                        if g_idx > 0 and not bcg_toks:
                            ctx.wait(te, "vv", state[("y2ev", g_idx - 1)])
                            if g_idx >= 2:
                                ctx.wait(te, "vv", state[("xn", g_idx - 2)])
                        if bcg_toks:
                            ctx.wait(te, "vv", bcg_toks[-1])
                        ctx.inc(te.matmul(ps_sh[:, 0:GT], oner[:], srow[0:1, gsl],
                                          start=True, stop=True), "pe")
                    bcg_pe = ctx.n("pe")

                    @VEC
                    def _(v, dst=dst, bcg_pe=bcg_pe):
                        ctx.wait(v, "pe", bcg_pe)
                        bcg_toks.append(ctx.inc(v.tensor_copy(out=dst[:],
                                                              in_=ps_sh[:, 0:GT]), "vv"))
                tok_bcg = ctx.n("vv")

                @VEC
                def _(v):
                    v.wait_ge(sem("ldxg"), ld_tok)
                    ctx.wait(v, "vv", tok_bcg)
                    if g_idx >= 2:
                        v.wait_ge(sem("out_w"), 16 * (g_idx - 1))
                        ctx.wait(v, "pe", state[("fcmm", g_idx - 2)])
                    for dc in range(DC):
                        v.tensor_tensor(out=scr1[:], in0=x_g[:, dc], in1=br_g[:],
                                        op=ALU.mult)
                        v.drain()
                        v.tensor_tensor(out=scr1[:], in0=scr1[:], in1=bmur_g[:],
                                        op=ALU.subtract)
                        v.drain()
                        v.tensor_tensor(out=out_acc[:, dc], in0=x_g[:, dc],
                                        in1=scr1[:], op=ALU.add)
                        v.tensor_copy(out=xnh[:, dc], in_=scr1[:])
                        v.drain()
                        v.tensor_tensor(out=xnl[:, dc], in0=scr1[:],
                                        in1=xnh[:, dc], op=ALU.subtract)
                        v.drain()
                    state[("xn", g_idx)] = ctx.inc(v.drain(), "vv")
                tok_xn = state[("xn", g_idx)]

                @blk.tensor
                def _(te):
                    ctx.wait(te, "vv", tok_xn)
                    for wop, xop in ((vlrh, xnh), (vlrh, xnl), (vlrl, xnh)):
                        for dc in range(DC):
                            i = te.matmul(ps_sh[0:RK, 0:GT], wop[:, dc], xop[:, dc],
                                          start=(wop is vlrh and xop is xnh and dc == 0),
                                          stop=(wop is vlrl and dc == DC - 1))
                    ctx.inc(i, "pe")
                tok_y1 = ctx.n("pe")

                @VEC
                def _(v):
                    ctx.wait(v, "pe", tok_y1)
                    v.tensor_copy(out=y1_sb[:], in_=ps_sh[0:RK, 0:GT])
                    ctx.inc(v.drain(), "vv")
                tok_y1e = ctx.n("vv")

                @blk.tensor
                def _(te):
                    ctx.wait(te, "vv", tok_y1e)
                    ctx.inc(te.matmul(ps_sh[0:RK, 0:GT], fb_sb[:], y1_sb[:], start=True,
                                      stop=True), "pe")
                tok_y2 = ctx.n("pe")

                @VEC
                def _(v):
                    ctx.wait(v, "pe", tok_y2)
                    v.tensor_copy(out=y2_sb[:], in_=ps_sh[0:RK, 0:GT])
                    state[("y2ev", g_idx)] = ctx.inc(v.drain(), "vv")

            def ffn_mid(g_idx, pump=None):
                p = g_idx % 2
                xnh = xnh2[:, p]; xnl = xnl2[:, p]
                out_acc = out_acc2[:, p]
                y2_sb = y2_sb2[:, p]
                tok_y2e = state[("y2ev", g_idx)]

                r2_tok = {}
                relu_tok = {}
                fcmm_tok = {}
                for ic in range(HC):
                    sl = ic % 3
                    tsl = ic % NTF
                    if pump is not None:
                        pump(3)

                    @blk.sync
                    def _(sy, ic=ic, tsl=tsl):
                        if ic == 0 and g_idx == 0:
                            sy.wait_ge(sem("stw_fc"), 16 * HC)
                        if ic >= NTF:
                            ctx.wait(sy, "pe", fcmm_tok[ic - NTF])
                        elif g_idx > 0:
                            ctx.wait(sy, "pe", state[("fcmm", g_idx - 1)])
                        ctx.inc(sy.dma_start(out=tfc_buf[:, tsl], in_=ternfc_dram[:, ic]),
                                f"ldtf{tsl}", 16)
                    ldtf_tok = ctx.n(f"ldtf{tsl}")

                    @blk.tensor
                    def _(te, ic=ic, sl=sl, tsl=tsl, ldtf_tok=ldtf_tok):
                        te.wait_ge(sem(f"ldtf{tsl}"), ldtf_tok)
                        if ic == 0:
                            ctx.wait(te, "vv", tok_y2e)
                            if g_idx == 0:
                                te.wait_ge(sem("lds2"), n_lds2)
                            else:
                                ctx.wait(te, "vv", state[("pjev", g_idx - 1)])
                                ctx.wait(te, "pe", state[("pjmm", g_idx - 1)])
                        if ic >= 3:
                            ctx.wait(te, "sc", relu_tok[ic - 3])
                        for xp in (xnh, xnl):
                            for dc in range(DC):
                                te.matmul(ps_fc[sl][:, 0:GT], tfc_buf[:, tsl, dc],
                                          xp[:, dc], start=(xp is xnh and dc == 0),
                                          stop=False)
                        i = te.matmul(ps_fc[sl][:, 0:GT],
                                      u_sb[:, ic * 128:(ic + 1) * 128], y2_sb[:],
                                      start=False, stop=True)
                        fcmm_tok[ic] = ctx.inc(i, "pe")
                    mm_tok = ctx.n("pe")

                    @SCA
                    def _(s, ic=ic, sl=sl, mm_tok=mm_tok):
                        ctx.wait(s, "pe", mm_tok)
                        if ic >= 1:
                            ctx.wait(s, "vv", r2_tok[ic - 1])
                        s.activation(out=scrR[:], in_=ps_fc[sl][:, 0:GT], func=AF.Relu,
                                     scale=sfc_col[:, ic:ic + 1])
                        s.drain()
                        i = s.activation(out=scrR2[:], in_=scrR[:], func=AF.Square)
                        relu_tok[ic] = ctx.inc(i, "sc")

                    @VEC
                    def _(v, ic=ic):
                        ctx.wait(v, "sc", relu_tok[ic])
                        if ic == 0 and g_idx > 0:
                            ctx.wait(v, "pe", state[("pjmm", g_idx - 1)])
                        v.tensor_copy(out=r2h[:, ic], in_=scrR2[:])
                        v.drain()
                        i = v.tensor_tensor(out=r2l[:, ic], in0=scrR2[:], in1=r2h[:, ic],
                                            op=ALU.subtract)
                        r2_tok[ic] = ctx.inc(i, "vv")

                state[("fcmm", g_idx)] = ctx.n("pe")
                if pump is not None:
                    pump(10 ** 9)
                tok_r2 = ctx.n("vv")

                pjev_tok = {}
                pjmm_tok = {}
                for fc8 in range(DC):
                    sl = fc8 % 3
                    psl = fc8 % NTP

                    @blk.sync
                    def _(sy, fc8=fc8, psl=psl):
                        if g_idx == 0:
                            sy.wait_ge(sem("stw_pj"), 64 * (fc8 + 1))
                        if fc8 >= NTP:
                            ctx.wait(sy, "pe", pjmm_tok[fc8 - NTP])
                        elif g_idx > 0:
                            ctx.wait(sy, "pe", state[("pjmm", g_idx - 1)])
                        ctx.inc(sy.dma_start(out=tpj_buf[:, psl], in_=ternpj_dram[:, fc8]),
                                f"ldtp{psl}", 16)
                    ldtp_tok = ctx.n(f"ldtp{psl}")

                    @blk.tensor
                    def _(te, fc8=fc8, sl=sl, psl=psl, ldtp_tok=ldtp_tok):
                        te.wait_ge(sem(f"ldtp{psl}"), ldtp_tok)
                        if fc8 == 0:
                            ctx.wait(te, "vv", tok_r2)
                        if fc8 >= 3:
                            ctx.wait(te, "vv", pjev_tok[fc8 - 3])
                        for rp in (r2h, r2l):
                            for ic in range(HC):
                                i = te.matmul(ps_fc[sl][:, 0:GT], tpj_buf[:, psl, ic],
                                              rp[:, ic], start=(rp is r2h and ic == 0),
                                              stop=(rp is r2l and ic == HC - 1))
                        pjmm_tok[fc8] = ctx.inc(i, "pe")
                    mm_tok = ctx.n("pe")

                    @VEC
                    def _(v, fc8=fc8, sl=sl, mm_tok=mm_tok):
                        ctx.wait(v, "pe", mm_tok)
                        i = v.scalar_tensor_tensor(out=out_acc[:, fc8],
                                                   in0=ps_fc[sl][:, 0:GT],
                                                   scalar=spj_col[:, fc8:fc8 + 1],
                                                   in1=out_acc[:, fc8],
                                                   op0=ALU.mult, op1=ALU.add)
                        pjev_tok[fc8] = ctx.inc(i, "vv")
                state[("pjev", g_idx)] = ctx.n("vv")
                state[("pjmm", g_idx)] = ctx.n("pe")

            def softmax_phase():
                exp_tok = {}
                w_tok = {}
                trev_tok = {}
                for qc in range(DC):
                    sl = qc % 2

                    @blk.sync
                    def _(sy, qc=qc, sl=sl):
                        if qc == 0:
                            ctx.wait(sy, "cc", 1)
                        if qc >= 2:
                            ctx.wait(sy, "sc", exp_tok[qc - 2])
                        ctx.inc(sy.dma_start(out=smS[:, sl], in_=S_full[qc]),
                                f"ldS{sl}", 16)
                    ld_tok = ctx.n(f"ldS{sl}")

                    @VEC
                    def _(v, qc=qc, sl=sl, ld_tok=ld_tok):
                        v.wait_ge(sem(f"ldS{sl}"), ld_tok)
                        if qc >= 1:
                            ctx.wait(v, "sc", exp_tok[qc - 1])
                        i = v.tensor_reduce(out=nmax[:], in_=smS[:, sl], axis=AX.X,
                                            op=ALU.max, negate=True)
                        ctx.inc(i, "vv")
                    red_tok = ctx.n("vv")

                    @SCA
                    def _(s, qc=qc, sl=sl, red_tok=red_tok):
                        ctx.wait(s, "vv", red_tok)
                        if qc >= 1:
                            ctx.wait(s, "vv", w_tok[qc - 1])
                        i = s.activation(out=scrE[:], in_=smS[:, sl], func=AF.Exp,
                                         bias=nmax[:, 0:1], accum_out=ssum[:, 0:1])
                        exp_tok[qc] = ctx.inc(i, "sc")

                    @VEC
                    def _(v, qc=qc):
                        ctx.wait(v, "sc", exp_tok[qc])
                        v.reciprocal(out=rsum[:], in_=ssum[:])
                        v.drain()
                        i = v.tensor_scalar(out=scrE[:], in0=scrE[:],
                                            scalar1=rsum[:, 0:1], scalar2=None,
                                            op0=ALU.mult)
                        w_tok[qc] = ctx.inc(i, "vv")

                    for kc in range(DC):
                        idx = qc * DC + kc

                        @blk.tensor
                        def _(te, qc=qc, kc=kc, idx=idx):
                            if kc == 0:
                                ctx.wait(te, "vv", w_tok[qc])
                                if qc == 0:
                                    ctx.wait(te, "sc", pj_state[("tev", DC * 4 * 8 - 1)])
                            if idx >= 2:
                                ctx.wait(te, "vv", trev_tok[idx - 2])
                            i = te.transpose(ps_tr2[idx % 2][:],
                                             scrE[:, kc * 128:(kc + 1) * 128], identF[:])
                            ctx.inc(i, "pe")
                        tr_tok = ctx.n("pe")

                        @VEC
                        def _(v, qc=qc, kc=kc, idx=idx, tr_tok=tr_tok):
                            ctx.wait(v, "pe", tr_tok)
                            if qc >= 1 and kc == 0:
                                v.wait_ge(sem("spW"), 32 * qc)
                            v.tensor_copy(out=stageWh[:, kc], in_=ps_tr2[idx % 2][:])
                            v.drain()
                            i = v.tensor_tensor(out=stageWl[:, kc], in0=ps_tr2[idx % 2][:],
                                                in1=stageWh[:, kc], op=ALU.subtract)
                            trev_tok[idx] = ctx.inc(i, "vv")

                    @SCA
                    def _(s, qc=qc):
                        ctx.wait(s, "vv", trev_tok[qc * DC + DC - 1])
                        ctx.inc(s.dma_start(out=wT_dramh[qc].rearrange("kc p i -> p kc i"),
                                            in_=stageWh[:]), "spW", 16)
                        ctx.inc(s.dma_start(out=wT_draml[qc].rearrange("kc p i -> p kc i"),
                                            in_=stageWl[:]), "spW", 16)
                return ctx.n("spW")

            def attn_group(g_idx, n_spW, pj_done_vv=None):
                p = g_idx % 2
                out_acc = out_acc2[:, p]
                gsl = slice(g_idx * GT, (g_idx + 1) * GT)

                @blk.sync
                def _(sy):
                    if g_idx == 0:
                        sy.wait_ge(sem("spv0"), n_spv0)
                        sy.wait_ge(sem("spv1"), n_spv1)
                        sy.wait_ge(sem("spW"), n_spW)
                    else:
                        ctx.wait(sy, "pe", state[("avmm", g_idx - 1)])
                    for kc in range(DC):
                        ctx.inc(sy.dma_start(out=v_gh[:, kc], in_=v_sph[kc, :, gsl]),
                                "ldvg", 16)
                        ctx.inc(sy.dma_start(out=v_gl[:, kc], in_=v_spl[kc, :, gsl]),
                                "ldvg", 16)
                ldv_tok = ctx.n("ldvg")

                avev_tok = {}
                avmm_tok = {}
                for qc in range(DC):
                    sl = qc % 2

                    @blk.sync
                    def _(sy, qc=qc, sl=sl):
                        if g_idx == 0 and qc == 0:
                            sy.wait_ge(sem("spW"), n_spW)
                        if qc >= 2:
                            ctx.wait(sy, "pe", avmm_tok[qc - 2])
                        elif g_idx > 0:
                            ctx.wait(sy, "pe", state[("avmm", g_idx - 1)])
                        ctx.inc(sy.dma_start(out=wTbufh[:, sl],
                                             in_=wT_dramh[qc].rearrange("kc p i -> p kc i")),
                                f"ldwT{sl}", 16)
                        ctx.inc(sy.dma_start(out=wTbufl[:, sl],
                                             in_=wT_draml[qc].rearrange("kc p i -> p kc i")),
                                f"ldwT{sl}", 16)
                    ldt_tok = ctx.n(f"ldwT{sl}")

                    @blk.tensor
                    def _(te, qc=qc, sl=sl, ldt_tok=ldt_tok):
                        te.wait_ge(sem(f"ldwT{sl}"), ldt_tok)
                        if qc == 0:
                            te.wait_ge(sem("ldvg"), ldv_tok)
                            ctx.wait(te, "vv", state[("pjev", g_idx)])
                            if pj_done_vv is not None:
                                ctx.wait(te, "sc", pj_done_vv)
                        if qc >= 2:
                            ctx.wait(te, "vv", avev_tok[qc - 2])
                        for wop, vop in ((wTbufh, v_gh), (wTbufh, v_gl), (wTbufl, v_gh)):
                            for kc in range(DC):
                                i = te.matmul(ps_av[sl][:, 0:GT],
                                              wop[:, sl, kc * 128:(kc + 1) * 128],
                                              vop[:, kc],
                                              start=(wop is wTbufh and vop is v_gh and kc == 0),
                                              stop=(wop is wTbufl and kc == DC - 1))
                        avmm_tok[qc] = ctx.inc(i, "pe")

                    @VEC
                    def _(v, qc=qc, sl=sl):
                        ctx.wait(v, "pe", avmm_tok[qc])
                        if g_idx > 0 and qc == 0:
                            ctx.wait(v, "pe", state[("o1mm", g_idx - 1)])
                        i = v.tensor_copy(out=a_g[:, qc], in_=ps_av[sl][:, 0:GT])
                        avev_tok[qc] = ctx.inc(i, "vv")
                state[("avmm", g_idx)] = ctx.n("pe")
                tok_a = ctx.n("vv")

                o1ev_tok = {}
                o1mm_tok = {}
                for oc in range(DC):
                    sl = oc % 2

                    @blk.sync
                    def _(sy, oc=oc, sl=sl):
                        if oc >= 2:
                            ctx.wait(sy, "pe", o1mm_tok[oc - 2])
                        elif g_idx > 0:
                            ctx.wait(sy, "pe", state[("o1mm", g_idx - 1)])
                        ctx.inc(sy.dma_start(out=wpj_buf[:, sl],
                                             in_=wprojT[oc].rearrange("(ac p) j -> p ac j", p=128)),
                                f"ldpj{sl}", 16)
                    ldpj_tok = ctx.n(f"ldpj{sl}")

                    @blk.tensor
                    def _(te, oc=oc, sl=sl, ldpj_tok=ldpj_tok):
                        te.wait_ge(sem(f"ldpj{sl}"), ldpj_tok)
                        if oc == 0:
                            ctx.wait(te, "vv", tok_a)
                        if oc >= 2:
                            ctx.wait(te, "vv", o1ev_tok[oc - 2])
                        for ac in range(DC):
                            i = te.matmul(ps_av[sl][:, 0:GT],
                                          wpj_buf[:, sl, ac],
                                          a_g[:, ac], start=(ac == 0), stop=(ac == DC - 1))
                        o1mm_tok[oc] = ctx.inc(i, "pe")
                    mm_tok = ctx.n("pe")

                    @VEC
                    def _(v, oc=oc, sl=sl, mm_tok=mm_tok):
                        ctx.wait(v, "pe", mm_tok)
                        i = v.scalar_tensor_tensor(out=out_acc[:, oc],
                                                   in0=ps_av[sl][:, 0:GT],
                                                   scalar=pb_sb[:, oc:oc + 1],
                                                   in1=out_acc[:, oc],
                                                   op0=ALU.add, op1=ALU.add)
                        o1ev_tok[oc] = ctx.inc(i, "vv")
                state[("o1mm", g_idx)] = ctx.n("pe")
                tok_out = ctx.n("vv")

                @blk.gpsimd
                def _(gp):
                    ctx.wait(gp, "vv", tok_out)
                    d = gp.dma_start(out=out_dram[g_idx].rearrange("dc p t -> p dc t"),
                                     in_=out_acc[:])
                    ctx.inc(d, "out_w", 16)

            prologue(0)
            pump_pj(10)
            ffn_mid(0, pump=pump_pj)
            pj_done_vv = ctx.n("sc")
            if NG > 1:
                prologue(1)
            n_spW = softmax_phase()
            attn_group(0, n_spW, pj_done_vv)
            for g_idx in range(1, NG):
                ffn_mid(g_idx)
                if g_idx + 1 < NG:
                    prologue(g_idx + 1)
                attn_group(g_idx, n_spW)

            @blk.gpsimd
            def _(gp):
                gp.wait_ge(sem("out_w"), 16 * NG)

    return nc


def shard_inputs(T, x, norm_w, norm_b, qkv_w, qkv_b, proj_w, proj_b,
                 fast_b, w_fc, w_proj, u, v, gate):
    x = np.asarray(x, np.float32)
    f32c = lambda a: np.ascontiguousarray(np.asarray(a, np.float32))
    qkv_w = np.asarray(qkv_w, np.float32)
    qkv_b = np.asarray(qkv_b, np.float32)
    shared = dict(
        wqT=f32c(qkv_w[0:D].T), wkT=f32c(qkv_w[D:2 * D].T), wvT=f32c(qkv_w[2 * D:3 * D].T),
        wprojT=np.ascontiguousarray(
            np.asarray(proj_w, np.float32).T.reshape(D, DC, 128).transpose(1, 0, 2)),
        wfc=f32c(w_fc), wpj=f32c(w_proj),
        vlrT=f32c(np.asarray(v).T), fbT=f32c(np.asarray(fast_b).T), uT=f32c(np.asarray(u).T),
        bq_row=f32c(qkv_b[0:D][None, :]), bk_row=f32c(qkv_b[D:2 * D][None, :]),
        bv_col=f32c(qkv_b[2 * D:3 * D].reshape(DC, 128).T),
        pb_col=f32c(np.asarray(proj_b).reshape(DC, 128).T),
        gam_col=f32c(np.asarray(norm_w).reshape(DC, 128).T),
        bet_col=f32c(np.asarray(norm_b).reshape(DC, 128).T),
        gate_in=f32c(np.asarray(gate).reshape(1, 1)),
    )
    in_maps = []
    for c in range(NCORES):
        g, j = divmod(c, 4)
        m = dict(shared)
        m["xT"] = f32c(x[g, j * T:(j + 1) * T, :].T)
        in_maps.append(m)
    return in_maps


def assemble_output(T, results):
    GT = min(256, T)
    NG = T // GT
    out = np.empty((2, 4 * T, D), np.float32)
    for c in range(NCORES):
        g, j = divmod(c, 4)
        o = np.asarray(results[c]["outT"])
        for gg in range(NG):
            blkv = o[gg].reshape(D, GT)
            out[g, j * T + gg * GT:j * T + (gg + 1) * GT, :] = blkv.T
    return out



_CACHED = {}


def kernel(L0_attn_x, L0_attn_norm_w, L0_attn_norm_b, L0_attn_qkv_w, L0_attn_qkv_b,
           L0_attn_proj_w, L0_attn_proj_b, L0_ffn_fast_b, L0_ffn_w_fc, L0_ffn_w_proj,
           L0_ffn_u, L0_ffn_v, L0_ffn_gate):
    from concourse.bass_utils import run_bass_kernel_spmd
    T = 2048
    if "nc" not in _CACHED:
        _CACHED["nc"] = build_nc(T)
    nc = _CACHED["nc"]
    in_maps = shard_inputs(
        T, L0_attn_x, L0_attn_norm_w, L0_attn_norm_b, L0_attn_qkv_w, L0_attn_qkv_b,
        L0_attn_proj_w, L0_attn_proj_b, L0_ffn_fast_b, L0_ffn_w_fc, L0_ffn_w_proj,
        L0_ffn_u, L0_ffn_v, L0_ffn_gate)
    res = run_bass_kernel_spmd(nc, in_maps, core_ids=list(range(NCORES)))
    return assemble_output(T, [res.results[c] for c in range(NCORES)])


# version 12
# speedup vs baseline: 1.2057x; 1.2057x over previous
import numpy as np
from contextlib import ExitStack

import concourse.bass as bass
import concourse.mybir as mybir

F32 = mybir.dt.float32
BF16 = mybir.dt.bfloat16
ALU = mybir.AluOpType
AF = mybir.ActivationFunctionType
AX = mybir.AxisListType

D = 1024
DC = D // 128
HID = 4096
HC = HID // 128
RK = 16
EPS = 1e-5
NCORES = 8
GROUPS = [[0, 1, 2, 3], [4, 5, 6, 7]]


class Ctx:
    def __init__(self, nc, es):
        self.nc = nc
        self.es = es
        self.sems = {}
        self.cnt = {}

    def sem(self, name):
        if name not in self.sems:
            self.sems[name] = self.es.enter_context(self.nc.semaphore(name))
            self.cnt[name] = 0
        return self.sems[name]

    def inc(self, instr, name, k=1):
        instr.then_inc(self.sem(name), k)
        self.cnt[name] += k
        return self.cnt[name]

    def n(self, name):
        self.sem(name)
        return self.cnt[name]

    def wait(self, eng, name, value=None):
        v = self.cnt[name] if value is None else value
        if v > 0:
            eng.wait_ge(self.sem(name), v)


def build_nc(T):
    NT = T // 128
    GT = min(256, T)
    NG = T // GT
    L_full = 4 * T
    ATT_SCALE = float(np.float32(1.0 / np.sqrt(L_full)))
    NB = max(1, T // 512)
    WB = min(512, T)
    TB = max(1, T // 1024)
    VB = min(1024, T)

    nc = bass.Bass("TRN2", target_bir_lowering=False)

    xT = nc.dram_tensor("xT", [D, T], F32, kind="ExternalInput").ap()
    wqT = nc.dram_tensor("wqT", [D, D], F32, kind="ExternalInput").ap()
    wkT = nc.dram_tensor("wkT", [D, D], F32, kind="ExternalInput").ap()
    wvT = nc.dram_tensor("wvT", [D, D], F32, kind="ExternalInput").ap()
    wprojT = nc.dram_tensor("wprojT", [DC, D, 128], F32, kind="ExternalInput").ap()
    wfc = nc.dram_tensor("wfc", [HID, D], F32, kind="ExternalInput").ap()
    wpj = nc.dram_tensor("wpj", [D, HID], F32, kind="ExternalInput").ap()
    vlrT = nc.dram_tensor("vlrT", [D, RK], F32, kind="ExternalInput").ap()
    fbT = nc.dram_tensor("fbT", [RK, RK], F32, kind="ExternalInput").ap()
    uT = nc.dram_tensor("uT", [RK, HID], F32, kind="ExternalInput").ap()
    bq_row = nc.dram_tensor("bq_row", [1, D], F32, kind="ExternalInput").ap()
    bk_row = nc.dram_tensor("bk_row", [1, D], F32, kind="ExternalInput").ap()
    bv_col = nc.dram_tensor("bv_col", [128, DC], F32, kind="ExternalInput").ap()
    pb_col = nc.dram_tensor("pb_col", [128, DC], F32, kind="ExternalInput").ap()
    gam_col = nc.dram_tensor("gam_col", [128, DC], F32, kind="ExternalInput").ap()
    bet_col = nc.dram_tensor("bet_col", [128, DC], F32, kind="ExternalInput").ap()
    gate_in = nc.dram_tensor("gate_in", [1, 1], F32, kind="ExternalInput").ap()
    out_dram = nc.dram_tensor("outT", [NG, DC, 128, GT], F32, kind="ExternalOutput").ap()

    ternfc_dram = nc.dram_tensor("ternfc_d", [128, HC, DC, 128], BF16).ap()
    ternpj_dram = nc.dram_tensor("ternpj_d", [128, DC, HC, 128], BF16).ap()
    k_sph = nc.dram_tensor("k_sph", [NT, 128, D], BF16).ap()
    k_spl = nc.dram_tensor("k_spl", [NT, 128, D], BF16).ap()
    q_sph = nc.dram_tensor("q_sph", [NT, 128, D], BF16).ap()
    q_spl = nc.dram_tensor("q_spl", [NT, 128, D], BF16).ap()
    v_sph = nc.dram_tensor("v_sph", [DC, 128, T], BF16).ap()
    v_spl = nc.dram_tensor("v_spl", [DC, 128, T], BF16).ap()
    S_part = nc.dram_tensor("S_part", [DC, 128, D], F32).ap()
    S_full = nc.dram_tensor("S_full", [DC, 128, D], F32).ap()
    wT_dramh = nc.dram_tensor("wT_dramh", [DC, DC, 128, 128], BF16).ap()
    wT_draml = nc.dram_tensor("wT_draml", [DC, DC, 128, 128], BF16).ap()

    es = ExitStack()
    with es:
        ctx = Ctx(nc, es)
        sem = ctx.sem
        sbt = lambda name, shape, dt=F32: es.enter_context(nc.sbuf_tensor(name, shape, dt))
        gam_sb = sbt("gam_sb", [128, DC]); bet_sb = sbt("bet_sb", [128, DC])
        bv_sb = sbt("bv_sb", [128, DC]); pb_sb = sbt("pb_sb", [128, DC])
        vlr_sb = sbt("vlr_sb", [128, DC, RK])
        fb_sb = sbt("fb_sb", [RK, RK])
        gate_sb = sbt("gate_sb", [1, 1]); gate16 = sbt("gate16", [RK, 1])
        onec = sbt("onec", [128, 1]); oner = sbt("oner", [1, 128])
        eps_c = sbt("eps_c", [128, 1])
        identF = sbt("identF", [128, 128]); identB = sbt("identB", [128, 128], BF16)
        sfc_col = sbt("sfc_col", [128, HC]); spj_col = sbt("spj_col", [128, DC])
        r_row = sbt("r_row", [1, T]); mur_row = sbt("mur_row", [1, T])

        blk = es.enter_context(nc.Block())

        def VEC(fn):
            def w(v):
                v.drain()
                fn(v)
            blk.vector(w)

        def SCA(fn):
            def w(s):
                s.drain()
                fn(s)
            blk.scalar(w)

        @blk.sync
        def _(sy):
            for dst, src in [(gam_sb, gam_col), (bet_sb, bet_col),
                             (bv_sb, bv_col), (pb_sb, pb_col),
                             (fb_sb, fbT), (gate_sb, gate_in)]:
                ctx.inc(sy.dma_start(out=dst[:], in_=src), "sml", 16)
            ctx.inc(sy.dma_start(out=vlr_sb[:],
                                 in_=vlrT.rearrange("(dc p) r -> p dc r", p=128)),
                    "sml", 16)

        @blk.gpsimd
        def _(g):
            g.iota(identF[:], pattern=[[-1, 128]], base=0, channel_multiplier=1,
                   allow_small_or_imprecise_dtypes=True)
            ctx.inc(g.engine_nop(), "gp")

        @VEC
        def _(v):
            ctx.wait(v, "gp")
            v.memset(onec[:], 1.0)
            v.memset(oner[:], 1.0)
            v.memset(eps_c[:], EPS)
            v.tensor_scalar(out=identF[:], in0=identF[:], scalar1=0.0, scalar2=None,
                            op0=ALU.is_equal)
            v.drain()
            v.tensor_copy(out=identB[:], in_=identF[:])
            ctx.inc(v.drain(), "vv")
        tok_consts = ctx.n("vv")

        es0 = ExitStack()
        with es0:
            ps_g = es0.enter_context(nc.psum_tensor("ps_g", [128, 1], F32))

            @blk.tensor
            def _(te):
                ctx.wait(te, "vv", tok_consts)
                ctx.wait(te, "sml")
                ctx.inc(te.matmul(ps_g[:], oner[:], gate_sb[:], start=True, stop=True),
                        "pe")
            tok_g = ctx.n("pe")

            @VEC
            def _(v):
                ctx.wait(v, "pe", tok_g)
                v.tensor_copy(out=gate16[:], in_=ps_g[0:RK, :])
                v.drain()
                v.tensor_scalar(out=vlr_sb[:], in0=vlr_sb[:],
                                scalar1=ps_g[:, 0:1], scalar2=None, op0=ALU.mult)
                ctx.inc(v.drain(), "vv")
        tok_consts = ctx.n("vv")

        def prep_bitlinear(src_dram, n_chunks, width, s_col, tern_dram, tag,
                           wait_prev=None):
            nw = width // 128
            esb = ExitStack()
            with esb:
                wbuf = esb.enter_context(nc.sbuf_tensor(f"wbuf_{tag}", [128, 2, width], F32))
                absum = esb.enter_context(nc.sbuf_tensor(f"absum_{tag}", [128, 2], F32))
                scr = esb.enter_context(nc.sbuf_tensor(f"scr_{tag}", [128, 2, width], F32))
                tern = esb.enter_context(nc.sbuf_tensor(f"tern_{tag}", [128, 2, width], BF16))
                hp = esb.enter_context(nc.sbuf_tensor(f"hp_{tag}", [128, 2], F32))
                stage = esb.enter_context(nc.sbuf_tensor(f"stage_{tag}", [128, 2, nw, 128], BF16))
                ps_tr = [esb.enter_context(nc.psum_tensor(f"pstr{i}_{tag}", [128, 128], BF16))
                         for i in range(4)]
                TRN = f"trn_{tag}"
                TEV = f"tev_{tag}"
                TRD = f"trd_{tag}"
                STW = f"stw_{tag}"

                def emit_evac(ic):
                    @SCA
                    def _(s, ic=ic):
                        for j in range(nw):
                            idx = ic * nw + j
                            if j == 0 and ic >= 2:
                                s.wait_ge(sem(f"{STW}{ic % 2}"), 16 * (ic // 2))
                            ctx.wait(s, TRN, idx + 1)
                            i = s.activation(out=stage[:, ic % 2, j],
                                             in_=ps_tr[idx % 4][:], func=AF.Identity)
                            ctx.inc(i, TEV)
                        s.drain()
                        d = s.dma_start(out=tern_dram[:, ic], in_=stage[:, ic % 2])
                        ctx.inc(d, f"{STW}{ic % 2}", 16)

                for ic in range(n_chunks):
                    sl = ic % 2

                    @blk.sync
                    def _(sy, ic=ic, sl=sl):
                        if ic == 0 and wait_prev is not None:
                            for nm, val in wait_prev:
                                sy.wait_ge(sem(nm), val)
                        if ic >= 2:
                            ctx.wait(sy, TRN, (ic - 1) * nw)
                        d = sy.dma_start(out=wbuf[:, sl],
                                         in_=src_dram[ic * 128:(ic + 1) * 128, :])
                        ctx.inc(d, f"ldw{sl}_{tag}", 16)

                    @SCA
                    def _(s, ic=ic, sl=sl):
                        s.wait_ge(sem(f"ldw{sl}_{tag}"), 16 * (ic // 2 + 1))
                        if ic >= 2:
                            ctx.wait(s, TRD, ic - 1)
                        i = s.activation(out=scr[:, sl], in_=wbuf[:, sl], func=AF.Abs,
                                         accum_out=absum[:, sl:sl + 1])
                        ctx.inc(i, "sc")
                    tok_abs = ctx.n("sc")

                    @VEC
                    def _(v, ic=ic, sl=sl, tok_abs=tok_abs):
                        ctx.wait(v, "sc", tok_abs)
                        if ic == 0 and tag == "fc":
                            ctx.wait(v, "vv", tok_consts)
                        if ic >= 2:
                            ctx.wait(v, TRN, (ic - 1) * nw)
                        v.tensor_scalar(out=s_col[:, ic:ic + 1], in0=absum[:, sl:sl + 1],
                                        scalar1=1.0 / width, scalar2=1e-5,
                                        op0=ALU.mult, op1=ALU.max)
                        v.drain()
                        v.tensor_scalar(out=hp[:, sl:sl + 1], in0=s_col[:, ic:ic + 1],
                                        scalar1=0.5, scalar2=None, op0=ALU.mult)
                        v.drain()
                        v.tensor_scalar(out=scr[:, sl], in0=wbuf[:, sl],
                                        scalar1=hp[:, sl:sl + 1], scalar2=None,
                                        op0=ALU.is_gt)
                        v.tensor_scalar(out=tern[:, sl], in0=wbuf[:, sl], scalar1=-1.0,
                                        scalar2=hp[:, sl:sl + 1], op0=ALU.mult,
                                        op1=ALU.is_gt)
                        v.drain()
                        i = v.tensor_tensor(out=tern[:, sl], in0=scr[:, sl], in1=tern[:, sl],
                                            op=ALU.subtract)
                        ctx.inc(i, TRD)

                    @blk.tensor
                    def _(te, ic=ic, sl=sl):
                        ctx.wait(te, TRD, ic + 1)
                        for j in range(nw):
                            idx = ic * nw + j
                            if idx >= 4:
                                ctx.wait(te, TEV, idx - 3)
                            i = te.transpose(ps_tr[idx % 4][:],
                                             tern[:, sl, j * 128:(j + 1) * 128], identB[:])
                            ctx.inc(i, TRN)

                    if ic >= 1:
                        emit_evac(ic - 1)
                emit_evac(n_chunks - 1)

        prep_bitlinear(wfc, HC, D, sfc_col, ternfc_dram, "fc")
        prep_bitlinear(wpj, DC, HID, spj_col, ternpj_dram, "pj",
                       wait_prev=[("stw_fc0", ctx.n("stw_fc0")),
                                  ("stw_fc1", ctx.n("stw_fc1"))])
        stw_fin = [(nm, ctx.n(nm)) for nm in
                   ("stw_fc0", "stw_fc1", "stw_pj0", "stw_pj1")]

        es3 = ExitStack()
        with es3:
            hh_sb = es3.enter_context(nc.sbuf_tensor("hh_sb", [128, DC, T], BF16))
            hl_sb = es3.enter_context(nc.sbuf_tensor("hl_sb", [128, DC, T], BF16))

            es4 = ExitStack()
            with es4:
                xbuf = es4.enter_context(nc.sbuf_tensor("xbuf", [128, 2, T], F32))
                scrA = es4.enter_context(nc.sbuf_tensor("scrA", [128, T], F32))
                rowA = es4.enter_context(nc.sbuf_tensor("rowA", [1, T], F32))
                rowB = es4.enter_context(nc.sbuf_tensor("rowB", [1, T], F32))
                rowC = es4.enter_context(nc.sbuf_tensor("rowC", [1, T], F32))
                bc1 = es4.enter_context(nc.sbuf_tensor("bc1", [128, T], F32))
                bc2 = es4.enter_context(nc.sbuf_tensor("bc2", [128, T], F32))

                esPS = ExitStack()
                esPS.__enter__()
                ps_sx = esPS.enter_context(nc.psum_tensor("ps_sx", [1, T], F32))
                ps_sx2 = esPS.enter_context(nc.psum_tensor("ps_sx2", [1, T], F32))

                pe_sq_tok = {}
                for dc in range(DC):
                    sl = dc % 2

                    @blk.sync
                    def _(sy, dc=dc, sl=sl):
                        if dc == 0:
                            for nm, val in stw_fin:
                                sy.wait_ge(sem(nm), val)
                        if dc >= 2:
                            ctx.wait(sy, "pe", pe_sq_tok[dc - 2])
                        d = sy.dma_start(out=xbuf[:, sl],
                                         in_=xT[dc * 128:(dc + 1) * 128, :])
                        ctx.inc(d, f"ldx{sl}", 16)
                    ldx_tok = ctx.n(f"ldx{sl}")

                    @SCA
                    def _(s, dc=dc, sl=sl, ldx_tok=ldx_tok):
                        s.wait_ge(sem(f"ldx{sl}"), ldx_tok)
                        if dc == 0:
                            for nm, val in stw_fin:
                                s.wait_ge(sem(nm), val)
                        else:
                            ctx.wait(s, "pe", pe_sq_tok[dc - 1])
                        i = s.activation(out=scrA[:], in_=xbuf[:, sl], func=AF.Square)
                        ctx.inc(i, "sc")
                    tok_sq = ctx.n("sc")

                    @blk.tensor
                    def _(te, dc=dc, sl=sl, tok_sq=tok_sq):
                        if dc == 0:
                            ctx.wait(te, "vv", tok_consts)
                        ctx.wait(te, "sc", tok_sq)
                        for b in range(NB):
                            te.matmul(ps_sx[0:1, b * 512:b * 512 + WB], onec[:],
                                      xbuf[:, sl, b * 512:b * 512 + WB],
                                      start=(dc == 0), stop=(dc == DC - 1))
                            i = te.matmul(ps_sx2[0:1, b * 512:b * 512 + WB], onec[:],
                                          scrA[:, b * 512:b * 512 + WB],
                                          start=(dc == 0), stop=(dc == DC - 1))
                        pe_sq_tok[dc] = ctx.inc(i, "pe")
                stats_pe = ctx.n("pe")

                @VEC
                def _(v):
                    ctx.wait(v, "pe", stats_pe)
                    v.tensor_scalar(out=rowA[:], in0=ps_sx[:], scalar1=1.0 / D,
                                    scalar2=None, op0=ALU.mult)
                    v.tensor_scalar(out=rowB[:], in0=ps_sx2[:], scalar1=1.0 / D,
                                    scalar2=None, op0=ALU.mult)
                    v.drain()
                    v.tensor_tensor(out=rowC[:], in0=rowA[:], in1=rowA[:], op=ALU.mult)
                    v.drain()
                    v.tensor_tensor(out=rowB[:], in0=rowB[:], in1=rowC[:],
                                    op=ALU.subtract)
                    ctx.inc(v.drain(), "vv")
                tok_var = ctx.n("vv")

                @SCA
                def _(s):
                    ctx.wait(s, "vv", tok_var)
                    ctx.inc(s.activation(out=rowC[:], in_=rowB[:], func=AF.Sqrt,
                                         bias=eps_c[0:1, 0:1]), "sc")
                tok_sd = ctx.n("sc")

                @VEC
                def _(v):
                    ctx.wait(v, "sc", tok_sd)
                    v.reciprocal(out=r_row[:], in_=rowC[:])
                    v.drain()
                    v.tensor_tensor(out=mur_row[:], in0=rowA[:], in1=r_row[:], op=ALU.mult)
                    v.tensor_tensor(out=rowC[:], in0=rowB[:], in1=r_row[:], op=ALU.mult)
                    v.drain()
                    v.tensor_tensor(out=rowC[:], in0=rowC[:], in1=r_row[:], op=ALU.mult)
                    ctx.inc(v.drain(), "vv")
                tok_v2 = ctx.n("vv")

                @SCA
                def _(s):
                    ctx.wait(s, "vv", tok_v2)
                    ctx.inc(s.activation(out=rowB[:], in_=rowC[:], func=AF.Sqrt,
                                         bias=eps_c[0:1, 0:1]), "sc")
                tok_sd2 = ctx.n("sc")

                @VEC
                def _(v):
                    ctx.wait(v, "sc", tok_sd2)
                    v.reciprocal(out=rowC[:], in_=rowB[:])
                    v.drain()
                    v.tensor_tensor(out=rowA[:], in0=r_row[:], in1=rowC[:], op=ALU.mult)
                    v.drain()
                    v.tensor_tensor(out=rowB[:], in0=mur_row[:], in1=rowC[:], op=ALU.mult)
                    ctx.inc(v.drain(), "vv")
                tok_rows = ctx.n("vv")

                esPS.__exit__(None, None, None)
                esBC = ExitStack()
                esBC.__enter__()
                ps_bc = esBC.enter_context(nc.psum_tensor("ps_bc", [128, 512], F32))
                bc_toks = []
                for bi, (dst, srow) in enumerate([(bc1, rowA), (bc2, rowB)]):
                    for b in range(NB):
                        w = min(512, T)

                        @blk.tensor
                        def _(te, bi=bi, b=b, w=w, dst=dst, srow=srow):
                            ctx.wait(te, "vv", tok_rows)
                            if bc_toks:
                                ctx.wait(te, "vv", bc_toks[-1])
                            ctx.inc(te.matmul(ps_bc[:, 0:w], oner[:],
                                              srow[:, b * 512:b * 512 + w],
                                              start=True, stop=True), "pe")
                        bc_pe = ctx.n("pe")

                        @VEC
                        def _(v, bi=bi, b=b, w=w, dst=dst, bc_pe=bc_pe):
                            ctx.wait(v, "pe", bc_pe)
                            i = v.tensor_copy(out=dst[:, b * 512:b * 512 + w],
                                              in_=ps_bc[:, 0:w])
                            bc_toks.append(ctx.inc(i, "vv"))
                esBC.__exit__(None, None, None)
                tok_bc = ctx.n("vv")

                h_tok = {}
                for dc in range(DC):
                    sl = dc % 2

                    @blk.sync
                    def _(sy, dc=dc, sl=sl):
                        if dc < 2:
                            ctx.wait(sy, "pe", stats_pe)
                        else:
                            ctx.wait(sy, "vv", h_tok[dc - 2])
                        d = sy.dma_start(out=xbuf[:, sl],
                                         in_=xT[dc * 128:(dc + 1) * 128, :])
                        ctx.inc(d, f"ldx{sl}", 16)
                    ldh_tok = ctx.n(f"ldx{sl}")

                    @VEC
                    def _(v, dc=dc, sl=sl, ldh_tok=ldh_tok):
                        v.wait_ge(sem(f"ldx{sl}"), ldh_tok)
                        if dc == 0:
                            ctx.wait(v, "vv", tok_bc)
                        v.tensor_tensor(out=scrA[:], in0=xbuf[:, sl], in1=bc1[:],
                                        op=ALU.mult)
                        v.drain()
                        v.tensor_tensor(out=scrA[:], in0=scrA[:], in1=bc2[:],
                                        op=ALU.subtract)
                        v.drain()
                        v.tensor_scalar(out=scrA[:], in0=scrA[:],
                                        scalar1=gam_sb[:, dc:dc + 1],
                                        scalar2=bet_sb[:, dc:dc + 1],
                                        op0=ALU.mult, op1=ALU.add)
                        v.drain()
                        v.tensor_copy(out=hh_sb[:, dc], in_=scrA[:])
                        v.drain()
                        i = v.tensor_tensor(out=hl_sb[:, dc], in0=scrA[:],
                                            in1=hh_sb[:, dc], op=ALU.subtract)
                        h_tok[dc] = ctx.inc(i, "vv")
                tok_h = ctx.n("vv")

            es5 = ExitStack()
            with es5:
                wbuf = es5.enter_context(nc.sbuf_tensor("wbuf", [128, DC, D], F32))
                wbh = es5.enter_context(nc.sbuf_tensor("wbh", [128, DC, D], BF16))
                wbl = es5.enter_context(nc.sbuf_tensor("wbl", [128, DC, D], BF16))
                bq_sb = es5.enter_context(nc.sbuf_tensor("bq_sb", [1, D], F32))
                bk_sb = es5.enter_context(nc.sbuf_tensor("bk_sb", [1, D], F32))
                stage_vh = es5.enter_context(nc.sbuf_tensor("stage_vh", [128, 2, VB], BF16))
                stage_vl = es5.enter_context(nc.sbuf_tensor("stage_vl", [128, 2, VB], BF16))
                stage_kqh = es5.enter_context(nc.sbuf_tensor("stage_kqh", [128, 2, D], BF16))
                stage_kql = es5.enter_context(nc.sbuf_tensor("stage_kql", [128, 2, D], BF16))
                ps_kq = [es5.enter_context(nc.psum_tensor(f"ps_kq{i}", [128, D], F32))
                         for i in range(2)]

                def split_wbuf(tag, ld_n):
                    @VEC
                    def _(v):
                        v.wait_ge(sem(f"ldw{tag}"), ld_n)
                        v.tensor_copy(out=wbh[:].rearrange("p a b -> p (a b)"),
                                      in_=wbuf[:].rearrange("p a b -> p (a b)"))
                        v.drain()
                        i = v.tensor_tensor(out=wbl[:].rearrange("p a b -> p (a b)"),
                                            in0=wbuf[:].rearrange("p a b -> p (a b)"),
                                            in1=wbh[:].rearrange("p a b -> p (a b)"),
                                            op=ALU.subtract)
                        ctx.inc(i, "vv")
                    return ctx.n("vv")

                @blk.sync
                def _(sy):
                    ctx.wait(sy, "vv", tok_h)
                    ctx.inc(sy.dma_start(out=bq_sb[:], in_=bq_row), "ldwv", 16)
                    ctx.inc(sy.dma_start(out=bk_sb[:], in_=bk_row), "ldwv", 16)
                    for dc in range(DC):
                        d = sy.dma_start(out=wbuf[:, dc],
                                         in_=wvT[dc * 128:(dc + 1) * 128, :])
                        ctx.inc(d, "ldwv", 16)

                tok_wv = split_wbuf("v", 16 * (DC + 2))
                vev_tok = {}
                vmm_tok = {}
                for vf in range(DC):
                    for tb in range(TB):
                        vi = vf * TB + tb
                        sl = vi % 2

                        @blk.tensor
                        def _(te, vf=vf, tb=tb, vi=vi, sl=sl):
                            if vi == 0:
                                ctx.wait(te, "vv", tok_wv)
                                ctx.wait(te, "vv", tok_h)
                            if vi >= 2:
                                ctx.wait(te, "vv", vev_tok[vi - 2])
                            for b in range(VB // 512 if VB >= 512 else 1):
                                w = min(512, VB)
                                t0 = tb * VB + b * 512
                                for wop, hop in ((wbh, hh_sb), (wbh, hl_sb), (wbl, hh_sb)):
                                    for dc in range(DC):
                                        i = te.matmul(
                                            ps_kq[sl][:, b * 512:b * 512 + w],
                                            wop[:, dc, vf * 128:(vf + 1) * 128],
                                            hop[:, dc, t0:t0 + w],
                                            start=(wop is wbh and hop is hh_sb and dc == 0),
                                            stop=(wop is wbl and dc == DC - 1))
                            vmm_tok[vi] = ctx.inc(i, "pe")

                        @VEC
                        def _(v, vf=vf, tb=tb, vi=vi, sl=sl):
                            ctx.wait(v, "pe", vmm_tok[vi])
                            if vi >= 2:
                                v.wait_ge(sem(f"spv{vi % 2}"), 32 * (vi // 2))
                            v.tensor_scalar(out=stage_vh[:, vi % 2],
                                            in0=ps_kq[sl][:, 0:VB],
                                            scalar1=bv_sb[:, vf:vf + 1], scalar2=None,
                                            op0=ALU.add)
                            v.drain()
                            i = v.scalar_tensor_tensor(out=stage_vl[:, vi % 2],
                                                       in0=ps_kq[sl][:, 0:VB],
                                                       scalar=bv_sb[:, vf:vf + 1],
                                                       in1=stage_vh[:, vi % 2],
                                                       op0=ALU.add, op1=ALU.subtract)
                            vev_tok[vi] = ctx.inc(i, "vv")

                        @SCA
                        def _(s, vf=vf, tb=tb, vi=vi):
                            ctx.wait(s, "vv", vev_tok[vi])
                            d = s.dma_start(out=v_sph[vf, :, tb * VB:(tb + 1) * VB],
                                            in_=stage_vh[:, vi % 2])
                            ctx.inc(d, f"spv{vi % 2}", 16)
                            d = s.dma_start(out=v_spl[vf, :, tb * VB:(tb + 1) * VB],
                                            in_=stage_vl[:, vi % 2])
                            ctx.inc(d, f"spv{vi % 2}", 16)

                def kq_phase(w_dram, b_sb_row, scale, sp_h, sp_l, tag, prev_spill=()):
                    ldname = f"ldw{tag}"
                    spname = f"sp{tag}"

                    @blk.sync
                    def _(sy):
                        ctx.wait(sy, "pe")
                        ctx.wait(sy, "vv")
                        for dc in range(DC):
                            d = sy.dma_start(out=wbuf[:, dc],
                                             in_=w_dram[dc * 128:(dc + 1) * 128, :])
                            ctx.inc(d, ldname, 16)

                    tok_w = split_wbuf(tag, 16 * DC)
                    ev_tok = {}
                    last_vv = ctx.n("vv")
                    for tt in range(NT):
                        sl = tt % 2

                        @blk.tensor
                        def _(te, tt=tt, sl=sl, last_vv=last_vv):
                            if tt == 0:
                                ctx.wait(te, "vv", tok_w)
                                ctx.wait(te, "vv", last_vv)
                            if tt >= 2:
                                ctx.wait(te, "vv", ev_tok[tt - 2])
                            for b in range(2):
                                for hop, wop in ((hh_sb, wbh), (hh_sb, wbl), (hl_sb, wbh)):
                                    for dc in range(DC):
                                        te.matmul(ps_kq[sl][:, b * 512:(b + 1) * 512],
                                                  hop[:, dc, tt * 128:(tt + 1) * 128],
                                                  wop[:, dc, b * 512:(b + 1) * 512],
                                                  start=(hop is hh_sb and wop is wbh and dc == 0),
                                                  stop=False)
                                i = te.matmul(ps_kq[sl][:, b * 512:(b + 1) * 512],
                                              oner[:], b_sb_row[:, b * 512:(b + 1) * 512],
                                              start=False, stop=True)
                            ctx.inc(i, "pe")
                        mm_tok = ctx.n("pe")

                        @VEC
                        def _(v, tt=tt, sl=sl, mm_tok=mm_tok):
                            ctx.wait(v, "pe", mm_tok)
                            if tt < 2:
                                for nm, val in prev_spill:
                                    v.wait_ge(sem(nm), val)
                            else:
                                v.wait_ge(sem(f"{spname}{tt % 2}"), 32 * (tt // 2))
                            v.tensor_scalar(out=stage_kqh[:, tt % 2], in0=ps_kq[sl][:],
                                            scalar1=scale, scalar2=None, op0=ALU.mult)
                            v.drain()
                            i = v.scalar_tensor_tensor(out=stage_kql[:, tt % 2],
                                                       in0=ps_kq[sl][:], scalar=scale,
                                                       in1=stage_kqh[:, tt % 2],
                                                       op0=ALU.mult, op1=ALU.subtract)
                            ev_tok[tt] = ctx.inc(i, "vv")

                        @SCA
                        def _(s, tt=tt):
                            ctx.wait(s, "vv", ev_tok[tt])
                            ctx.inc(s.dma_start(out=sp_h[tt], in_=stage_kqh[:, tt % 2]),
                                    f"{spname}{tt % 2}", 16)
                            ctx.inc(s.dma_start(out=sp_l[tt], in_=stage_kql[:, tt % 2]),
                                    f"{spname}{tt % 2}", 16)

                kq_phase(wkT, bk_sb, ATT_SCALE, k_sph, k_spl, "k")
                kq_phase(wqT, bq_sb, 1.0, q_sph, q_spl, "q",
                         prev_spill=[("spk0", ctx.n("spk0")), ("spk1", ctx.n("spk1"))])
            n_spk0 = ctx.n("spk0"); n_spk1 = ctx.n("spk1")
            n_spq0 = ctx.n("spq0"); n_spq1 = ctx.n("spq1")
            n_spv0 = ctx.n("spv0"); n_spv1 = ctx.n("spv1")
            bar_kq_pe = ctx.n("pe")
            bar_kq_vv = ctx.n("vv")

            es6 = ExitStack()
            with es6:
                kbufh = es6.enter_context(nc.sbuf_tensor("kbufh", [128, 2, D], BF16))
                kbufl = es6.enter_context(nc.sbuf_tensor("kbufl", [128, 2, D], BF16))
                qbufh = es6.enter_context(nc.sbuf_tensor("qbufh", [128, 2, 512], BF16))
                qbufl = es6.enter_context(nc.sbuf_tensor("qbufl", [128, 2, 512], BF16))
                stageS = es6.enter_context(nc.sbuf_tensor("stageS", [128, 2, D], F32))
                ps_S = [es6.enter_context(nc.psum_tensor(f"ps_S{i}", [128, D], F32))
                        for i in range(4)]
                QH = 512
                smm_tok = {}
                sev_tok = {}
                for half in range(2):
                    for tt in range(NT):
                        sl = tt % 2
                        idx = half * NT + tt

                        @blk.sync
                        def _(sy, half=half, tt=tt, sl=sl, idx=idx):
                            if idx == 0:
                                for nm, val in [("spk0", n_spk0), ("spk1", n_spk1),
                                                ("spq0", n_spq0), ("spq1", n_spq1)]:
                                    sy.wait_ge(sem(nm), val)
                                ctx.wait(sy, "vv", bar_kq_vv)
                            if idx >= 2:
                                ctx.wait(sy, "pe", smm_tok[idx - 2])
                            if tt == 0 and half == 1:
                                for qc in range(4):
                                    ctx.wait(sy, "vv", sev_tok[(0, qc)])
                            ctx.inc(sy.dma_start(out=kbufh[:, sl], in_=k_sph[tt]),
                                    f"ldk{sl}", 16)
                            ctx.inc(sy.dma_start(out=kbufl[:, sl], in_=k_spl[tt]),
                                    f"ldk{sl}", 16)
                            d = sy.dma_start(out=qbufh[:, sl],
                                             in_=q_sph[tt, :, half * QH:(half + 1) * QH])
                            ctx.inc(d, f"ldk{sl}", 16)
                            d = sy.dma_start(out=qbufl[:, sl],
                                             in_=q_spl[tt, :, half * QH:(half + 1) * QH])
                            ctx.inc(d, f"ldk{sl}", 16)
                        ldk_tok = ctx.n(f"ldk{sl}")

                        @blk.tensor
                        def _(te, half=half, tt=tt, sl=sl, idx=idx, ldk_tok=ldk_tok):
                            te.wait_ge(sem(f"ldk{sl}"), ldk_tok)
                            if idx == 0:
                                ctx.wait(te, "pe", bar_kq_pe)
                                ctx.wait(te, "vv", bar_kq_vv)
                            for qc in range(4):
                                for b in range(2):
                                    for qop, kop in ((qbufh, kbufh), (qbufh, kbufl),
                                                     (qbufl, kbufh)):
                                        i = te.matmul(
                                            ps_S[qc][:, b * 512:(b + 1) * 512],
                                            qop[:, sl, qc * 128:(qc + 1) * 128],
                                            kop[:, sl, b * 512:(b + 1) * 512],
                                            start=(tt == 0 and qop is qbufh and kop is kbufh),
                                            stop=(tt == NT - 1 and qop is qbufl))
                            smm_tok[idx] = ctx.inc(i, "pe")

                    for qc in range(4):
                        @VEC
                        def _(v, half=half, qc=qc):
                            ctx.wait(v, "pe", smm_tok[half * NT + NT - 1])
                            di = 4 * half + qc
                            if di >= 2:
                                v.wait_ge(sem(f"spS{di % 2}"), 16 * (di // 2))
                            i = v.tensor_copy(out=stageS[:, qc % 2], in_=ps_S[qc][:])
                            sev_tok[(half, qc)] = ctx.inc(i, "vv")

                        @SCA
                        def _(s, half=half, qc=qc):
                            di = 4 * half + qc
                            ctx.wait(s, "vv", sev_tok[(half, qc)])
                            d = s.dma_start(out=S_part[half * 4 + qc],
                                            in_=stageS[:, qc % 2])
                            ctx.inc(d, f"spS{di % 2}", 16)
                n_spS_all = [("spS0", ctx.n("spS0")), ("spS1", ctx.n("spS1"))]
        bar_s1_pe = ctx.n("pe")
        bar_s1_vv = ctx.n("vv")
        bar_s1_sc = ctx.n("sc")
        n_ldwv = ctx.n("ldwv"); n_ldwk = ctx.n("ldwk"); n_ldwq = ctx.n("ldwq")
        n_ldk0 = ctx.n("ldk0"); n_ldk1 = ctx.n("ldk1")
        n_ldx0 = ctx.n("ldx0"); n_ldx1 = ctx.n("ldx1")
        n_spS0 = ctx.n("spS0"); n_spS1 = ctx.n("spS1")

        @blk.gpsimd
        def _(g):
            for nm, val in n_spS_all:
                g.wait_ge(sem(nm), val)
            i = g.collective_compute("AllReduce", ALU.add, replica_groups=GROUPS,
                                     ins=[S_part.opt()], outs=[S_full.opt()])
            ctx.inc(i, "cc")

        es7 = ExitStack()
        with es7:
            sb7 = lambda name, shape, dt=F32: es7.enter_context(nc.sbuf_tensor(name, shape, dt))
            tfc_buf = sb7("tfc_buf", [128, 2, DC, 128], BF16)
            tpj_buf = sb7("tpj_buf", [128, 2, HC, 128], BF16)
            wpj_buf = sb7("wpj_buf", [128, 2, DC, 128])
            u_sb = sb7("u_sb", [RK, HID])
            out_acc = sb7("out_acc", [128, DC, GT])
            xn_g = sb7("xn_g", [128, DC, GT])
            xnh = sb7("xnh", [128, DC, GT], BF16)
            xnl = sb7("xnl", [128, DC, GT], BF16)
            x_g = sb7("x_g", [128, DC, GT])
            r2h = sb7("r2h", [128, HC, GT], BF16)
            r2l = sb7("r2l", [128, HC, GT], BF16)
            arena = sb7("arena", [128, max(2 * DC * GT, 4 * D)])
            a_g = arena[:, 0:DC * GT].rearrange("p (dc t) -> p dc t", t=GT)
            smS = arena[:, 0:2 * D].rearrange("p (s d) -> p s d", d=D)
            scrE = arena[:, 2 * D:3 * D]
            v_gh = sb7("v_gh", [128, DC, GT], BF16)
            v_gl = sb7("v_gl", [128, DC, GT], BF16)
            stageWh = sb7("stageWh", [128, DC, 128], BF16)
            stageWl = sb7("stageWl", [128, DC, 128], BF16)
            wTbufh = sb7("wTbufh", [128, 2, DC * 128], BF16)
            wTbufl = sb7("wTbufl", [128, 2, DC * 128], BF16)
            scr1 = sb7("scr1", [128, GT])
            scrR = sb7("scrR", [128, GT])
            scrR2 = sb7("scrR2", [128, GT])
            br_g = sb7("br_g", [128, GT])
            bmur_g = sb7("bmur_g", [128, GT])
            y1_sb = sb7("y1_sb", [RK, GT])
            y2_sb = sb7("y2_sb", [RK, GT])
            nmax = sb7("nmax", [128, 1])
            ssum = sb7("ssum", [128, 1])
            rsum = sb7("rsum", [128, 1])

            ps_fc = [es7.enter_context(nc.psum_tensor(f"ps_fc{i}", [128, 512], F32))
                     for i in range(2)]
            ps_sh = es7.enter_context(nc.psum_tensor("ps_sh", [128, 512], F32))
            ps_tr2 = [es7.enter_context(nc.psum_tensor(f"ps_tr2{i}", [128, 128], F32))
                      for i in range(2)]
            ps_av = [es7.enter_context(nc.psum_tensor(f"ps_av{i}", [128, 512], F32))
                     for i in range(2)]

            def wait_bar_s1(eng):
                ctx.wait(eng, "pe", bar_s1_pe)
                ctx.wait(eng, "vv", bar_s1_vv)
                ctx.wait(eng, "sc", bar_s1_sc)
                for nm, val in [("ldk0", n_ldk0), ("ldk1", n_ldk1),
                                ("spS0", n_spS0), ("spS1", n_spS1),
                                ("ldx0", n_ldx0), ("ldx1", n_ldx1),
                                ("spv0", n_spv0), ("spv1", n_spv1),
                                ("spk0", n_spk0), ("spk1", n_spk1),
                                ("spq0", n_spq0), ("spq1", n_spq1),
                                ("ldwv", n_ldwv), ("ldwk", n_ldwk),
                                ("ldwq", n_ldwq)]:
                    eng.wait_ge(sem(nm), val)

            @blk.sync
            def _(sy):
                wait_bar_s1(sy)
                for nm, val in stw_fin:
                    sy.wait_ge(sem(nm), val)
                ctx.inc(sy.dma_start(out=u_sb[:], in_=uT), "lds2", 16)
            n_lds2 = ctx.n("lds2")

            esU = ExitStack()
            with esU:
                rs16 = esU.enter_context(nc.sbuf_tensor("rs16", [RK, HID], F32))
                rs_row = esU.enter_context(nc.sbuf_tensor("rs_row", [1, HID], F32))
                rs_tr = esU.enter_context(nc.sbuf_tensor("rs_tr", [HC, 128], F32))
                rs_col = esU.enter_context(nc.sbuf_tensor("rs_col", [128, HC], F32))

                @VEC
                def _(v):
                    wait_bar_s1(v)
                    v.reciprocal(out=rs_col[:], in_=sfc_col[:])
                    ctx.inc(v.drain(), "vv")
                tok_rsc = ctx.n("vv")

                @blk.tensor
                def _(te):
                    ctx.wait(te, "vv", tok_rsc)
                    ctx.wait(te, "pe", bar_s1_pe)
                    ctx.inc(te.transpose(ps_tr2[0][0:HC, :], rs_col[:], identF[:]), "pe")
                tok_rst = ctx.n("pe")

                @VEC
                def _(v):
                    ctx.wait(v, "pe", tok_rst)
                    ctx.inc(v.tensor_copy(out=rs_tr[:], in_=ps_tr2[0][0:HC, :]), "vv")
                tok_rstr = ctx.n("vv")

                @SCA
                def _(s):
                    ctx.wait(s, "vv", tok_rstr)
                    d = s.dma_start(out=rs_row[:].rearrange("o (c p) -> o c p", p=128),
                                    in_=rs_tr[:])
                    ctx.inc(d, "ldrs", 16)

                ps_rs16 = esU.enter_context(nc.psum_tensor("ps_rs16", [RK, 512], F32))
                rsb_toks = []
                for b in range(HID // 512):
                    @blk.tensor
                    def _(te, b=b):
                        if b == 0:
                            te.wait_ge(sem("ldrs"), 16)
                        if rsb_toks:
                            ctx.wait(te, "vv", rsb_toks[-1])
                        ctx.inc(te.matmul(ps_rs16[:], oner[0:1, 0:RK],
                                          rs_row[:, b * 512:(b + 1) * 512],
                                          start=True, stop=True), "pe")
                    rsb_pe = ctx.n("pe")

                    @VEC
                    def _(v, b=b, rsb_pe=rsb_pe):
                        ctx.wait(v, "pe", rsb_pe)
                        i = v.tensor_copy(out=rs16[:, b * 512:(b + 1) * 512],
                                          in_=ps_rs16[:])
                        rsb_toks.append(ctx.inc(i, "vv"))

                @VEC
                def _(v):
                    ctx.wait(v, "vv", rsb_toks[-1])
                    v.wait_ge(sem("lds2"), n_lds2)
                    v.tensor_tensor(out=u_sb[:], in0=u_sb[:], in1=rs16[:], op=ALU.mult)
                    ctx.inc(v.drain(), "vv")
            tok_uprime = ctx.n("vv")

            state = {}

            def ffn_group(g_idx):
                gsl = slice(g_idx * GT, (g_idx + 1) * GT)

                @blk.sync
                def _(sy):
                    if g_idx == 0:
                        wait_bar_s1(sy)
                    else:
                        ctx.wait(sy, "vv", state[("xn", g_idx - 1)])
                    for dc in range(DC):
                        ctx.inc(sy.dma_start(out=x_g[:, dc],
                                             in_=xT[dc * 128:(dc + 1) * 128, gsl]),
                                "ldxg", 16)
                ld_tok = ctx.n("ldxg")

                bcg_toks = []
                for dst, srow in [(br_g, r_row), (bmur_g, mur_row)]:
                    @blk.tensor
                    def _(te, dst=dst, srow=srow):
                        if g_idx == 0 and not bcg_toks:
                            ctx.wait(te, "vv", tok_uprime)
                        if g_idx > 0 and not bcg_toks:
                            ctx.wait(te, "vv", state[("xn", g_idx - 1)])
                            ctx.wait(te, "vv", state[("y2ev", g_idx - 1)])
                        if bcg_toks:
                            ctx.wait(te, "vv", bcg_toks[-1])
                        ctx.inc(te.matmul(ps_sh[:, 0:GT], oner[:], srow[0:1, gsl],
                                          start=True, stop=True), "pe")
                    bcg_pe = ctx.n("pe")

                    @VEC
                    def _(v, dst=dst, bcg_pe=bcg_pe):
                        ctx.wait(v, "pe", bcg_pe)
                        bcg_toks.append(ctx.inc(v.tensor_copy(out=dst[:], in_=ps_sh[:, 0:GT]),
                                                "vv"))
                tok_bcg = ctx.n("vv")

                @VEC
                def _(v):
                    v.wait_ge(sem("ldxg"), ld_tok)
                    ctx.wait(v, "vv", tok_bcg)
                    if g_idx > 0:
                        v.wait_ge(sem("out_w"), 16 * g_idx)
                        ctx.wait(v, "pe", state[("fcmm", g_idx - 1)])
                    for dc in range(DC):
                        v.tensor_tensor(out=scr1[:], in0=x_g[:, dc], in1=br_g[:],
                                        op=ALU.mult)
                        v.drain()
                        v.tensor_tensor(out=xn_g[:, dc], in0=scr1[:], in1=bmur_g[:],
                                        op=ALU.subtract)
                        v.drain()
                        v.tensor_tensor(out=out_acc[:, dc], in0=x_g[:, dc],
                                        in1=xn_g[:, dc], op=ALU.add)
                        v.tensor_copy(out=xnh[:, dc], in_=xn_g[:, dc])
                        v.drain()
                        v.tensor_tensor(out=xnl[:, dc], in0=xn_g[:, dc],
                                        in1=xnh[:, dc], op=ALU.subtract)
                        v.drain()
                    state[("xn", g_idx)] = ctx.inc(v.drain(), "vv")
                tok_xn = state[("xn", g_idx)]

                @blk.tensor
                def _(te):
                    ctx.wait(te, "vv", tok_xn)
                    for dc in range(DC):
                        i = te.matmul(ps_sh[0:RK, 0:GT], vlr_sb[:, dc], xn_g[:, dc],
                                      start=(dc == 0), stop=(dc == DC - 1))
                    state[("y1", g_idx)] = ctx.inc(i, "pe")

                @VEC
                def _(v):
                    ctx.wait(v, "pe", state[("y1", g_idx)])
                    v.tensor_copy(out=y1_sb[:], in_=ps_sh[0:RK, 0:GT])
                    ctx.inc(v.drain(), "vv")
                tok_y1e = ctx.n("vv")

                @blk.tensor
                def _(te):
                    ctx.wait(te, "vv", tok_y1e)
                    ctx.inc(te.matmul(ps_sh[0:RK, 0:GT], fb_sb[:], y1_sb[:], start=True,
                                      stop=True), "pe")
                tok_y2 = ctx.n("pe")

                @VEC
                def _(v):
                    ctx.wait(v, "pe", tok_y2)
                    v.tensor_copy(out=y2_sb[:], in_=ps_sh[0:RK, 0:GT])
                    state[("y2ev", g_idx)] = ctx.inc(v.drain(), "vv")
                tok_y2e = ctx.n("vv")

                r2_tok = {}
                relu_tok = {}
                fcmm_tok = {}
                for ic in range(HC):
                    sl = ic % 2

                    @blk.sync
                    def _(sy, ic=ic, sl=sl):
                        if ic == 0 and g_idx == 0:
                            for nm, val in stw_fin[:2]:
                                sy.wait_ge(sem(nm), val)
                        if ic >= 2:
                            ctx.wait(sy, "pe", fcmm_tok[ic - 2])
                        elif g_idx > 0:
                            ctx.wait(sy, "pe", state[("fcmm", g_idx - 1)])
                        ctx.inc(sy.dma_start(out=tfc_buf[:, sl], in_=ternfc_dram[:, ic]),
                                f"ldtf{sl}", 16)
                    ldtf_tok = ctx.n(f"ldtf{sl}")

                    @blk.tensor
                    def _(te, ic=ic, sl=sl, ldtf_tok=ldtf_tok):
                        te.wait_ge(sem(f"ldtf{sl}"), ldtf_tok)
                        if ic == 0:
                            ctx.wait(te, "vv", tok_y2e)
                            if g_idx == 0:
                                te.wait_ge(sem("lds2"), n_lds2)
                            else:
                                ctx.wait(te, "vv", state[("pjev", g_idx - 1)])
                        if ic >= 2:
                            ctx.wait(te, "sc", relu_tok[ic - 2])
                        for xp in (xnh, xnl):
                            for dc in range(DC):
                                te.matmul(ps_fc[sl][:, 0:GT], tfc_buf[:, sl, dc],
                                          xp[:, dc], start=(xp is xnh and dc == 0),
                                          stop=False)
                        i = te.matmul(ps_fc[sl][:, 0:GT],
                                      u_sb[:, ic * 128:(ic + 1) * 128], y2_sb[:],
                                      start=False, stop=True)
                        fcmm_tok[ic] = ctx.inc(i, "pe")
                    mm_tok = ctx.n("pe")

                    @SCA
                    def _(s, ic=ic, sl=sl, mm_tok=mm_tok):
                        ctx.wait(s, "pe", mm_tok)
                        if ic >= 1:
                            ctx.wait(s, "vv", r2_tok[ic - 1])
                        s.activation(out=scrR[:], in_=ps_fc[sl][:, 0:GT], func=AF.Relu,
                                     scale=sfc_col[:, ic:ic + 1])
                        s.drain()
                        i = s.activation(out=scrR2[:], in_=scrR[:], func=AF.Square)
                        relu_tok[ic] = ctx.inc(i, "sc")

                    @VEC
                    def _(v, ic=ic):
                        ctx.wait(v, "sc", relu_tok[ic])
                        v.tensor_copy(out=r2h[:, ic], in_=scrR2[:])
                        v.drain()
                        i = v.tensor_tensor(out=r2l[:, ic], in0=scrR2[:], in1=r2h[:, ic],
                                            op=ALU.subtract)
                        r2_tok[ic] = ctx.inc(i, "vv")
                state[("fcmm", g_idx)] = ctx.n("pe")
                tok_r2 = ctx.n("vv")

                pjev_tok = {}
                pjmm_tok = {}
                for fc8 in range(DC):
                    sl = fc8 % 2

                    @blk.sync
                    def _(sy, fc8=fc8, sl=sl):
                        if fc8 == 0 and g_idx == 0:
                            for nm, val in stw_fin[2:]:
                                sy.wait_ge(sem(nm), val)
                        if fc8 >= 2:
                            ctx.wait(sy, "pe", pjmm_tok[fc8 - 2])
                        elif g_idx > 0:
                            ctx.wait(sy, "pe", state[("pjmm", g_idx - 1)])
                        ctx.inc(sy.dma_start(out=tpj_buf[:, sl], in_=ternpj_dram[:, fc8]),
                                f"ldtp{sl}", 16)
                    ldtp_tok = ctx.n(f"ldtp{sl}")

                    @blk.tensor
                    def _(te, fc8=fc8, sl=sl, ldtp_tok=ldtp_tok):
                        te.wait_ge(sem(f"ldtp{sl}"), ldtp_tok)
                        if fc8 == 0:
                            ctx.wait(te, "vv", tok_r2)
                        if fc8 >= 2:
                            ctx.wait(te, "vv", pjev_tok[fc8 - 2])
                        for rp in (r2h, r2l):
                            for ic in range(HC):
                                i = te.matmul(ps_fc[sl][:, 0:GT], tpj_buf[:, sl, ic],
                                              rp[:, ic], start=(rp is r2h and ic == 0),
                                              stop=(rp is r2l and ic == HC - 1))
                        pjmm_tok[fc8] = ctx.inc(i, "pe")
                    mm_tok = ctx.n("pe")

                    @VEC
                    def _(v, fc8=fc8, sl=sl, mm_tok=mm_tok):
                        ctx.wait(v, "pe", mm_tok)
                        i = v.scalar_tensor_tensor(out=out_acc[:, fc8],
                                                   in0=ps_fc[sl][:, 0:GT],
                                                   scalar=spj_col[:, fc8:fc8 + 1],
                                                   in1=out_acc[:, fc8],
                                                   op0=ALU.mult, op1=ALU.add)
                        pjev_tok[fc8] = ctx.inc(i, "vv")
                state[("pjev", g_idx)] = ctx.n("vv")
                state[("pjmm", g_idx)] = ctx.n("pe")

            def softmax_phase():
                exp_tok = {}
                w_tok = {}
                trev_tok = {}
                for qc in range(DC):
                    sl = qc % 2

                    @blk.sync
                    def _(sy, qc=qc, sl=sl):
                        if qc == 0:
                            ctx.wait(sy, "cc", 1)
                        if qc >= 2:
                            ctx.wait(sy, "sc", exp_tok[qc - 2])
                        ctx.inc(sy.dma_start(out=smS[:, sl], in_=S_full[qc]),
                                f"ldS{sl}", 16)
                    ld_tok = ctx.n(f"ldS{sl}")

                    @VEC
                    def _(v, qc=qc, sl=sl, ld_tok=ld_tok):
                        v.wait_ge(sem(f"ldS{sl}"), ld_tok)
                        if qc >= 1:
                            ctx.wait(v, "sc", exp_tok[qc - 1])
                        i = v.tensor_reduce(out=nmax[:], in_=smS[:, sl], axis=AX.X,
                                            op=ALU.max, negate=True)
                        ctx.inc(i, "vv")
                    red_tok = ctx.n("vv")

                    @SCA
                    def _(s, qc=qc, sl=sl, red_tok=red_tok):
                        ctx.wait(s, "vv", red_tok)
                        if qc >= 1:
                            ctx.wait(s, "vv", w_tok[qc - 1])
                        i = s.activation(out=scrE[:], in_=smS[:, sl], func=AF.Exp,
                                         bias=nmax[:, 0:1], accum_out=ssum[:, 0:1])
                        exp_tok[qc] = ctx.inc(i, "sc")

                    @VEC
                    def _(v, qc=qc):
                        ctx.wait(v, "sc", exp_tok[qc])
                        v.reciprocal(out=rsum[:], in_=ssum[:])
                        v.drain()
                        i = v.tensor_scalar(out=scrE[:], in0=scrE[:],
                                            scalar1=rsum[:, 0:1], scalar2=None,
                                            op0=ALU.mult)
                        w_tok[qc] = ctx.inc(i, "vv")

                    for kc in range(DC):
                        idx = qc * DC + kc

                        @blk.tensor
                        def _(te, qc=qc, kc=kc, idx=idx):
                            if kc == 0:
                                ctx.wait(te, "vv", w_tok[qc])
                            if idx >= 2:
                                ctx.wait(te, "vv", trev_tok[idx - 2])
                            i = te.transpose(ps_tr2[idx % 2][:],
                                             scrE[:, kc * 128:(kc + 1) * 128], identF[:])
                            ctx.inc(i, "pe")
                        tr_tok = ctx.n("pe")

                        @VEC
                        def _(v, qc=qc, kc=kc, idx=idx, tr_tok=tr_tok):
                            ctx.wait(v, "pe", tr_tok)
                            if qc >= 1 and kc == 0:
                                v.wait_ge(sem("spW"), 32 * qc)
                            v.tensor_copy(out=stageWh[:, kc], in_=ps_tr2[idx % 2][:])
                            v.drain()
                            i = v.tensor_tensor(out=stageWl[:, kc], in0=ps_tr2[idx % 2][:],
                                                in1=stageWh[:, kc], op=ALU.subtract)
                            trev_tok[idx] = ctx.inc(i, "vv")

                    @SCA
                    def _(s, qc=qc):
                        ctx.wait(s, "vv", trev_tok[qc * DC + DC - 1])
                        ctx.inc(s.dma_start(out=wT_dramh[qc].rearrange("kc p i -> p kc i"),
                                            in_=stageWh[:]), "spW", 16)
                        ctx.inc(s.dma_start(out=wT_draml[qc].rearrange("kc p i -> p kc i"),
                                            in_=stageWl[:]), "spW", 16)
                return ctx.n("spW")

            def attn_group(g_idx, n_spW):
                gsl = slice(g_idx * GT, (g_idx + 1) * GT)

                @blk.sync
                def _(sy):
                    if g_idx == 0:
                        sy.wait_ge(sem("spv0"), n_spv0)
                        sy.wait_ge(sem("spv1"), n_spv1)
                        sy.wait_ge(sem("spW"), n_spW)
                    else:
                        ctx.wait(sy, "pe", state[("avmm", g_idx - 1)])
                    for kc in range(DC):
                        ctx.inc(sy.dma_start(out=v_gh[:, kc], in_=v_sph[kc, :, gsl]),
                                "ldvg", 16)
                        ctx.inc(sy.dma_start(out=v_gl[:, kc], in_=v_spl[kc, :, gsl]),
                                "ldvg", 16)
                ldv_tok = ctx.n("ldvg")

                avev_tok = {}
                avmm_tok = {}
                for qc in range(DC):
                    sl = qc % 2

                    @blk.sync
                    def _(sy, qc=qc, sl=sl):
                        if g_idx == 0 and qc == 0:
                            sy.wait_ge(sem("spW"), n_spW)
                        if qc >= 2:
                            ctx.wait(sy, "pe", avmm_tok[qc - 2])
                        elif g_idx > 0:
                            ctx.wait(sy, "pe", state[("avmm", g_idx - 1)])
                        ctx.inc(sy.dma_start(out=wTbufh[:, sl],
                                             in_=wT_dramh[qc].rearrange("kc p i -> p kc i")),
                                f"ldwT{sl}", 16)
                        ctx.inc(sy.dma_start(out=wTbufl[:, sl],
                                             in_=wT_draml[qc].rearrange("kc p i -> p kc i")),
                                f"ldwT{sl}", 16)
                    ldt_tok = ctx.n(f"ldwT{sl}")

                    @blk.tensor
                    def _(te, qc=qc, sl=sl, ldt_tok=ldt_tok):
                        te.wait_ge(sem(f"ldwT{sl}"), ldt_tok)
                        if qc == 0:
                            te.wait_ge(sem("ldvg"), ldv_tok)
                            ctx.wait(te, "vv", state[("pjev", g_idx)])
                        if qc >= 2:
                            ctx.wait(te, "vv", avev_tok[qc - 2])
                        for wop, vop in ((wTbufh, v_gh), (wTbufh, v_gl), (wTbufl, v_gh)):
                            for kc in range(DC):
                                i = te.matmul(ps_av[sl][:, 0:GT],
                                              wop[:, sl, kc * 128:(kc + 1) * 128],
                                              vop[:, kc],
                                              start=(wop is wTbufh and vop is v_gh and kc == 0),
                                              stop=(wop is wTbufl and kc == DC - 1))
                        avmm_tok[qc] = ctx.inc(i, "pe")

                    @VEC
                    def _(v, qc=qc, sl=sl):
                        ctx.wait(v, "pe", avmm_tok[qc])
                        if g_idx > 0 and qc == 0:
                            ctx.wait(v, "pe", state[("o1mm", g_idx - 1)])
                        i = v.tensor_copy(out=a_g[:, qc], in_=ps_av[sl][:, 0:GT])
                        avev_tok[qc] = ctx.inc(i, "vv")
                state[("avmm", g_idx)] = ctx.n("pe")
                tok_a = ctx.n("vv")

                o1ev_tok = {}
                o1mm_tok = {}
                for oc in range(DC):
                    sl = oc % 2

                    @blk.sync
                    def _(sy, oc=oc, sl=sl):
                        if oc >= 2:
                            ctx.wait(sy, "pe", o1mm_tok[oc - 2])
                        elif g_idx > 0:
                            ctx.wait(sy, "pe", state[("o1mm", g_idx - 1)])
                        ctx.inc(sy.dma_start(out=wpj_buf[:, sl],
                                             in_=wprojT[oc].rearrange("(ac p) j -> p ac j", p=128)),
                                f"ldpj{sl}", 16)
                    ldpj_tok = ctx.n(f"ldpj{sl}")

                    @blk.tensor
                    def _(te, oc=oc, sl=sl, ldpj_tok=ldpj_tok):
                        te.wait_ge(sem(f"ldpj{sl}"), ldpj_tok)
                        if oc == 0:
                            ctx.wait(te, "vv", tok_a)
                        if oc >= 2:
                            ctx.wait(te, "vv", o1ev_tok[oc - 2])
                        for ac in range(DC):
                            i = te.matmul(ps_av[sl][:, 0:GT],
                                          wpj_buf[:, sl, ac],
                                          a_g[:, ac], start=(ac == 0), stop=(ac == DC - 1))
                        o1mm_tok[oc] = ctx.inc(i, "pe")
                    mm_tok = ctx.n("pe")

                    @VEC
                    def _(v, oc=oc, sl=sl, mm_tok=mm_tok):
                        ctx.wait(v, "pe", mm_tok)
                        i = v.scalar_tensor_tensor(out=out_acc[:, oc],
                                                   in0=ps_av[sl][:, 0:GT],
                                                   scalar=pb_sb[:, oc:oc + 1],
                                                   in1=out_acc[:, oc],
                                                   op0=ALU.add, op1=ALU.add)
                        o1ev_tok[oc] = ctx.inc(i, "vv")
                state[("o1mm", g_idx)] = ctx.n("pe")
                tok_out = ctx.n("vv")

                @blk.gpsimd
                def _(gp):
                    ctx.wait(gp, "vv", tok_out)
                    d = gp.dma_start(out=out_dram[g_idx].rearrange("dc p t -> p dc t"),
                                     in_=out_acc[:])
                    ctx.inc(d, "out_w", 16)

            ffn_group(0)
            n_spW = softmax_phase()
            attn_group(0, n_spW)
            for g_idx in range(1, NG):
                ffn_group(g_idx)
                attn_group(g_idx, n_spW)

            @blk.gpsimd
            def _(gp):
                gp.wait_ge(sem("out_w"), 16 * NG)

    return nc


def shard_inputs(T, x, norm_w, norm_b, qkv_w, qkv_b, proj_w, proj_b,
                 fast_b, w_fc, w_proj, u, v, gate):
    x = np.asarray(x, np.float32)
    f32c = lambda a: np.ascontiguousarray(np.asarray(a, np.float32))
    qkv_w = np.asarray(qkv_w, np.float32)
    qkv_b = np.asarray(qkv_b, np.float32)
    shared = dict(
        wqT=f32c(qkv_w[0:D].T), wkT=f32c(qkv_w[D:2 * D].T), wvT=f32c(qkv_w[2 * D:3 * D].T),
        wprojT=np.ascontiguousarray(
            np.asarray(proj_w, np.float32).T.reshape(D, DC, 128).transpose(1, 0, 2)),
        wfc=f32c(w_fc), wpj=f32c(w_proj),
        vlrT=f32c(np.asarray(v).T), fbT=f32c(np.asarray(fast_b).T), uT=f32c(np.asarray(u).T),
        bq_row=f32c(qkv_b[0:D][None, :]), bk_row=f32c(qkv_b[D:2 * D][None, :]),
        bv_col=f32c(qkv_b[2 * D:3 * D].reshape(DC, 128).T),
        pb_col=f32c(np.asarray(proj_b).reshape(DC, 128).T),
        gam_col=f32c(np.asarray(norm_w).reshape(DC, 128).T),
        bet_col=f32c(np.asarray(norm_b).reshape(DC, 128).T),
        gate_in=f32c(np.asarray(gate).reshape(1, 1)),
    )
    in_maps = []
    for c in range(NCORES):
        g, j = divmod(c, 4)
        m = dict(shared)
        m["xT"] = f32c(x[g, j * T:(j + 1) * T, :].T)
        in_maps.append(m)
    return in_maps


def assemble_output(T, results):
    GT = min(256, T)
    NG = T // GT
    out = np.empty((2, 4 * T, D), np.float32)
    for c in range(NCORES):
        g, j = divmod(c, 4)
        o = np.asarray(results[c]["outT"])
        for gg in range(NG):
            blkv = o[gg].reshape(D, GT)
            out[g, j * T + gg * GT:j * T + (gg + 1) * GT, :] = blkv.T
    return out




_CACHED = {}


def kernel(L0_attn_x, L0_attn_norm_w, L0_attn_norm_b, L0_attn_qkv_w, L0_attn_qkv_b,
           L0_attn_proj_w, L0_attn_proj_b, L0_ffn_fast_b, L0_ffn_w_fc, L0_ffn_w_proj,
           L0_ffn_u, L0_ffn_v, L0_ffn_gate):
    from concourse.bass_utils import run_bass_kernel_spmd
    T = 2048
    if "nc" not in _CACHED:
        _CACHED["nc"] = build_nc(T)
    nc = _CACHED["nc"]
    in_maps = shard_inputs(
        T, L0_attn_x, L0_attn_norm_w, L0_attn_norm_b, L0_attn_qkv_w, L0_attn_qkv_b,
        L0_attn_proj_w, L0_attn_proj_b, L0_ffn_fast_b, L0_ffn_w_fc, L0_ffn_w_proj,
        L0_ffn_u, L0_ffn_v, L0_ffn_gate)
    res = run_bass_kernel_spmd(nc, in_maps, core_ids=list(range(NCORES)))
    return assemble_output(T, [res.results[c] for c in range(NCORES)])


# version 13
# speedup vs baseline: 1.2586x; 1.0439x over previous
import numpy as np
from contextlib import ExitStack

import concourse.bass as bass
import concourse.mybir as mybir

F32 = mybir.dt.float32
BF16 = mybir.dt.bfloat16
ALU = mybir.AluOpType
AF = mybir.ActivationFunctionType
AX = mybir.AxisListType

D = 1024
DC = D // 128
HID = 4096
HC = HID // 128
RK = 16
EPS = 1e-5
NCORES = 8
GROUPS = [[0, 1, 2, 3], [4, 5, 6, 7]]


class Ctx:
    def __init__(self, nc, es):
        self.nc = nc
        self.es = es
        self.sems = {}
        self.cnt = {}

    def sem(self, name):
        if name not in self.sems:
            self.sems[name] = self.es.enter_context(self.nc.semaphore(name))
            self.cnt[name] = 0
        return self.sems[name]

    def inc(self, instr, name, k=1):
        instr.then_inc(self.sem(name), k)
        self.cnt[name] += k
        return self.cnt[name]

    def n(self, name):
        self.sem(name)
        return self.cnt[name]

    def wait(self, eng, name, value=None):
        v = self.cnt[name] if value is None else value
        if v > 0:
            eng.wait_ge(self.sem(name), v)


def build_nc(T):
    NT = T // 128
    GT = min(256, T)
    NG = T // GT
    L_full = 4 * T
    ATT_SCALE = float(np.float32(1.0 / np.sqrt(L_full)))
    NB = max(1, T // 512)
    WB = min(512, T)
    TB = max(1, T // 1024)
    VB = min(1024, T)

    nc = bass.Bass("TRN2", target_bir_lowering=False)

    xT = nc.dram_tensor("xT", [D, T], F32, kind="ExternalInput").ap()
    wqT = nc.dram_tensor("wqT", [D, D], F32, kind="ExternalInput").ap()
    wkT = nc.dram_tensor("wkT", [D, D], F32, kind="ExternalInput").ap()
    wvT = nc.dram_tensor("wvT", [D, D], F32, kind="ExternalInput").ap()
    wprojT = nc.dram_tensor("wprojT", [DC, D, 128], F32, kind="ExternalInput").ap()
    wfc = nc.dram_tensor("wfc", [HID, D], F32, kind="ExternalInput").ap()
    wpj = nc.dram_tensor("wpj", [D, HID], F32, kind="ExternalInput").ap()
    vlrT = nc.dram_tensor("vlrT", [D, RK], F32, kind="ExternalInput").ap()
    fbT = nc.dram_tensor("fbT", [RK, RK], F32, kind="ExternalInput").ap()
    uT = nc.dram_tensor("uT", [RK, HID], F32, kind="ExternalInput").ap()
    bq_row = nc.dram_tensor("bq_row", [1, D], F32, kind="ExternalInput").ap()
    bk_row = nc.dram_tensor("bk_row", [1, D], F32, kind="ExternalInput").ap()
    bv_col = nc.dram_tensor("bv_col", [128, DC], F32, kind="ExternalInput").ap()
    pb_col = nc.dram_tensor("pb_col", [128, DC], F32, kind="ExternalInput").ap()
    gam_col = nc.dram_tensor("gam_col", [128, DC], F32, kind="ExternalInput").ap()
    bet_col = nc.dram_tensor("bet_col", [128, DC], F32, kind="ExternalInput").ap()
    gate_in = nc.dram_tensor("gate_in", [1, 1], F32, kind="ExternalInput").ap()
    out_dram = nc.dram_tensor("outT", [NG, DC, 128, GT], F32, kind="ExternalOutput").ap()

    ternfc_dram = nc.dram_tensor("ternfc_d", [128, HC, DC, 128], BF16).ap()
    ternpj_dram = nc.dram_tensor("ternpj_d", [128, DC, HC, 128], BF16).ap()
    k_sph = nc.dram_tensor("k_sph", [NT, 128, D], BF16).ap()
    k_spl = nc.dram_tensor("k_spl", [NT, 128, D], BF16).ap()
    q_sph = nc.dram_tensor("q_sph", [NT, 128, D], BF16).ap()
    q_spl = nc.dram_tensor("q_spl", [NT, 128, D], BF16).ap()
    v_sph = nc.dram_tensor("v_sph", [DC, 128, T], BF16).ap()
    v_spl = nc.dram_tensor("v_spl", [DC, 128, T], BF16).ap()
    S_part = nc.dram_tensor("S_part", [DC, 128, D], F32).ap()
    S_full = nc.dram_tensor("S_full", [DC, 128, D], F32).ap()
    wT_dramh = nc.dram_tensor("wT_dramh", [DC, DC, 128, 128], BF16).ap()
    wT_draml = nc.dram_tensor("wT_draml", [DC, DC, 128, 128], BF16).ap()

    es = ExitStack()
    with es:
        ctx = Ctx(nc, es)
        sem = ctx.sem
        sbt = lambda name, shape, dt=F32: es.enter_context(nc.sbuf_tensor(name, shape, dt))
        gam_sb = sbt("gam_sb", [128, DC]); bet_sb = sbt("bet_sb", [128, DC])
        bv_sb = sbt("bv_sb", [128, DC]); pb_sb = sbt("pb_sb", [128, DC])
        vlr_sb = sbt("vlr_sb", [128, DC, RK])
        fb_sb = sbt("fb_sb", [RK, RK])
        gate_sb = sbt("gate_sb", [1, 1]); gate16 = sbt("gate16", [RK, 1])
        onec = sbt("onec", [128, 1]); oner = sbt("oner", [1, 128])
        eps_c = sbt("eps_c", [128, 1])
        identF = sbt("identF", [128, 128]); identB = sbt("identB", [128, 128], BF16)
        sfc_col = sbt("sfc_col", [128, HC]); spj_col = sbt("spj_col", [128, DC])
        r_row = sbt("r_row", [1, T]); mur_row = sbt("mur_row", [1, T])

        blk = es.enter_context(nc.Block())

        def VEC(fn):
            def w(v):
                v.drain()
                fn(v)
            blk.vector(w)

        def SCA(fn):
            def w(s):
                s.drain()
                fn(s)
            blk.scalar(w)

        @blk.sync
        def _(sy):
            for dst, src in [(gam_sb, gam_col), (bet_sb, bet_col),
                             (bv_sb, bv_col), (pb_sb, pb_col),
                             (fb_sb, fbT), (gate_sb, gate_in)]:
                ctx.inc(sy.dma_start(out=dst[:], in_=src), "sml", 16)
            ctx.inc(sy.dma_start(out=vlr_sb[:],
                                 in_=vlrT.rearrange("(dc p) r -> p dc r", p=128)),
                    "sml", 16)

        @blk.gpsimd
        def _(g):
            g.iota(identF[:], pattern=[[-1, 128]], base=0, channel_multiplier=1,
                   allow_small_or_imprecise_dtypes=True)
            ctx.inc(g.engine_nop(), "gp")

        @VEC
        def _(v):
            ctx.wait(v, "gp")
            v.memset(onec[:], 1.0)
            v.memset(oner[:], 1.0)
            v.memset(eps_c[:], EPS)
            v.tensor_scalar(out=identF[:], in0=identF[:], scalar1=0.0, scalar2=None,
                            op0=ALU.is_equal)
            v.drain()
            v.tensor_copy(out=identB[:], in_=identF[:])
            ctx.inc(v.drain(), "vv")
        tok_consts = ctx.n("vv")

        es0 = ExitStack()
        with es0:
            ps_g = es0.enter_context(nc.psum_tensor("ps_g", [128, 1], F32))

            @blk.tensor
            def _(te):
                ctx.wait(te, "vv", tok_consts)
                ctx.wait(te, "sml")
                ctx.inc(te.matmul(ps_g[:], oner[:], gate_sb[:], start=True, stop=True),
                        "pe")
            tok_g = ctx.n("pe")

            @VEC
            def _(v):
                ctx.wait(v, "pe", tok_g)
                v.tensor_copy(out=gate16[:], in_=ps_g[0:RK, :])
                v.drain()
                v.tensor_scalar(out=vlr_sb[:], in0=vlr_sb[:],
                                scalar1=ps_g[:, 0:1], scalar2=None, op0=ALU.mult)
                ctx.inc(v.drain(), "vv")
        tok_consts = ctx.n("vv")

        def prep_bitlinear(src_dram, n_chunks, width, s_col, tern_dram, tag,
                           wait_prev=None):
            nw = width // 128
            esb = ExitStack()
            with esb:
                wbuf = esb.enter_context(nc.sbuf_tensor(f"wbuf_{tag}", [128, 2, width], F32))
                absum = esb.enter_context(nc.sbuf_tensor(f"absum_{tag}", [128, 2], F32))
                scr = esb.enter_context(nc.sbuf_tensor(f"scr_{tag}", [128, width], F32))
                tern = esb.enter_context(nc.sbuf_tensor(f"tern_{tag}", [128, 2, width], BF16))
                hp = esb.enter_context(nc.sbuf_tensor(f"hp_{tag}", [128, 2], F32))
                stage = esb.enter_context(nc.sbuf_tensor(f"stage_{tag}", [128, 2, nw, 128], BF16))
                ps_tr = [esb.enter_context(nc.psum_tensor(f"pstr{i}_{tag}", [128, 128], BF16))
                         for i in range(4)]
                TRN = f"trn_{tag}"
                TEV = f"tev_{tag}"
                TRD = f"trd_{tag}"
                STW = f"stw_{tag}"

                def emit_evac(ic):
                    @SCA
                    def _(s, ic=ic):
                        for j in range(nw):
                            idx = ic * nw + j
                            if j == 0 and ic >= 2:
                                s.wait_ge(sem(f"{STW}{ic % 2}"), 16 * (ic // 2))
                            ctx.wait(s, TRN, idx + 1)
                            i = s.activation(out=stage[:, ic % 2, j],
                                             in_=ps_tr[idx % 4][:], func=AF.Identity)
                            ctx.inc(i, TEV)
                        s.drain()
                        d = s.dma_start(out=tern_dram[:, ic], in_=stage[:, ic % 2])
                        ctx.inc(d, f"{STW}{ic % 2}", 16)

                for ic in range(n_chunks):
                    sl = ic % 2

                    @blk.sync
                    def _(sy, ic=ic, sl=sl):
                        if ic == 0 and wait_prev is not None:
                            for nm, val in wait_prev:
                                sy.wait_ge(sem(nm), val)
                        if ic >= 2:
                            ctx.wait(sy, TRN, (ic - 1) * nw)
                        d = sy.dma_start(out=wbuf[:, sl],
                                         in_=src_dram[ic * 128:(ic + 1) * 128, :])
                        ctx.inc(d, f"ldw{sl}_{tag}", 16)

                    @VEC
                    def _(v, ic=ic, sl=sl):
                        v.wait_ge(sem(f"ldw{sl}_{tag}"), 16 * (ic // 2 + 1))
                        if ic == 0 and tag == "fc":
                            ctx.wait(v, "vv", tok_consts)
                        if ic >= 2:
                            ctx.wait(v, TRN, (ic - 1) * nw)
                        v.tensor_reduce(out=absum[:, sl:sl + 1], in_=wbuf[:, sl],
                                        axis=AX.X, op=ALU.add,
                                        apply_absolute_value=True)
                        v.drain()
                        v.tensor_scalar(out=s_col[:, ic:ic + 1], in0=absum[:, sl:sl + 1],
                                        scalar1=1.0 / width, scalar2=1e-5,
                                        op0=ALU.mult, op1=ALU.max)
                        v.drain()
                        v.tensor_scalar(out=hp[:, sl:sl + 1], in0=s_col[:, ic:ic + 1],
                                        scalar1=0.5, scalar2=None, op0=ALU.mult)
                        v.drain()
                        v.tensor_scalar(out=scr[:], in0=wbuf[:, sl],
                                        scalar1=hp[:, sl:sl + 1], scalar2=None,
                                        op0=ALU.is_gt)
                        v.tensor_scalar(out=tern[:, sl], in0=wbuf[:, sl], scalar1=-1.0,
                                        scalar2=hp[:, sl:sl + 1], op0=ALU.mult,
                                        op1=ALU.is_gt)
                        v.drain()
                        i = v.tensor_tensor(out=tern[:, sl], in0=scr[:], in1=tern[:, sl],
                                            op=ALU.subtract)
                        ctx.inc(i, TRD)

                    @blk.tensor
                    def _(te, ic=ic, sl=sl):
                        ctx.wait(te, TRD, ic + 1)
                        for j in range(nw):
                            idx = ic * nw + j
                            if idx >= 4:
                                ctx.wait(te, TEV, idx - 3)
                            i = te.transpose(ps_tr[idx % 4][:],
                                             tern[:, sl, j * 128:(j + 1) * 128], identB[:])
                            ctx.inc(i, TRN)

                    if ic >= 1:
                        emit_evac(ic - 1)
                emit_evac(n_chunks - 1)

        prep_bitlinear(wfc, HC, D, sfc_col, ternfc_dram, "fc")
        prep_bitlinear(wpj, DC, HID, spj_col, ternpj_dram, "pj",
                       wait_prev=[("stw_fc0", ctx.n("stw_fc0")),
                                  ("stw_fc1", ctx.n("stw_fc1"))])
        stw_fin = [(nm, ctx.n(nm)) for nm in
                   ("stw_fc0", "stw_fc1", "stw_pj0", "stw_pj1")]

        es3 = ExitStack()
        with es3:
            hh_sb = es3.enter_context(nc.sbuf_tensor("hh_sb", [128, DC, T], BF16))
            hl_sb = es3.enter_context(nc.sbuf_tensor("hl_sb", [128, DC, T], BF16))

            es4 = ExitStack()
            with es4:
                xbuf = es4.enter_context(nc.sbuf_tensor("xbuf", [128, 2, T], F32))
                scrA = es4.enter_context(nc.sbuf_tensor("scrA", [128, T], F32))
                rowA = es4.enter_context(nc.sbuf_tensor("rowA", [1, T], F32))
                rowB = es4.enter_context(nc.sbuf_tensor("rowB", [1, T], F32))
                rowC = es4.enter_context(nc.sbuf_tensor("rowC", [1, T], F32))
                bc1 = es4.enter_context(nc.sbuf_tensor("bc1", [128, T], F32))
                bc2 = es4.enter_context(nc.sbuf_tensor("bc2", [128, T], F32))

                esPS = ExitStack()
                esPS.__enter__()
                ps_sx = esPS.enter_context(nc.psum_tensor("ps_sx", [1, T], F32))
                ps_sx2 = esPS.enter_context(nc.psum_tensor("ps_sx2", [1, T], F32))

                pe_sq_tok = {}
                for dc in range(DC):
                    sl = dc % 2

                    @blk.sync
                    def _(sy, dc=dc, sl=sl):
                        if dc == 0:
                            for nm, val in stw_fin:
                                sy.wait_ge(sem(nm), val)
                        if dc >= 2:
                            ctx.wait(sy, "pe", pe_sq_tok[dc - 2])
                        d = sy.dma_start(out=xbuf[:, sl],
                                         in_=xT[dc * 128:(dc + 1) * 128, :])
                        ctx.inc(d, f"ldx{sl}", 16)
                    ldx_tok = ctx.n(f"ldx{sl}")

                    @SCA
                    def _(s, dc=dc, sl=sl, ldx_tok=ldx_tok):
                        s.wait_ge(sem(f"ldx{sl}"), ldx_tok)
                        if dc == 0:
                            for nm, val in stw_fin:
                                s.wait_ge(sem(nm), val)
                        else:
                            ctx.wait(s, "pe", pe_sq_tok[dc - 1])
                        i = s.activation(out=scrA[:], in_=xbuf[:, sl], func=AF.Square)
                        ctx.inc(i, "sc")
                    tok_sq = ctx.n("sc")

                    @blk.tensor
                    def _(te, dc=dc, sl=sl, tok_sq=tok_sq):
                        if dc == 0:
                            ctx.wait(te, "vv", tok_consts)
                        ctx.wait(te, "sc", tok_sq)
                        for b in range(NB):
                            te.matmul(ps_sx[0:1, b * 512:b * 512 + WB], onec[:],
                                      xbuf[:, sl, b * 512:b * 512 + WB],
                                      start=(dc == 0), stop=(dc == DC - 1))
                            i = te.matmul(ps_sx2[0:1, b * 512:b * 512 + WB], onec[:],
                                          scrA[:, b * 512:b * 512 + WB],
                                          start=(dc == 0), stop=(dc == DC - 1))
                        pe_sq_tok[dc] = ctx.inc(i, "pe")
                stats_pe = ctx.n("pe")

                @VEC
                def _(v):
                    ctx.wait(v, "pe", stats_pe)
                    v.tensor_scalar(out=rowA[:], in0=ps_sx[:], scalar1=1.0 / D,
                                    scalar2=None, op0=ALU.mult)
                    v.tensor_scalar(out=rowB[:], in0=ps_sx2[:], scalar1=1.0 / D,
                                    scalar2=None, op0=ALU.mult)
                    v.drain()
                    v.tensor_tensor(out=rowC[:], in0=rowA[:], in1=rowA[:], op=ALU.mult)
                    v.drain()
                    v.tensor_tensor(out=rowB[:], in0=rowB[:], in1=rowC[:],
                                    op=ALU.subtract)
                    ctx.inc(v.drain(), "vv")
                tok_var = ctx.n("vv")

                @SCA
                def _(s):
                    ctx.wait(s, "vv", tok_var)
                    ctx.inc(s.activation(out=rowC[:], in_=rowB[:], func=AF.Sqrt,
                                         bias=eps_c[0:1, 0:1]), "sc")
                tok_sd = ctx.n("sc")

                @VEC
                def _(v):
                    ctx.wait(v, "sc", tok_sd)
                    v.reciprocal(out=r_row[:], in_=rowC[:])
                    v.drain()
                    v.tensor_tensor(out=mur_row[:], in0=rowA[:], in1=r_row[:], op=ALU.mult)
                    v.tensor_tensor(out=rowC[:], in0=rowB[:], in1=r_row[:], op=ALU.mult)
                    v.drain()
                    v.tensor_tensor(out=rowC[:], in0=rowC[:], in1=r_row[:], op=ALU.mult)
                    ctx.inc(v.drain(), "vv")
                tok_v2 = ctx.n("vv")

                @SCA
                def _(s):
                    ctx.wait(s, "vv", tok_v2)
                    ctx.inc(s.activation(out=rowB[:], in_=rowC[:], func=AF.Sqrt,
                                         bias=eps_c[0:1, 0:1]), "sc")
                tok_sd2 = ctx.n("sc")

                @VEC
                def _(v):
                    ctx.wait(v, "sc", tok_sd2)
                    v.reciprocal(out=rowC[:], in_=rowB[:])
                    v.drain()
                    v.tensor_tensor(out=rowA[:], in0=r_row[:], in1=rowC[:], op=ALU.mult)
                    v.drain()
                    v.tensor_tensor(out=rowB[:], in0=mur_row[:], in1=rowC[:], op=ALU.mult)
                    ctx.inc(v.drain(), "vv")
                tok_rows = ctx.n("vv")

                esPS.__exit__(None, None, None)
                esBC = ExitStack()
                esBC.__enter__()
                ps_bc = esBC.enter_context(nc.psum_tensor("ps_bc", [128, 512], F32))
                bc_toks = []
                for bi, (dst, srow) in enumerate([(bc1, rowA), (bc2, rowB)]):
                    for b in range(NB):
                        w = min(512, T)

                        @blk.tensor
                        def _(te, bi=bi, b=b, w=w, dst=dst, srow=srow):
                            ctx.wait(te, "vv", tok_rows)
                            if bc_toks:
                                ctx.wait(te, "vv", bc_toks[-1])
                            ctx.inc(te.matmul(ps_bc[:, 0:w], oner[:],
                                              srow[:, b * 512:b * 512 + w],
                                              start=True, stop=True), "pe")
                        bc_pe = ctx.n("pe")

                        @VEC
                        def _(v, bi=bi, b=b, w=w, dst=dst, bc_pe=bc_pe):
                            ctx.wait(v, "pe", bc_pe)
                            i = v.tensor_copy(out=dst[:, b * 512:b * 512 + w],
                                              in_=ps_bc[:, 0:w])
                            bc_toks.append(ctx.inc(i, "vv"))
                esBC.__exit__(None, None, None)
                tok_bc = ctx.n("vv")

                h_tok = {}
                for dc in range(DC):
                    sl = dc % 2

                    @blk.sync
                    def _(sy, dc=dc, sl=sl):
                        if dc < 2:
                            ctx.wait(sy, "pe", stats_pe)
                        else:
                            ctx.wait(sy, "vv", h_tok[dc - 2])
                        d = sy.dma_start(out=xbuf[:, sl],
                                         in_=xT[dc * 128:(dc + 1) * 128, :])
                        ctx.inc(d, f"ldx{sl}", 16)
                    ldh_tok = ctx.n(f"ldx{sl}")

                    @VEC
                    def _(v, dc=dc, sl=sl, ldh_tok=ldh_tok):
                        v.wait_ge(sem(f"ldx{sl}"), ldh_tok)
                        if dc == 0:
                            ctx.wait(v, "vv", tok_bc)
                        v.tensor_tensor(out=scrA[:], in0=xbuf[:, sl], in1=bc1[:],
                                        op=ALU.mult)
                        v.drain()
                        v.tensor_tensor(out=scrA[:], in0=scrA[:], in1=bc2[:],
                                        op=ALU.subtract)
                        v.drain()
                        v.tensor_scalar(out=scrA[:], in0=scrA[:],
                                        scalar1=gam_sb[:, dc:dc + 1],
                                        scalar2=bet_sb[:, dc:dc + 1],
                                        op0=ALU.mult, op1=ALU.add)
                        v.drain()
                        v.tensor_copy(out=hh_sb[:, dc], in_=scrA[:])
                        v.drain()
                        i = v.tensor_tensor(out=hl_sb[:, dc], in0=scrA[:],
                                            in1=hh_sb[:, dc], op=ALU.subtract)
                        h_tok[dc] = ctx.inc(i, "vv")
                tok_h = ctx.n("vv")

            es5 = ExitStack()
            with es5:
                wbuf = es5.enter_context(nc.sbuf_tensor("wbuf", [128, DC, D], F32))
                wbh = es5.enter_context(nc.sbuf_tensor("wbh", [128, DC, D], BF16))
                wbl = es5.enter_context(nc.sbuf_tensor("wbl", [128, DC, D], BF16))
                bq_sb = es5.enter_context(nc.sbuf_tensor("bq_sb", [1, D], F32))
                bk_sb = es5.enter_context(nc.sbuf_tensor("bk_sb", [1, D], F32))
                stage_vh = es5.enter_context(nc.sbuf_tensor("stage_vh", [128, 2, VB], BF16))
                stage_vl = es5.enter_context(nc.sbuf_tensor("stage_vl", [128, 2, VB], BF16))
                stage_kqh = es5.enter_context(nc.sbuf_tensor("stage_kqh", [128, 2, D], BF16))
                stage_kql = es5.enter_context(nc.sbuf_tensor("stage_kql", [128, 2, D], BF16))
                ps_kq = [es5.enter_context(nc.psum_tensor(f"ps_kq{i}", [128, D], F32))
                         for i in range(2)]

                def split_wbuf(tag, ld_n):
                    @VEC
                    def _(v):
                        v.wait_ge(sem(f"ldw{tag}"), ld_n)
                        v.tensor_copy(out=wbh[:].rearrange("p a b -> p (a b)"),
                                      in_=wbuf[:].rearrange("p a b -> p (a b)"))
                        v.drain()
                        i = v.tensor_tensor(out=wbl[:].rearrange("p a b -> p (a b)"),
                                            in0=wbuf[:].rearrange("p a b -> p (a b)"),
                                            in1=wbh[:].rearrange("p a b -> p (a b)"),
                                            op=ALU.subtract)
                        ctx.inc(i, "vv")
                    return ctx.n("vv")

                @blk.sync
                def _(sy):
                    ctx.wait(sy, "vv", tok_h)
                    ctx.inc(sy.dma_start(out=bq_sb[:], in_=bq_row), "ldwv", 16)
                    ctx.inc(sy.dma_start(out=bk_sb[:], in_=bk_row), "ldwv", 16)
                    for dc in range(DC):
                        d = sy.dma_start(out=wbuf[:, dc],
                                         in_=wvT[dc * 128:(dc + 1) * 128, :])
                        ctx.inc(d, "ldwv", 16)

                tok_wv = split_wbuf("v", 16 * (DC + 2))
                vev_tok = {}
                vmm_tok = {}
                for vf in range(DC):
                    for tb in range(TB):
                        vi = vf * TB + tb
                        sl = vi % 2

                        @blk.tensor
                        def _(te, vf=vf, tb=tb, vi=vi, sl=sl):
                            if vi == 0:
                                ctx.wait(te, "vv", tok_wv)
                                ctx.wait(te, "vv", tok_h)
                            if vi >= 2:
                                ctx.wait(te, "vv", vev_tok[vi - 2])
                            for b in range(VB // 512 if VB >= 512 else 1):
                                w = min(512, VB)
                                t0 = tb * VB + b * 512
                                for wop, hop in ((wbh, hh_sb), (wbh, hl_sb), (wbl, hh_sb)):
                                    for dc in range(DC):
                                        i = te.matmul(
                                            ps_kq[sl][:, b * 512:b * 512 + w],
                                            wop[:, dc, vf * 128:(vf + 1) * 128],
                                            hop[:, dc, t0:t0 + w],
                                            start=(wop is wbh and hop is hh_sb and dc == 0),
                                            stop=(wop is wbl and dc == DC - 1))
                            vmm_tok[vi] = ctx.inc(i, "pe")

                        @VEC
                        def _(v, vf=vf, tb=tb, vi=vi, sl=sl):
                            ctx.wait(v, "pe", vmm_tok[vi])
                            if vi >= 2:
                                v.wait_ge(sem(f"spv{vi % 2}"), 32 * (vi // 2))
                            v.tensor_scalar(out=stage_vh[:, vi % 2],
                                            in0=ps_kq[sl][:, 0:VB],
                                            scalar1=bv_sb[:, vf:vf + 1], scalar2=None,
                                            op0=ALU.add)
                            v.drain()
                            i = v.scalar_tensor_tensor(out=stage_vl[:, vi % 2],
                                                       in0=ps_kq[sl][:, 0:VB],
                                                       scalar=bv_sb[:, vf:vf + 1],
                                                       in1=stage_vh[:, vi % 2],
                                                       op0=ALU.add, op1=ALU.subtract)
                            vev_tok[vi] = ctx.inc(i, "vv")

                        @SCA
                        def _(s, vf=vf, tb=tb, vi=vi):
                            ctx.wait(s, "vv", vev_tok[vi])
                            d = s.dma_start(out=v_sph[vf, :, tb * VB:(tb + 1) * VB],
                                            in_=stage_vh[:, vi % 2])
                            ctx.inc(d, f"spv{vi % 2}", 16)
                            d = s.dma_start(out=v_spl[vf, :, tb * VB:(tb + 1) * VB],
                                            in_=stage_vl[:, vi % 2])
                            ctx.inc(d, f"spv{vi % 2}", 16)

                def kq_phase(w_dram, b_sb_row, scale, sp_h, sp_l, tag, prev_spill=()):
                    ldname = f"ldw{tag}"
                    spname = f"sp{tag}"

                    @blk.sync
                    def _(sy):
                        ctx.wait(sy, "pe")
                        ctx.wait(sy, "vv")
                        for dc in range(DC):
                            d = sy.dma_start(out=wbuf[:, dc],
                                             in_=w_dram[dc * 128:(dc + 1) * 128, :])
                            ctx.inc(d, ldname, 16)

                    tok_w = split_wbuf(tag, 16 * DC)
                    ev_tok = {}
                    last_vv = ctx.n("vv")
                    for tt in range(NT):
                        sl = tt % 2

                        @blk.tensor
                        def _(te, tt=tt, sl=sl, last_vv=last_vv):
                            if tt == 0:
                                ctx.wait(te, "vv", tok_w)
                                ctx.wait(te, "vv", last_vv)
                            if tt >= 2:
                                ctx.wait(te, "vv", ev_tok[tt - 2])
                            for b in range(2):
                                for hop, wop in ((hh_sb, wbh), (hh_sb, wbl), (hl_sb, wbh)):
                                    for dc in range(DC):
                                        te.matmul(ps_kq[sl][:, b * 512:(b + 1) * 512],
                                                  hop[:, dc, tt * 128:(tt + 1) * 128],
                                                  wop[:, dc, b * 512:(b + 1) * 512],
                                                  start=(hop is hh_sb and wop is wbh and dc == 0),
                                                  stop=False)
                                i = te.matmul(ps_kq[sl][:, b * 512:(b + 1) * 512],
                                              oner[:], b_sb_row[:, b * 512:(b + 1) * 512],
                                              start=False, stop=True)
                            ctx.inc(i, "pe")
                        mm_tok = ctx.n("pe")

                        @VEC
                        def _(v, tt=tt, sl=sl, mm_tok=mm_tok):
                            ctx.wait(v, "pe", mm_tok)
                            if tt < 2:
                                for nm, val in prev_spill:
                                    v.wait_ge(sem(nm), val)
                            else:
                                v.wait_ge(sem(f"{spname}{tt % 2}"), 32 * (tt // 2))
                            v.tensor_scalar(out=stage_kqh[:, tt % 2], in0=ps_kq[sl][:],
                                            scalar1=scale, scalar2=None, op0=ALU.mult)
                            v.drain()
                            i = v.scalar_tensor_tensor(out=stage_kql[:, tt % 2],
                                                       in0=ps_kq[sl][:], scalar=scale,
                                                       in1=stage_kqh[:, tt % 2],
                                                       op0=ALU.mult, op1=ALU.subtract)
                            ev_tok[tt] = ctx.inc(i, "vv")

                        @SCA
                        def _(s, tt=tt):
                            ctx.wait(s, "vv", ev_tok[tt])
                            ctx.inc(s.dma_start(out=sp_h[tt], in_=stage_kqh[:, tt % 2]),
                                    f"{spname}{tt % 2}", 16)
                            ctx.inc(s.dma_start(out=sp_l[tt], in_=stage_kql[:, tt % 2]),
                                    f"{spname}{tt % 2}", 16)

                kq_phase(wkT, bk_sb, ATT_SCALE, k_sph, k_spl, "k")
                kq_phase(wqT, bq_sb, 1.0, q_sph, q_spl, "q",
                         prev_spill=[("spk0", ctx.n("spk0")), ("spk1", ctx.n("spk1"))])
            n_spk0 = ctx.n("spk0"); n_spk1 = ctx.n("spk1")
            n_spq0 = ctx.n("spq0"); n_spq1 = ctx.n("spq1")
            n_spv0 = ctx.n("spv0"); n_spv1 = ctx.n("spv1")
            bar_kq_pe = ctx.n("pe")
            bar_kq_vv = ctx.n("vv")

            es6 = ExitStack()
            with es6:
                kbufh = es6.enter_context(nc.sbuf_tensor("kbufh", [128, 2, D], BF16))
                kbufl = es6.enter_context(nc.sbuf_tensor("kbufl", [128, 2, D], BF16))
                qbufh = es6.enter_context(nc.sbuf_tensor("qbufh", [128, 2, 512], BF16))
                qbufl = es6.enter_context(nc.sbuf_tensor("qbufl", [128, 2, 512], BF16))
                stageS = es6.enter_context(nc.sbuf_tensor("stageS", [128, 2, D], F32))
                ps_S = [es6.enter_context(nc.psum_tensor(f"ps_S{i}", [128, D], F32))
                        for i in range(4)]
                QH = 512
                smm_tok = {}
                sev_tok = {}
                for half in range(2):
                    for tt in range(NT):
                        sl = tt % 2
                        idx = half * NT + tt

                        @blk.sync
                        def _(sy, half=half, tt=tt, sl=sl, idx=idx):
                            if idx == 0:
                                for nm, val in [("spk0", n_spk0), ("spk1", n_spk1),
                                                ("spq0", n_spq0), ("spq1", n_spq1)]:
                                    sy.wait_ge(sem(nm), val)
                                ctx.wait(sy, "vv", bar_kq_vv)
                            if idx >= 2:
                                ctx.wait(sy, "pe", smm_tok[idx - 2])
                            if tt == 0 and half == 1:
                                for qc in range(4):
                                    ctx.wait(sy, "vv", sev_tok[(0, qc)])
                            ctx.inc(sy.dma_start(out=kbufh[:, sl], in_=k_sph[tt]),
                                    f"ldk{sl}", 16)
                            ctx.inc(sy.dma_start(out=kbufl[:, sl], in_=k_spl[tt]),
                                    f"ldk{sl}", 16)
                            d = sy.dma_start(out=qbufh[:, sl],
                                             in_=q_sph[tt, :, half * QH:(half + 1) * QH])
                            ctx.inc(d, f"ldk{sl}", 16)
                            d = sy.dma_start(out=qbufl[:, sl],
                                             in_=q_spl[tt, :, half * QH:(half + 1) * QH])
                            ctx.inc(d, f"ldk{sl}", 16)
                        ldk_tok = ctx.n(f"ldk{sl}")

                        @blk.tensor
                        def _(te, half=half, tt=tt, sl=sl, idx=idx, ldk_tok=ldk_tok):
                            te.wait_ge(sem(f"ldk{sl}"), ldk_tok)
                            if idx == 0:
                                ctx.wait(te, "pe", bar_kq_pe)
                                ctx.wait(te, "vv", bar_kq_vv)
                            for qc in range(4):
                                for b in range(2):
                                    for qop, kop in ((qbufh, kbufh), (qbufh, kbufl),
                                                     (qbufl, kbufh)):
                                        i = te.matmul(
                                            ps_S[qc][:, b * 512:(b + 1) * 512],
                                            qop[:, sl, qc * 128:(qc + 1) * 128],
                                            kop[:, sl, b * 512:(b + 1) * 512],
                                            start=(tt == 0 and qop is qbufh and kop is kbufh),
                                            stop=(tt == NT - 1 and qop is qbufl))
                            smm_tok[idx] = ctx.inc(i, "pe")

                    for qc in range(4):
                        @VEC
                        def _(v, half=half, qc=qc):
                            ctx.wait(v, "pe", smm_tok[half * NT + NT - 1])
                            di = 4 * half + qc
                            if di >= 2:
                                v.wait_ge(sem(f"spS{di % 2}"), 16 * (di // 2))
                            i = v.tensor_copy(out=stageS[:, qc % 2], in_=ps_S[qc][:])
                            sev_tok[(half, qc)] = ctx.inc(i, "vv")

                        @SCA
                        def _(s, half=half, qc=qc):
                            di = 4 * half + qc
                            ctx.wait(s, "vv", sev_tok[(half, qc)])
                            d = s.dma_start(out=S_part[half * 4 + qc],
                                            in_=stageS[:, qc % 2])
                            ctx.inc(d, f"spS{di % 2}", 16)
                n_spS_all = [("spS0", ctx.n("spS0")), ("spS1", ctx.n("spS1"))]
        bar_s1_pe = ctx.n("pe")
        bar_s1_vv = ctx.n("vv")
        bar_s1_sc = ctx.n("sc")
        n_ldwv = ctx.n("ldwv"); n_ldwk = ctx.n("ldwk"); n_ldwq = ctx.n("ldwq")
        n_ldk0 = ctx.n("ldk0"); n_ldk1 = ctx.n("ldk1")
        n_ldx0 = ctx.n("ldx0"); n_ldx1 = ctx.n("ldx1")
        n_spS0 = ctx.n("spS0"); n_spS1 = ctx.n("spS1")

        @blk.gpsimd
        def _(g):
            for nm, val in n_spS_all:
                g.wait_ge(sem(nm), val)
            i = g.collective_compute("AllReduce", ALU.add, replica_groups=GROUPS,
                                     ins=[S_part.opt()], outs=[S_full.opt()])
            ctx.inc(i, "cc")

        es7 = ExitStack()
        with es7:
            sb7 = lambda name, shape, dt=F32: es7.enter_context(nc.sbuf_tensor(name, shape, dt))
            tfc_buf = sb7("tfc_buf", [128, 2, DC, 128], BF16)
            tpj_buf = sb7("tpj_buf", [128, 2, HC, 128], BF16)
            wpj_buf = sb7("wpj_buf", [128, 2, DC, 128])
            u_sb = sb7("u_sb", [RK, HID])
            out_acc = sb7("out_acc", [128, DC, GT])
            xn_g = sb7("xn_g", [128, DC, GT])
            xnh = sb7("xnh", [128, DC, GT], BF16)
            xnl = sb7("xnl", [128, DC, GT], BF16)
            x_g = sb7("x_g", [128, DC, GT])
            r2h = sb7("r2h", [128, HC, GT], BF16)
            r2l = sb7("r2l", [128, HC, GT], BF16)
            arena = sb7("arena", [128, max(2 * DC * GT, 4 * D)])
            a_g = arena[:, 0:DC * GT].rearrange("p (dc t) -> p dc t", t=GT)
            smS = arena[:, 0:2 * D].rearrange("p (s d) -> p s d", d=D)
            scrE = arena[:, 2 * D:3 * D]
            v_gh = sb7("v_gh", [128, DC, GT], BF16)
            v_gl = sb7("v_gl", [128, DC, GT], BF16)
            stageWh = sb7("stageWh", [128, DC, 128], BF16)
            stageWl = sb7("stageWl", [128, DC, 128], BF16)
            wTbufh = sb7("wTbufh", [128, 2, DC * 128], BF16)
            wTbufl = sb7("wTbufl", [128, 2, DC * 128], BF16)
            scr1 = sb7("scr1", [128, GT])
            scrR = sb7("scrR", [128, GT])
            scrR2 = sb7("scrR2", [128, GT])
            br_g = sb7("br_g", [128, GT])
            bmur_g = sb7("bmur_g", [128, GT])
            y1_sb = sb7("y1_sb", [RK, GT])
            y2_sb = sb7("y2_sb", [RK, GT])
            nmax = sb7("nmax", [128, 1])
            ssum = sb7("ssum", [128, 1])
            rsum = sb7("rsum", [128, 1])

            ps_fc = [es7.enter_context(nc.psum_tensor(f"ps_fc{i}", [128, 512], F32))
                     for i in range(3)]
            ps_sh = es7.enter_context(nc.psum_tensor("ps_sh", [128, 512], F32))
            ps_tr2 = [es7.enter_context(nc.psum_tensor(f"ps_tr2{i}", [128, 128], F32))
                      for i in range(2)]
            ps_av = [es7.enter_context(nc.psum_tensor(f"ps_av{i}", [128, 512], F32))
                     for i in range(2)]

            def wait_bar_s1(eng):
                ctx.wait(eng, "pe", bar_s1_pe)
                ctx.wait(eng, "vv", bar_s1_vv)
                ctx.wait(eng, "sc", bar_s1_sc)
                for nm, val in [("ldk0", n_ldk0), ("ldk1", n_ldk1),
                                ("spS0", n_spS0), ("spS1", n_spS1),
                                ("ldx0", n_ldx0), ("ldx1", n_ldx1),
                                ("spv0", n_spv0), ("spv1", n_spv1),
                                ("spk0", n_spk0), ("spk1", n_spk1),
                                ("spq0", n_spq0), ("spq1", n_spq1),
                                ("ldwv", n_ldwv), ("ldwk", n_ldwk),
                                ("ldwq", n_ldwq)]:
                    eng.wait_ge(sem(nm), val)

            @blk.sync
            def _(sy):
                wait_bar_s1(sy)
                for nm, val in stw_fin:
                    sy.wait_ge(sem(nm), val)
                ctx.inc(sy.dma_start(out=u_sb[:], in_=uT), "lds2", 16)
            n_lds2 = ctx.n("lds2")

            esU = ExitStack()
            with esU:
                rs16 = esU.enter_context(nc.sbuf_tensor("rs16", [RK, HID], F32))
                rs_row = esU.enter_context(nc.sbuf_tensor("rs_row", [1, HID], F32))
                rs_tr = esU.enter_context(nc.sbuf_tensor("rs_tr", [HC, 128], F32))
                rs_col = esU.enter_context(nc.sbuf_tensor("rs_col", [128, HC], F32))

                @VEC
                def _(v):
                    wait_bar_s1(v)
                    v.reciprocal(out=rs_col[:], in_=sfc_col[:])
                    ctx.inc(v.drain(), "vv")
                tok_rsc = ctx.n("vv")

                @blk.tensor
                def _(te):
                    ctx.wait(te, "vv", tok_rsc)
                    ctx.wait(te, "pe", bar_s1_pe)
                    ctx.inc(te.transpose(ps_tr2[0][0:HC, :], rs_col[:], identF[:]), "pe")
                tok_rst = ctx.n("pe")

                @VEC
                def _(v):
                    ctx.wait(v, "pe", tok_rst)
                    ctx.inc(v.tensor_copy(out=rs_tr[:], in_=ps_tr2[0][0:HC, :]), "vv")
                tok_rstr = ctx.n("vv")

                @SCA
                def _(s):
                    ctx.wait(s, "vv", tok_rstr)
                    d = s.dma_start(out=rs_row[:].rearrange("o (c p) -> o c p", p=128),
                                    in_=rs_tr[:])
                    ctx.inc(d, "ldrs", 16)

                ps_rs16 = ps_sh[0:RK, :]
                rsb_toks = []
                for b in range(HID // 512):
                    @blk.tensor
                    def _(te, b=b):
                        if b == 0:
                            te.wait_ge(sem("ldrs"), 16)
                        if rsb_toks:
                            ctx.wait(te, "vv", rsb_toks[-1])
                        ctx.inc(te.matmul(ps_rs16[:], oner[0:1, 0:RK],
                                          rs_row[:, b * 512:(b + 1) * 512],
                                          start=True, stop=True), "pe")
                    rsb_pe = ctx.n("pe")

                    @VEC
                    def _(v, b=b, rsb_pe=rsb_pe):
                        ctx.wait(v, "pe", rsb_pe)
                        i = v.tensor_copy(out=rs16[:, b * 512:(b + 1) * 512],
                                          in_=ps_rs16[:])
                        rsb_toks.append(ctx.inc(i, "vv"))

                @VEC
                def _(v):
                    ctx.wait(v, "vv", rsb_toks[-1])
                    v.wait_ge(sem("lds2"), n_lds2)
                    v.tensor_tensor(out=u_sb[:], in0=u_sb[:], in1=rs16[:], op=ALU.mult)
                    ctx.inc(v.drain(), "vv")
            tok_uprime = ctx.n("vv")

            state = {}

            def ffn_group(g_idx):
                gsl = slice(g_idx * GT, (g_idx + 1) * GT)

                @blk.sync
                def _(sy):
                    if g_idx == 0:
                        wait_bar_s1(sy)
                    else:
                        ctx.wait(sy, "vv", state[("xn", g_idx - 1)])
                    for dc in range(DC):
                        ctx.inc(sy.dma_start(out=x_g[:, dc],
                                             in_=xT[dc * 128:(dc + 1) * 128, gsl]),
                                "ldxg", 16)
                ld_tok = ctx.n("ldxg")

                bcg_toks = []
                for dst, srow in [(br_g, r_row), (bmur_g, mur_row)]:
                    @blk.tensor
                    def _(te, dst=dst, srow=srow):
                        if g_idx == 0 and not bcg_toks:
                            ctx.wait(te, "vv", tok_uprime)
                        if g_idx > 0 and not bcg_toks:
                            ctx.wait(te, "vv", state[("xn", g_idx - 1)])
                            ctx.wait(te, "vv", state[("y2ev", g_idx - 1)])
                        if bcg_toks:
                            ctx.wait(te, "vv", bcg_toks[-1])
                        ctx.inc(te.matmul(ps_sh[:, 0:GT], oner[:], srow[0:1, gsl],
                                          start=True, stop=True), "pe")
                    bcg_pe = ctx.n("pe")

                    @VEC
                    def _(v, dst=dst, bcg_pe=bcg_pe):
                        ctx.wait(v, "pe", bcg_pe)
                        bcg_toks.append(ctx.inc(v.tensor_copy(out=dst[:], in_=ps_sh[:, 0:GT]),
                                                "vv"))
                tok_bcg = ctx.n("vv")

                @VEC
                def _(v):
                    v.wait_ge(sem("ldxg"), ld_tok)
                    ctx.wait(v, "vv", tok_bcg)
                    if g_idx > 0:
                        v.wait_ge(sem("out_w"), 16 * g_idx)
                        ctx.wait(v, "pe", state[("fcmm", g_idx - 1)])
                    for dc in range(DC):
                        v.tensor_tensor(out=scr1[:], in0=x_g[:, dc], in1=br_g[:],
                                        op=ALU.mult)
                        v.drain()
                        v.tensor_tensor(out=xn_g[:, dc], in0=scr1[:], in1=bmur_g[:],
                                        op=ALU.subtract)
                        v.drain()
                        v.tensor_tensor(out=out_acc[:, dc], in0=x_g[:, dc],
                                        in1=xn_g[:, dc], op=ALU.add)
                        v.tensor_copy(out=xnh[:, dc], in_=xn_g[:, dc])
                        v.drain()
                        v.tensor_tensor(out=xnl[:, dc], in0=xn_g[:, dc],
                                        in1=xnh[:, dc], op=ALU.subtract)
                        v.drain()
                    state[("xn", g_idx)] = ctx.inc(v.drain(), "vv")
                tok_xn = state[("xn", g_idx)]

                @blk.tensor
                def _(te):
                    ctx.wait(te, "vv", tok_xn)
                    for dc in range(DC):
                        i = te.matmul(ps_sh[0:RK, 0:GT], vlr_sb[:, dc], xn_g[:, dc],
                                      start=(dc == 0), stop=(dc == DC - 1))
                    state[("y1", g_idx)] = ctx.inc(i, "pe")

                @VEC
                def _(v):
                    ctx.wait(v, "pe", state[("y1", g_idx)])
                    v.tensor_copy(out=y1_sb[:], in_=ps_sh[0:RK, 0:GT])
                    ctx.inc(v.drain(), "vv")
                tok_y1e = ctx.n("vv")

                @blk.tensor
                def _(te):
                    ctx.wait(te, "vv", tok_y1e)
                    ctx.inc(te.matmul(ps_sh[0:RK, 0:GT], fb_sb[:], y1_sb[:], start=True,
                                      stop=True), "pe")
                tok_y2 = ctx.n("pe")

                @VEC
                def _(v):
                    ctx.wait(v, "pe", tok_y2)
                    v.tensor_copy(out=y2_sb[:], in_=ps_sh[0:RK, 0:GT])
                    state[("y2ev", g_idx)] = ctx.inc(v.drain(), "vv")
                tok_y2e = ctx.n("vv")

                r2_tok = {}
                relu_tok = {}
                fcmm_tok = {}
                for ic in range(HC):
                    sl = ic % 2
                    sl3 = ic % 3

                    @blk.sync
                    def _(sy, ic=ic, sl=sl):
                        if ic == 0 and g_idx == 0:
                            for nm, val in stw_fin[:2]:
                                sy.wait_ge(sem(nm), val)
                        if ic >= 2:
                            ctx.wait(sy, "pe", fcmm_tok[ic - 2])
                        elif g_idx > 0:
                            ctx.wait(sy, "pe", state[("fcmm", g_idx - 1)])
                        ctx.inc(sy.dma_start(out=tfc_buf[:, sl], in_=ternfc_dram[:, ic]),
                                f"ldtf{sl}", 16)
                    ldtf_tok = ctx.n(f"ldtf{sl}")

                    @blk.tensor
                    def _(te, ic=ic, sl=sl, sl3=sl3, ldtf_tok=ldtf_tok):
                        te.wait_ge(sem(f"ldtf{sl}"), ldtf_tok)
                        if ic == 0:
                            ctx.wait(te, "vv", tok_y2e)
                            if g_idx == 0:
                                te.wait_ge(sem("lds2"), n_lds2)
                            else:
                                ctx.wait(te, "vv", state[("pjev", g_idx - 1)])
                        if ic >= 3:
                            ctx.wait(te, "sc", relu_tok[ic - 3])
                        for xp in (xnh, xnl):
                            for dc in range(DC):
                                te.matmul(ps_fc[sl3][:, 0:GT], tfc_buf[:, sl, dc],
                                          xp[:, dc], start=(xp is xnh and dc == 0),
                                          stop=False)
                        i = te.matmul(ps_fc[sl3][:, 0:GT],
                                      u_sb[:, ic * 128:(ic + 1) * 128], y2_sb[:],
                                      start=False, stop=True)
                        fcmm_tok[ic] = ctx.inc(i, "pe")
                    mm_tok = ctx.n("pe")

                    @SCA
                    def _(s, ic=ic, sl3=sl3, mm_tok=mm_tok):
                        ctx.wait(s, "pe", mm_tok)
                        if ic >= 1:
                            ctx.wait(s, "vv", r2_tok[ic - 1])
                        i = s.activation(out=scrR[:], in_=ps_fc[sl3][:, 0:GT], func=AF.Relu,
                                         scale=sfc_col[:, ic:ic + 1])
                        relu_tok[ic] = ctx.inc(i, "sc")

                    @VEC
                    def _(v, ic=ic):
                        ctx.wait(v, "sc", relu_tok[ic])
                        v.tensor_tensor(out=scrR2[:], in0=scrR[:], in1=scrR[:],
                                        op=ALU.mult)
                        v.drain()
                        v.tensor_copy(out=r2h[:, ic], in_=scrR2[:])
                        v.drain()
                        i = v.tensor_tensor(out=r2l[:, ic], in0=scrR2[:], in1=r2h[:, ic],
                                            op=ALU.subtract)
                        r2_tok[ic] = ctx.inc(i, "vv")
                state[("fcmm", g_idx)] = ctx.n("pe")
                tok_r2 = ctx.n("vv")

                pjev_tok = {}
                pjmm_tok = {}
                for fc8 in range(DC):
                    sl = fc8 % 2
                    sl3 = fc8 % 3

                    @blk.sync
                    def _(sy, fc8=fc8, sl=sl):
                        if fc8 == 0 and g_idx == 0:
                            for nm, val in stw_fin[2:]:
                                sy.wait_ge(sem(nm), val)
                        if fc8 >= 2:
                            ctx.wait(sy, "pe", pjmm_tok[fc8 - 2])
                        elif g_idx > 0:
                            ctx.wait(sy, "pe", state[("pjmm", g_idx - 1)])
                        ctx.inc(sy.dma_start(out=tpj_buf[:, sl], in_=ternpj_dram[:, fc8]),
                                f"ldtp{sl}", 16)
                    ldtp_tok = ctx.n(f"ldtp{sl}")

                    @blk.tensor
                    def _(te, fc8=fc8, sl=sl, sl3=sl3, ldtp_tok=ldtp_tok):
                        te.wait_ge(sem(f"ldtp{sl}"), ldtp_tok)
                        if fc8 == 0:
                            ctx.wait(te, "vv", tok_r2)
                        if fc8 >= 3:
                            ctx.wait(te, "vv", pjev_tok[fc8 - 3])
                        for rp in (r2h, r2l):
                            for ic in range(HC):
                                i = te.matmul(ps_fc[sl3][:, 0:GT], tpj_buf[:, sl, ic],
                                              rp[:, ic], start=(rp is r2h and ic == 0),
                                              stop=(rp is r2l and ic == HC - 1))
                        pjmm_tok[fc8] = ctx.inc(i, "pe")
                    mm_tok = ctx.n("pe")

                    @VEC
                    def _(v, fc8=fc8, sl3=sl3, mm_tok=mm_tok):
                        ctx.wait(v, "pe", mm_tok)
                        i = v.scalar_tensor_tensor(out=out_acc[:, fc8],
                                                   in0=ps_fc[sl3][:, 0:GT],
                                                   scalar=spj_col[:, fc8:fc8 + 1],
                                                   in1=out_acc[:, fc8],
                                                   op0=ALU.mult, op1=ALU.add)
                        pjev_tok[fc8] = ctx.inc(i, "vv")
                state[("pjev", g_idx)] = ctx.n("vv")
                state[("pjmm", g_idx)] = ctx.n("pe")

            def softmax_phase():
                exp_tok = {}
                w_tok = {}
                trev_tok = {}
                for qc in range(DC):
                    sl = qc % 2

                    @blk.sync
                    def _(sy, qc=qc, sl=sl):
                        if qc == 0:
                            ctx.wait(sy, "cc", 1)
                        if qc >= 2:
                            ctx.wait(sy, "sc", exp_tok[qc - 2])
                        ctx.inc(sy.dma_start(out=smS[:, sl], in_=S_full[qc]),
                                f"ldS{sl}", 16)
                    ld_tok = ctx.n(f"ldS{sl}")

                    @VEC
                    def _(v, qc=qc, sl=sl, ld_tok=ld_tok):
                        v.wait_ge(sem(f"ldS{sl}"), ld_tok)
                        if qc >= 1:
                            ctx.wait(v, "sc", exp_tok[qc - 1])
                        i = v.tensor_reduce(out=nmax[:], in_=smS[:, sl], axis=AX.X,
                                            op=ALU.max, negate=True)
                        ctx.inc(i, "vv")
                    red_tok = ctx.n("vv")

                    @SCA
                    def _(s, qc=qc, sl=sl, red_tok=red_tok):
                        ctx.wait(s, "vv", red_tok)
                        if qc >= 1:
                            ctx.wait(s, "vv", w_tok[qc - 1])
                        i = s.activation(out=scrE[:], in_=smS[:, sl], func=AF.Exp,
                                         bias=nmax[:, 0:1], accum_out=ssum[:, 0:1])
                        exp_tok[qc] = ctx.inc(i, "sc")

                    @VEC
                    def _(v, qc=qc):
                        ctx.wait(v, "sc", exp_tok[qc])
                        v.reciprocal(out=rsum[:], in_=ssum[:])
                        v.drain()
                        i = v.tensor_scalar(out=scrE[:], in0=scrE[:],
                                            scalar1=rsum[:, 0:1], scalar2=None,
                                            op0=ALU.mult)
                        w_tok[qc] = ctx.inc(i, "vv")

                    for kc in range(DC):
                        idx = qc * DC + kc

                        @blk.tensor
                        def _(te, qc=qc, kc=kc, idx=idx):
                            if kc == 0:
                                ctx.wait(te, "vv", w_tok[qc])
                            if idx >= 2:
                                ctx.wait(te, "vv", trev_tok[idx - 2])
                            i = te.transpose(ps_tr2[idx % 2][:],
                                             scrE[:, kc * 128:(kc + 1) * 128], identF[:])
                            ctx.inc(i, "pe")
                        tr_tok = ctx.n("pe")

                        @VEC
                        def _(v, qc=qc, kc=kc, idx=idx, tr_tok=tr_tok):
                            ctx.wait(v, "pe", tr_tok)
                            if qc >= 1 and kc == 0:
                                v.wait_ge(sem("spW"), 32 * qc)
                            v.tensor_copy(out=stageWh[:, kc], in_=ps_tr2[idx % 2][:])
                            v.drain()
                            i = v.tensor_tensor(out=stageWl[:, kc], in0=ps_tr2[idx % 2][:],
                                                in1=stageWh[:, kc], op=ALU.subtract)
                            trev_tok[idx] = ctx.inc(i, "vv")

                    @SCA
                    def _(s, qc=qc):
                        ctx.wait(s, "vv", trev_tok[qc * DC + DC - 1])
                        ctx.inc(s.dma_start(out=wT_dramh[qc].rearrange("kc p i -> p kc i"),
                                            in_=stageWh[:]), "spW", 16)
                        ctx.inc(s.dma_start(out=wT_draml[qc].rearrange("kc p i -> p kc i"),
                                            in_=stageWl[:]), "spW", 16)
                return ctx.n("spW")

            def attn_group(g_idx, n_spW):
                gsl = slice(g_idx * GT, (g_idx + 1) * GT)

                @blk.sync
                def _(sy):
                    if g_idx == 0:
                        sy.wait_ge(sem("spv0"), n_spv0)
                        sy.wait_ge(sem("spv1"), n_spv1)
                        sy.wait_ge(sem("spW"), n_spW)
                    else:
                        ctx.wait(sy, "pe", state[("avmm", g_idx - 1)])
                    for kc in range(DC):
                        ctx.inc(sy.dma_start(out=v_gh[:, kc], in_=v_sph[kc, :, gsl]),
                                "ldvg", 16)
                        ctx.inc(sy.dma_start(out=v_gl[:, kc], in_=v_spl[kc, :, gsl]),
                                "ldvg", 16)
                ldv_tok = ctx.n("ldvg")

                avev_tok = {}
                avmm_tok = {}
                for qc in range(DC):
                    sl = qc % 2

                    @blk.sync
                    def _(sy, qc=qc, sl=sl):
                        if g_idx == 0 and qc == 0:
                            sy.wait_ge(sem("spW"), n_spW)
                        if qc >= 2:
                            ctx.wait(sy, "pe", avmm_tok[qc - 2])
                        elif g_idx > 0:
                            ctx.wait(sy, "pe", state[("avmm", g_idx - 1)])
                        ctx.inc(sy.dma_start(out=wTbufh[:, sl],
                                             in_=wT_dramh[qc].rearrange("kc p i -> p kc i")),
                                f"ldwT{sl}", 16)
                        ctx.inc(sy.dma_start(out=wTbufl[:, sl],
                                             in_=wT_draml[qc].rearrange("kc p i -> p kc i")),
                                f"ldwT{sl}", 16)
                    ldt_tok = ctx.n(f"ldwT{sl}")

                    @blk.tensor
                    def _(te, qc=qc, sl=sl, ldt_tok=ldt_tok):
                        te.wait_ge(sem(f"ldwT{sl}"), ldt_tok)
                        if qc == 0:
                            te.wait_ge(sem("ldvg"), ldv_tok)
                            ctx.wait(te, "vv", state[("pjev", g_idx)])
                        if qc >= 2:
                            ctx.wait(te, "vv", avev_tok[qc - 2])
                        for wop, vop in ((wTbufh, v_gh), (wTbufh, v_gl), (wTbufl, v_gh)):
                            for kc in range(DC):
                                i = te.matmul(ps_av[sl][:, 0:GT],
                                              wop[:, sl, kc * 128:(kc + 1) * 128],
                                              vop[:, kc],
                                              start=(wop is wTbufh and vop is v_gh and kc == 0),
                                              stop=(wop is wTbufl and kc == DC - 1))
                        avmm_tok[qc] = ctx.inc(i, "pe")

                    @VEC
                    def _(v, qc=qc, sl=sl):
                        ctx.wait(v, "pe", avmm_tok[qc])
                        if g_idx > 0 and qc == 0:
                            ctx.wait(v, "pe", state[("o1mm", g_idx - 1)])
                        i = v.tensor_copy(out=a_g[:, qc], in_=ps_av[sl][:, 0:GT])
                        avev_tok[qc] = ctx.inc(i, "vv")
                state[("avmm", g_idx)] = ctx.n("pe")
                tok_a = ctx.n("vv")

                o1ev_tok = {}
                o1mm_tok = {}
                for oc in range(DC):
                    sl = oc % 2

                    @blk.sync
                    def _(sy, oc=oc, sl=sl):
                        if oc >= 2:
                            ctx.wait(sy, "pe", o1mm_tok[oc - 2])
                        elif g_idx > 0:
                            ctx.wait(sy, "pe", state[("o1mm", g_idx - 1)])
                        ctx.inc(sy.dma_start(out=wpj_buf[:, sl],
                                             in_=wprojT[oc].rearrange("(ac p) j -> p ac j", p=128)),
                                f"ldpj{sl}", 16)
                    ldpj_tok = ctx.n(f"ldpj{sl}")

                    @blk.tensor
                    def _(te, oc=oc, sl=sl, ldpj_tok=ldpj_tok):
                        te.wait_ge(sem(f"ldpj{sl}"), ldpj_tok)
                        if oc == 0:
                            ctx.wait(te, "vv", tok_a)
                        if oc >= 2:
                            ctx.wait(te, "vv", o1ev_tok[oc - 2])
                        for ac in range(DC):
                            i = te.matmul(ps_av[sl][:, 0:GT],
                                          wpj_buf[:, sl, ac],
                                          a_g[:, ac], start=(ac == 0), stop=(ac == DC - 1))
                        o1mm_tok[oc] = ctx.inc(i, "pe")
                    mm_tok = ctx.n("pe")

                    @VEC
                    def _(v, oc=oc, sl=sl, mm_tok=mm_tok):
                        ctx.wait(v, "pe", mm_tok)
                        i = v.scalar_tensor_tensor(out=out_acc[:, oc],
                                                   in0=ps_av[sl][:, 0:GT],
                                                   scalar=pb_sb[:, oc:oc + 1],
                                                   in1=out_acc[:, oc],
                                                   op0=ALU.add, op1=ALU.add)
                        o1ev_tok[oc] = ctx.inc(i, "vv")
                state[("o1mm", g_idx)] = ctx.n("pe")
                tok_out = ctx.n("vv")

                @blk.gpsimd
                def _(gp):
                    ctx.wait(gp, "vv", tok_out)
                    d = gp.dma_start(out=out_dram[g_idx].rearrange("dc p t -> p dc t"),
                                     in_=out_acc[:])
                    ctx.inc(d, "out_w", 16)

            ffn_group(0)
            n_spW = softmax_phase()
            attn_group(0, n_spW)
            for g_idx in range(1, NG):
                ffn_group(g_idx)
                attn_group(g_idx, n_spW)

            @blk.gpsimd
            def _(gp):
                gp.wait_ge(sem("out_w"), 16 * NG)

    return nc


def shard_inputs(T, x, norm_w, norm_b, qkv_w, qkv_b, proj_w, proj_b,
                 fast_b, w_fc, w_proj, u, v, gate):
    x = np.asarray(x, np.float32)
    f32c = lambda a: np.ascontiguousarray(np.asarray(a, np.float32))
    qkv_w = np.asarray(qkv_w, np.float32)
    qkv_b = np.asarray(qkv_b, np.float32)
    shared = dict(
        wqT=f32c(qkv_w[0:D].T), wkT=f32c(qkv_w[D:2 * D].T), wvT=f32c(qkv_w[2 * D:3 * D].T),
        wprojT=np.ascontiguousarray(
            np.asarray(proj_w, np.float32).T.reshape(D, DC, 128).transpose(1, 0, 2)),
        wfc=f32c(w_fc), wpj=f32c(w_proj),
        vlrT=f32c(np.asarray(v).T), fbT=f32c(np.asarray(fast_b).T), uT=f32c(np.asarray(u).T),
        bq_row=f32c(qkv_b[0:D][None, :]), bk_row=f32c(qkv_b[D:2 * D][None, :]),
        bv_col=f32c(qkv_b[2 * D:3 * D].reshape(DC, 128).T),
        pb_col=f32c(np.asarray(proj_b).reshape(DC, 128).T),
        gam_col=f32c(np.asarray(norm_w).reshape(DC, 128).T),
        bet_col=f32c(np.asarray(norm_b).reshape(DC, 128).T),
        gate_in=f32c(np.asarray(gate).reshape(1, 1)),
    )
    in_maps = []
    for c in range(NCORES):
        g, j = divmod(c, 4)
        m = dict(shared)
        m["xT"] = f32c(x[g, j * T:(j + 1) * T, :].T)
        in_maps.append(m)
    return in_maps


def assemble_output(T, results):
    GT = min(256, T)
    NG = T // GT
    out = np.empty((2, 4 * T, D), np.float32)
    for c in range(NCORES):
        g, j = divmod(c, 4)
        o = np.asarray(results[c]["outT"])
        for gg in range(NG):
            blkv = o[gg].reshape(D, GT)
            out[g, j * T + gg * GT:j * T + (gg + 1) * GT, :] = blkv.T
    return out




_CACHED = {}


def kernel(L0_attn_x, L0_attn_norm_w, L0_attn_norm_b, L0_attn_qkv_w, L0_attn_qkv_b,
           L0_attn_proj_w, L0_attn_proj_b, L0_ffn_fast_b, L0_ffn_w_fc, L0_ffn_w_proj,
           L0_ffn_u, L0_ffn_v, L0_ffn_gate):
    from concourse.bass_utils import run_bass_kernel_spmd
    T = 2048
    if "nc" not in _CACHED:
        _CACHED["nc"] = build_nc(T)
    nc = _CACHED["nc"]
    in_maps = shard_inputs(
        T, L0_attn_x, L0_attn_norm_w, L0_attn_norm_b, L0_attn_qkv_w, L0_attn_qkv_b,
        L0_attn_proj_w, L0_attn_proj_b, L0_ffn_fast_b, L0_ffn_w_fc, L0_ffn_w_proj,
        L0_ffn_u, L0_ffn_v, L0_ffn_gate)
    res = run_bass_kernel_spmd(nc, in_maps, core_ids=list(range(NCORES)))
    return assemble_output(T, [res.results[c] for c in range(NCORES)])
